# version 05
# speedup vs baseline: 1.2843x; 1.0164x over previous
import contextlib
import numpy as np
import ml_dtypes
import concourse.bass as bass
import concourse.mybir as mybir
from concourse.ap import AP
from concourse.bass_utils import run_bass_kernel_spmd

F32 = mybir.dt.float32
BF16 = mybir.dt.bfloat16
I32 = mybir.dt.int32
AF = mybir.ActivationFunctionType
ALU = mybir.AluOpType
AX = mybir.AxisListType
BF = ml_dtypes.bfloat16

T = 2048
D = 1024
DI = 2048
NS = 16
DTR = 64
KC = 4
NT = T // 128
ND = D // 128
NDI = DI // 128
EPS = 1e-5
NH = 16
HD = 64
NR = 1024
IFF = 2816
QMAX = 127.0

ENGINES = ("sync", "tensor", "scalar", "vector", "gpsimd")
SELF_SYNC = False
SELF_SYNC_ENGINES = {'vector', 'gpsimd', 'scalar', 'tensor'}
CVALS = [EPS, 1.0, 0.5, -0.5, -1.0, 2.0, 0.0, -1e9, 127.0, -128.0, 0.125]


class Op:
    __slots__ = ("eng", "fn", "sem", "inc", "waits", "label")

    def __init__(self, eng, fn, sem, inc, waits, label):
        self.eng, self.fn, self.sem, self.inc, self.waits, self.label = (
            eng, fn, sem, inc, waits, label)


class Rec:
    def __init__(self):
        self.ops = []
        self.cnt = {}
        self.marks = {}
        self._auto = 0

    def op(self, eng, fn, *, dma=False, mark=False, waits=(), label=None):
        if dma:
            stream = dma if isinstance(dma, str) else "d"
            sem = f"{eng}.{stream}"
            inc = 16
        else:
            sem = eng + "_c"
            inc = 1 if mark else 0
        wl = []
        if dma:
            prev = self.cnt.get(sem, 0)
            if prev:
                wl.append((sem, prev))
        if inc:
            v = self.cnt.get(sem, 0) + inc
            self.cnt[sem] = v
            if label is None:
                label = f"_a{self._auto}"
                self._auto += 1
            self.marks[label] = (eng, sem, v)
        for w in waits:
            if w is None:
                continue
            weng, wsem, wval = self.marks[w]
            if (weng == eng and not SELF_SYNC and eng not in SELF_SYNC_ENGINES
                    and "." not in wsem):
                continue
            wl.append((wsem, wval))
        self.ops.append(Op(eng, fn, sem if inc else None, inc, wl, label))
        return label

    def barrier_label(self, eng_needed=ENGINES):
        out = []
        for label, (eng, sem, v) in self.marks.items():
            if self.cnt.get(sem) == v:
                out.append(label)
        return out


def play(rec, nc):
    with contextlib.ExitStack() as stack:
        sems = {}
        for name in rec.cnt:
            sems[name] = stack.enter_context(nc.semaphore(name))
        per_eng = {e: [] for e in ENGINES}
        for op in rec.ops:
            per_eng[op.eng].append(op)
        with nc.Block() as blk:
            for eng in ENGINES:
                ops = per_eng[eng]
                if not ops:
                    continue

                def mk(ops):
                    def f(e):
                        waited = {}
                        for op in ops:
                            for (wsem, wval) in op.waits:
                                if waited.get(wsem, -1) >= wval:
                                    continue
                                e.wait_ge(sems[wsem], wval)
                                waited[wsem] = wval
                            ins = op.fn(e)
                            if op.inc:
                                ins.then_inc(sems[op.sem], op.inc)
                    return f
                getattr(blk, eng)(mk(ops))


class Ctx:

    def __init__(self):
        self.nc = bass.Bass("TRN2", target_bir_lowering=False, debug=False)
        self.rec = Rec()
        self._stack = contextlib.ExitStack()
        self._sbuf = {}
        self._psum = {}
        self._dram = {}

    def sb(self, name, shape=None, dt=None):
        if name not in self._sbuf:
            assert shape is not None, name
            cm = self.nc.sbuf_tensor(name, shape, dt)
            self._sbuf[name] = (cm.__enter__(), cm)
        return self._sbuf[name][0]

    def ps(self, name, shape=None, dt=F32):
        if name not in self._psum:
            assert shape is not None, name
            cm = self.nc.psum_tensor(name, shape, dt)
            self._psum[name] = (cm.__enter__(), cm)
        return self._psum[name][0]

    def free(self, prefix):
        for pool in (self._sbuf, self._psum):
            for name in reversed([n for n in pool if n.startswith(prefix)]):
                pool[name][1].__exit__(None, None, None)
                del pool[name]

    def dram(self, name, shape, dt, kind="Internal"):
        if name not in self._dram:
            self._dram[name] = self.nc.dram_tensor(name, shape, dt, kind=kind)
        return self._dram[name]

    def din(self, name, shape, dt=F32):
        return self.dram(name, shape, dt, kind="ExternalInput").ap()

    def dout(self, name, shape, dt=F32):
        return self.dram(name, shape, dt, kind="ExternalOutput").ap()

    def build(self):
        play(self.rec, self.nc)
        self._stack.close()
        return self.nc


def host_constants():
    c = {}
    c["c_ident_bf"] = np.eye(128, dtype=BF)
    c["c_ident_f"] = np.eye(128, dtype=np.float32)
    c["c_ones_f"] = np.ones((1, 128), np.float32)
    c["c_ones_bf"] = np.ones((1, 128), BF)
    pat = np.zeros((128, 16, 128), BF)
    for j in range(16):
        for ds in range(8):
            pat[8 * j + ds, j, ds * 16:ds * 16 + 16] = 1
    c["c_rep_pat"] = np.ascontiguousarray(pat.reshape(128, 16 * 128))
    red = np.zeros((128, 16, 128), BF)
    for j in range(16):
        for ds in range(8):
            red[ds * 16:ds * 16 + 16, j, 8 * j + ds] = 1
    c["c_red_pat"] = np.ascontiguousarray(red.reshape(128, 16 * 128))
    c["c_cvec"] = np.tile(np.asarray(CVALS, np.float32)[None, :], (128, 1))
    bcp = np.zeros((16, 128), BF)
    for ds in range(8):
        for n in range(16):
            bcp[n, ds * 16 + n] = 1
    c["c_bc_pat"] = bcp
    m = np.zeros((128, 1024), np.float32)
    for p in range(128):
        m[p, p + 513:] = -1e9
    c["c_maskP"] = m.astype(BF)
    return c


def load_consts(ctx):
    r = ctx.rec
    ident = ctx.din("c_ident_bf", [128, 128], BF16)
    identf = ctx.din("c_ident_f", [128, 128], F32)
    ones_f = ctx.din("c_ones_f", [1, 128], F32)
    ones_bf = ctx.din("c_ones_bf", [1, 128], BF16)
    s_id = ctx.sb("s_ident", [128, 128], BF16)
    s_idf = ctx.sb("s_identf", [128, 128], F32)
    s_onesf = ctx.sb("s_onesf", [1, 128], F32)
    s_onesbf = ctx.sb("s_onesbf", [1, 128], BF16)
    ctx.ps("p_guard", [1, 16], F32)
    cvec = ctx.din("c_cvec", [128, len(CVALS)], F32)
    s_cvec = ctx.sb("s_cvec", [128, len(CVALS)], F32)
    l0 = r.op("sync", lambda e: e.dma_start(out=s_cvec[:], in_=cvec), dma=True)
    for j, v in enumerate(CVALS):
        ctx.nc.const_aps.register(F32, v, s_cvec[:, j:j + 1])
    l1 = r.op("sync", lambda e: e.dma_start(out=s_id[:], in_=ident), dma=True)
    l1b = r.op("sync", lambda e: e.dma_start(out=s_idf[:], in_=identf), dma=True)
    l2 = r.op("sync", lambda e: e.dma_start(out=s_onesf[:], in_=ones_f), dma=True)
    l3 = r.op("sync", lambda e: e.dma_start(out=s_onesbf[:], in_=ones_bf), dma=True)
    return dict(ident=s_id, identf=s_idf, onesf=s_onesf, onesbf=s_onesbf,
                ld=[l0, l1, l1b, l2, l3])


def pe_done(ctx, *labs):
    r = ctx.rec
    pg = ctx.ps("p_guard", [1, 16], F32)
    ob = ctx.sb("s_onesbf")
    return r.op("tensor", lambda e: e.matmul(
        pg[:], ob[0:1, 0:1], ob[0:1, 0:16], start=True, stop=True),
        mark=True, waits=labs)


def dap(base_ap, offset, axes):
    return AP(base_ap.tensor, offset, [list(a) for a in axes])


def bcast_row(ctx, pre, row_ap, width, out_dt, ld_label, n_parts=128):
    r = ctx.rec
    c = ctx.sb("s_onesf")
    p = ctx.ps(pre + "_pb", [128, 512], F32)
    out = ctx.sb(pre + "_rep", [128, width], out_dt)
    done = None
    for c0 in range(0, width, 512):
        w = min(512, width - c0)
        mm = r.op("tensor", lambda e, c0=c0, w=w: e.matmul(
            p[:, 0:w], c[:, 0:n_parts], row_ap[:, c0:c0 + w],
            start=True, stop=True), mark=True, waits=(ld_label, done))
        g = pe_done(ctx, mm)
        done = r.op("scalar", lambda e, c0=c0, w=w: e.copy(
            out[:, c0:c0 + w], p[:, 0:w]), mark=True, waits=(g,))
    return out, done



def chain_maker(r, *init):
    state = {"last": None, "init": tuple(x for x in init if x)}

    def em(eng, fn, *, dma=False, waits=()):
        w = state["init"] + (state["last"],) + tuple(waits)
        state["init"] = ()
        lab = r.op(eng, fn, dma=dma, mark=True, waits=w)
        state["last"] = lab
        return lab
    return em


def phase_ln(ctx, pre, x_ap, outs, affine=None, after=()):
    r = ctx.rec
    s_x = [ctx.sb(pre + f"x{k}", [128, D], F32) for k in range(2)]
    s_sq = ctx.sb(pre + "sq", [128, D], F32)
    s_af = ctx.sb(pre + "af", [128, D], F32) if affine else None
    s_st = ctx.sb(pre + "st", [128, 12], F32)
    s_xn = ctx.sb(pre + "xn", [128, D], F32)
    s_xnb = ctx.sb(pre + "xnb", [128, D], BF16)
    s_x2b = ctx.sb(pre + "x2b", [128, D], BF16) if affine else None
    s_tr = ctx.sb(pre + "tr", [128, 512], BF16)
    s_id = ctx.sb("s_ident")
    p_tr = ctx.ps(pre + "ptr", [128, 512], BF16)

    grep = brep = None
    pend = []
    if affine:
        gamma_ap, beta_ap = affine
        s_grow = ctx.sb(pre + "grow", [1, D], F32)
        s_brow = ctx.sb(pre + "brow", [1, D], F32)
        lg = r.op("sync", lambda e: e.dma_start(out=s_grow[:], in_=gamma_ap[None, :]), dma=True)
        lb = r.op("sync", lambda e: e.dma_start(out=s_brow[:], in_=beta_ap[None, :]), dma=True)
        grep, gd = bcast_row(ctx, pre + "g", s_grow, D, F32, lg)
        brep, bd = bcast_row(ctx, pre + "b", s_brow, D, F32, lb)
        pend = [gd, bd]

    dones = []
    ld_prev = None
    ld_cur = r.op("sync", lambda e: e.dma_start(out=s_x[0][:], in_=x_ap[0:128, :]),
                  dma=True, waits=tuple(after))
    last_tile = None
    for i in range(NT):
        x = s_x[i % 2]
        em = chain_maker(r, ld_cur, last_tile, *(pend if i == 0 else []))
        if i + 1 < NT:
            ld_nxt = r.op("sync", lambda e, i=i: e.dma_start(
                out=s_x[(i + 1) % 2][:], in_=x_ap[128 * (i + 1):128 * (i + 2), :]),
                dma=True, waits=(ld_prev, last_tile))
        em("scalar", lambda e, x=x: e.activation(s_sq[:], x[:], AF.Square, accum_out=s_st[:, 1:2]))
        em("vector", lambda e, x=x: e.reduce_sum(s_st[:, 0:1], x[:], AX.X))
        em("vector", lambda e: e.tensor_scalar_mul(s_st[:, 2:3], s_st[:, 0:1], 1.0 / D))
        em("vector", lambda e: e.tensor_scalar_mul(s_st[:, 6:7], s_st[:, 1:2], 1.0 / D))
        em("vector", lambda e: e.tensor_tensor(s_st[:, 7:8], s_st[:, 2:3], s_st[:, 2:3], ALU.mult))
        em("vector", lambda e: e.tensor_tensor(s_st[:, 3:4], s_st[:, 6:7], s_st[:, 7:8], ALU.subtract))
        em("scalar", lambda e: e.activation(s_st[:, 4:5], s_st[:, 3:4], AF.Ln, bias=EPS))
        em("scalar", lambda e: e.activation(s_st[:, 5:6], s_st[:, 4:5], AF.Exp, scale=-0.5))
        xnd = em("vector", lambda e, x=x: e.tensor_scalar(
            s_xn[:], x[:], s_st[:, 2:3], s_st[:, 5:6], ALU.subtract, ALU.mult))
        em("vector", lambda e: e.tensor_copy(s_xnb[:], s_xn[:]))
        if outs.get("xn_f32") is not None:
            em("sync", lambda e, i=i: e.dma_start(
                out=outs["xn_f32"][128 * i:128 * (i + 1), :], in_=s_xn[:]), dma=True)
        if outs.get("xn_bf") is not None:
            em("sync", lambda e, i=i: e.dma_start(
                out=outs["xn_bf"][128 * i:128 * (i + 1), :], in_=s_xnb[:]), dma=True)
        if affine:
            em("scalar", lambda e: e.activation(s_st[:, 8:9], s_st[:, 4:5], AF.Exp, scale=-1.0))
            em("vector", lambda e: e.tensor_tensor(s_st[:, 9:10], s_st[:, 3:4], s_st[:, 8:9], ALU.mult))
            em("scalar", lambda e: e.activation(s_st[:, 10:11], s_st[:, 9:10], AF.Ln, bias=EPS))
            em("scalar", lambda e: e.activation(s_st[:, 11:12], s_st[:, 10:11], AF.Exp, scale=-0.5))
            em("vector", lambda e: e.tensor_scalar(s_af[:], s_xn[:], s_st[:, 11:12], None, ALU.mult))
            em("vector", lambda e: e.tensor_tensor(s_af[:], s_af[:], grep[:], ALU.mult))
            em("vector", lambda e: e.tensor_add(s_af[:], s_af[:], brep[:]))
            em("vector", lambda e: e.tensor_copy(s_x2b[:], s_af[:]))
        for (key, src) in (("xnT_bf", s_xnb), ("xn2T_bf", s_x2b)):
            if outs.get(key) is None:
                continue
            for c in range(0, 8, 4):
                for cc in range(c, c + 4):
                    em("tensor", lambda e, src=src, cc=cc: e.transpose(
                        p_tr[:, 128 * (cc % 4):128 * (cc % 4 + 1)],
                        src[:, 128 * cc:128 * (cc + 1)], s_id[:]))
                em("tensor", lambda e: e.matmul(
                    ctx.ps("p_guard")[:], ctx.sb("s_onesbf")[0:1, 0:1],
                    ctx.sb("s_onesbf")[0:1, 0:16], start=True, stop=True))
                em("scalar", lambda e: e.copy(s_tr[:], p_tr[:]))
                em("sync", lambda e, key=key, i=i, c=c: e.dma_start(
                    out=dap(outs[key], 128 * c * T + 128 * i,
                            [[T, 128], [128 * T, 4], [1, 128]]),
                    in_=s_tr[:].rearrange("p (a t) -> p a t", a=4)), dma=True)
        last_tile = em("vector", lambda e: e.tensor_copy(s_st[:, 0:1], s_st[:, 0:1]))
        dones.append(last_tile)
        ld_prev = ld_cur
        if i + 1 < NT:
            ld_cur = ld_nxt
    return dones[-1:]


def vec2cols(ctx, pre, vec_ap, n, ld_waits=(), dt_in=F32, psname=None):
    r = ctx.rec
    nch = n // 128
    s_row = ctx.sb(pre + "row", [1, n], dt_in)
    s_col = ctx.sb(pre + "col", [128, nch], dt_in)
    s_id = ctx.sb("s_identf")
    p = ctx.ps(psname or (pre + "pv"), [128, 16], F32)
    ld = r.op("sync", lambda e: e.dma_start(out=s_row[:], in_=vec_ap[None, :]),
              dma=True, waits=ld_waits)
    done = None
    for j in range(nch):
        mm = r.op("tensor", lambda e, j=j: e.transpose(
            p[:, j % 16:j % 16 + 1], s_row[:, 128 * j:128 * (j + 1)], s_id[0:1, 0:1]),
            mark=True, waits=(ld, done))
        g = pe_done(ctx, mm)
        done = r.op("scalar", lambda e, j=j: e.copy(
            s_col[:, j:j + 1], p[:, j % 16:j % 16 + 1]), mark=True, waits=(g,))
    return s_col, done


def wcast_load(ctx, name, w_ap, rows, row0, cols, col0, shape, waits=()):
    r = ctx.rec
    wrow = w_ap.shape[1]
    if rows < 128:
        t = ctx.sb(name, [rows, cols], BF16)
        lab = r.op("gpsimd", lambda e: e.dma_start(
            out=t[:], in_=dap(w_ap, row0 * wrow + col0, [[wrow, rows], [1, cols]])),
            dma=True, waits=waits)
        return t, lab
    nci = rows // 128
    t = ctx.sb(name, [128, nci * cols], BF16)
    lab = r.op("gpsimd", lambda e: e.dma_start(
        out=t[:].rearrange("p (a c) -> p a c", a=nci),
        in_=dap(w_ap, row0 * wrow + col0, [[wrow, 128], [128 * wrow, nci], [1, cols]])),
        dma=True, waits=waits)
    return t, lab


def phase_mamba_front(ctx, pre, xn2T_ap, W_in_ap, b_in_ap, conv_w_ap, conv_b_ap,
                      W_x_ap, W_dt_ap, b_dt_ap, outs, after=()):
    r = ctx.rec
    TC = 512
    NTC = T // TC
    bin_u, d1 = vec2cols(ctx, pre + "biu", b_in_ap[0:DI], DI, after, psname=pre + "pv")
    bin_r, d2 = vec2cols(ctx, pre + "bir", b_in_ap[DI:2 * DI], DI, psname=pre + "pv")
    cb, d3 = vec2cols(ctx, pre + "cb", conv_b_ap, DI, psname=pre + "pv")
    bdt, d4 = vec2cols(ctx, pre + "bdt", b_dt_ap, DI, psname=pre + "pv")
    cw = []
    for k in range(KC):
        cwk, d5 = vec2cols(ctx, pre + f"cw{k}", conv_w_ap[k, :], DI, psname=pre + "pv")
        cw.append(cwk)
    vdone = (d1, d2, d3, d4, d5)

    s_xn2 = ctx.sb(pre + "xn2", [128, ND * T], BF16)
    xl = []
    for c in range(ND):
        xl.append(r.op("sync", lambda e, c=c: e.dma_start(
            out=s_xn2[:, T * c:T * (c + 1)], in_=xn2T_ap[128 * c:128 * (c + 1), :]),
            dma=True, waits=tuple(after)))
    s_w = [ctx.sb(pre + f"w{k}", [128, 8 * 128], BF16) for k in range(2)]
    p_mm = [ctx.ps(pre + f"pm{k}", [128, 512], F32) for k in range(2)]
    s_u = ctx.sb(pre + "u", [128, T + KC - 1], F32)
    s_acc = ctx.sb(pre + "acc", [128, T], F32)
    s_ob = ctx.sb(pre + "ob", [128, T], BF16)
    zpad = r.op("vector", lambda e: e.memset(s_u[:, 0:KC - 1], 0.0), mark=True, waits=tuple(after))

    wl_prev = [None, None]
    ev_prev = [None, None]
    st_prev = None
    u_stores = {}
    NCO = 2 * DI // 128
    for co in range(NCO):
        b = co % 2
        wl = r.op("gpsimd", lambda e, co=co, b=b: e.dma_start(
            out=s_w[b][:].rearrange("p (a c) -> p a c", a=8),
            in_=dap(W_in_ap, 128 * co, [[4096, 128], [128 * 4096, 8], [1, 128]])),
            dma=True, waits=(wl_prev[b],))
        is_u = co < NDI
        evs = []
        for tc in range(NTC):
            pb = tc % 2
            mms = []
            for ci in range(ND):
                mms.append(r.op("tensor", lambda e, b=b, pb=pb, ci=ci, tc=tc: e.matmul(
                    p_mm[pb][:], s_w[b][:, 128 * ci:128 * (ci + 1)],
                    s_xn2[:, T * ci + TC * tc: T * ci + TC * (tc + 1)],
                    start=(ci == 0), stop=(ci == ND - 1)),
                    mark=(ci == ND - 1),
                    waits=(wl, xl[-1], ev_prev[pb]) if ci == 0 else ()))
            g = pe_done(ctx, mms[-1])
            if is_u:
                ev = r.op("scalar", lambda e, pb=pb, tc=tc, co=co: e.activation(
                    s_u[:, KC - 1 + TC * tc: KC - 1 + TC * (tc + 1)], p_mm[pb][:],
                    AF.Identity, bias=bin_u[:, co:co + 1]),
                    mark=True, waits=(g, vdone[0], zpad))
            else:
                ev = r.op("scalar", lambda e, pb=pb, tc=tc, co=co: e.activation(
                    s_ob[:, TC * tc: TC * (tc + 1)], p_mm[pb][:],
                    AF.Silu, bias=bin_r[:, co - NDI:co - NDI + 1]),
                    mark=True, waits=(g, vdone[1], st_prev))
            ev_prev[pb] = ev
            evs.append(ev)
        if is_u:
            c0 = r.op("vector", lambda e, co=co: e.tensor_scalar(
                s_acc[:], s_u[:, 0:T], cw[0][:, co:co + 1], None, ALU.mult),
                mark=True, waits=(evs[-1], vdone[2], st_prev))
            for k in range(1, KC):
                c0 = r.op("vector", lambda e, k=k, co=co: e.scalar_tensor_tensor(
                    s_acc[:], s_u[:, k:T + k], cw[k][:, co:co + 1], s_acc[:],
                    ALU.mult, ALU.add), mark=True, waits=(c0,))
            sl = r.op("scalar", lambda e, co=co: e.activation(
                s_ob[:, 0:T], s_acc[:], AF.Silu, bias=cb[:, co:co + 1]),
                mark=True, waits=(c0,))
            st = r.op("sync", lambda e, co=co: e.dma_start(
                out=outs["uT_bf"][128 * co:128 * (co + 1), :], in_=s_ob[:, 0:T]),
                dma=True, waits=(sl,))
            u_stores[co] = st
        else:
            st = r.op("sync", lambda e, co=co: e.dma_start(
                out=outs["resT_bf"][128 * (co - NDI):128 * (co - NDI + 1), :],
                in_=s_ob[:, 0:T]), dma=True, waits=(evs[-1],))
        st_prev = st
        wl_prev[b] = st if not is_u else st
    s_wx, wxl = wcast_load(ctx, pre + "wx", W_x_ap, DI, 0, 96, 0, None, waits=())
    s_wdt, wdtl = wcast_load(ctx, pre + "wdt", W_dt_ap, 64, 0, DI, 0, None, waits=())
    s_ut = [ctx.sb(pre + f"ut{k}", [128, T], BF16) for k in range(2)]
    s_xd = ctx.sb(pre + "xd", [96, T], F32)
    s_xdb = ctx.sb(pre + "xdb", [64, T], BF16)
    p_xd = ctx.ps(pre + "pxd", [96, 512], F32)
    ul_prev = [None, None]
    ev = None
    for tc in range(NTC):
        mms = []
        for di in range(NDI):
            b = di % 2
            ul = r.op("sync", lambda e, di=di, tc=tc, b=b: e.dma_start(
                out=s_ut[b][:, 0:512], in_=outs["uT_bf"][128 * di:128 * (di + 1), TC * tc:TC * (tc + 1)]),
                dma=True, waits=(ul_prev[b], u_stores[di]))
            mms.append(r.op("tensor", lambda e, b=b, di=di, tc=tc: e.matmul(
                p_xd[:], s_wx[:, 96 * di:96 * (di + 1)], s_ut[b][:, 0:512],
                start=(di == 0), stop=(di == NDI - 1)),
                mark=True, waits=(ul, wxl) + ((ev,) if di == 0 else ())))
            ul_prev[b] = mms[-1]
        g = pe_done(ctx, mms[-1])
        ev = r.op("scalar", lambda e, tc=tc: e.copy(
            s_xd[:, TC * tc:TC * (tc + 1)], p_xd[:]), mark=True, waits=(g,))
    bc = r.op("sync", lambda e: e.dma_start(
        out=outs["bc_f32"], in_=s_xd[64:96, :]), dma=True, waits=(ev,))
    dtb = r.op("vector", lambda e: e.tensor_copy(s_xdb[:], s_xd[0:64, :]), mark=True, waits=(ev,))
    p_dt = [ctx.ps(pre + f"pdt{k}", [128, 512], F32) for k in range(2)]
    s_z = ctx.sb(pre + "z", [128, 512], F32)
    s_a = ctx.sb(pre + "a", [128, 512], F32)
    s_db = ctx.sb(pre + "db", [128, 512], BF16)
    ev2 = [None, None]
    x5 = None
    dones = []
    for di in range(NDI):
        for tc in range(NTC):
            pb = (di * NTC + tc) % 2
            mm = r.op("tensor", lambda e, di=di, tc=tc, pb=pb: e.matmul(
                p_dt[pb][:], s_wdt[:, 128 * di:128 * (di + 1)],
                s_xdb[:, TC * tc:TC * (tc + 1)], start=True, stop=True),
                mark=True, waits=(dtb, wdtl, ev2[pb]))
            g = pe_done(ctx, mm)
            z = r.op("scalar", lambda e, pb=pb, di=di: e.activation(
                s_z[:], p_dt[pb][:], AF.Identity, bias=bdt[:, di:di + 1]),
                mark=True, waits=(g, vdone[3], x5, dones[-1] if dones else None))
            ev2[pb] = z
            a = r.op("scalar", lambda e: e.activation(s_a[:], s_z[:], AF.Abs), mark=True, waits=(z,))
            x1 = r.op("scalar", lambda e: e.activation(s_a[:], s_a[:], AF.Exp, scale=-1.0), mark=True, waits=(a,))
            x2 = r.op("scalar", lambda e: e.activation(s_a[:], s_a[:], AF.Ln, bias=1.0), mark=True, waits=(x1,))
            x3 = r.op("vector", lambda e: e.tensor_single_scalar(s_z[:], s_z[:], 0.0, ALU.max), mark=True, waits=(x2,))
            x4 = r.op("vector", lambda e: e.tensor_add(s_z[:], s_z[:], s_a[:]), mark=True, waits=(x3,))
            x5 = r.op("vector", lambda e: e.tensor_copy(s_db[:], s_z[:]), mark=True, waits=(x4,))
            dones.append(r.op("sync", lambda e, di=di, tc=tc: e.dma_start(
                out=outs["deltaT_bf"][128 * di:128 * (di + 1), TC * tc:TC * (tc + 1)],
                in_=s_db[:]), dma=True, waits=(x5,)))
    return dones[-1:] + [bc]


def a_colmaj_host(A_log):
    A = np.asarray(A_log).reshape(16, 16, 8, NS)
    return np.ascontiguousarray(A.transpose(2, 3, 0, 1).reshape(128, 256))


def phase_scan(ctx, pre, deltaT_ap, uT_ap, resT_ap, bc_ap, A_colmaj_ap, D_ap,
               outs, after=()):
    r = ctx.rec
    TC = 512
    NTC = T // TC

    rep_pat_ap = ctx.din("c_rep_pat", [128, 2048], BF16)
    red_pat_ap = ctx.din("c_red_pat", [128, 2048], BF16)
    bc_pat_ap = ctx.din("c_bc_pat", [16, 128], BF16)
    s_rep = ctx.sb(pre + "reppat", [128, 2048], BF16)
    s_red = ctx.sb(pre + "redpat", [128, 2048], BF16)
    s_bcp = ctx.sb(pre + "bcpat", [16, 128], BF16)
    lds = [r.op("sync", lambda e: e.dma_start(out=s_rep[:], in_=rep_pat_ap), dma=True, waits=tuple(after)),
           r.op("sync", lambda e: e.dma_start(out=s_red[:], in_=red_pat_ap), dma=True),
           r.op("sync", lambda e: e.dma_start(out=s_bcp[:], in_=bc_pat_ap), dma=True)]
    s_acol = ctx.sb(pre + "acol", [128, 256], F32)
    p_sc = [ctx.ps(pre + f"psc{k}", [128, 512], F32) for k in range(3)]
    la = r.op("sync", lambda e: e.dma_start(out=s_acol[:], in_=A_colmaj_ap), dma=True)
    ea = r.op("scalar", lambda e: e.activation(s_acol[:], s_acol[:], AF.Exp), mark=True, waits=(la,))
    na = r.op("vector", lambda e: e.tensor_scalar_mul(s_acol[:], s_acol[:], -1.0), mark=True, waits=(ea,))
    dcol, dd = vec2cols(ctx, pre + "dcol", D_ap, DI, psname=pre + "psc2")

    s_bf = ctx.sb(pre + "bf", [16, T], F32)
    s_cf = ctx.sb(pre + "cf", [16, T], F32)
    s_bb = ctx.sb(pre + "bb", [16, T], BF16)
    s_cb = ctx.sb(pre + "cb2", [16, T], BF16)
    s_brep = ctx.sb(pre + "brep", [128, T], F32)
    s_crep = ctx.sb(pre + "crep", [128, T], F32)
    p_bc = p_sc[0]
    lb_ = r.op("sync", lambda e: e.dma_start(out=s_bf[:], in_=bc_ap[0:16, :]), dma=True)
    lc_ = r.op("sync", lambda e: e.dma_start(out=s_cf[:], in_=bc_ap[16:32, :]), dma=True)
    cb1 = r.op("vector", lambda e: e.tensor_copy(s_bb[:], s_bf[:]), mark=True, waits=(lb_,))
    cb2 = r.op("vector", lambda e: e.tensor_copy(s_cb[:], s_cf[:]), mark=True, waits=(lc_,))
    ev = None
    for (srcb, dst, cbl) in ((s_bb, s_brep, cb1), (s_cb, s_crep, cb2)):
        for tc in range(NTC):
            mm = r.op("tensor", lambda e, srcb=srcb, tc=tc: e.matmul(
                p_bc[:], s_bcp[:], srcb[:, TC * tc:TC * (tc + 1)],
                start=True, stop=True), mark=True, waits=(cbl, lds[2], ev))
            g = pe_done(ctx, mm)
            ev = r.op("scalar", lambda e, dst=dst, tc=tc: e.copy(
                dst[:, TC * tc:TC * (tc + 1)], p_bc[:]), mark=True, waits=(g,))

    s_dl = [ctx.sb(pre + f"dl{k}", [128, T], BF16) for k in range(2)]
    s_ul = [ctx.sb(pre + f"ul{k}", [128, T], BF16) for k in range(2)]
    s_rl = [ctx.sb(pre + f"rl{k}", [128, T], BF16) for k in range(2)]
    s_du = [ctx.sb(pre + f"du{k}", [128, T], BF16) for k in range(2)]
    s_dA = [ctx.sb(pre + f"dA{k}", [128, T], F32) for k in range(2)]
    s_x = [ctx.sb(pre + f"x{k}", [128, T], BF16) for k in range(2)]
    s_h = [ctx.sb(pre + f"h{k}", [128, T], F32) for k in range(2)]
    s_hc = [ctx.sb(pre + f"hc{k}", [128, T], BF16) for k in range(2)]
    s_ep = ctx.sb(pre + "ep", [128, T], F32)
    s_yb = ctx.sb(pre + "yb", [128, T], BF16)
    p_y = [ctx.ps(pre + f"py{k}", [128, 512], F32) for k in range(4)]

    ld_prev = [None, None]
    scan_prev = None
    hc_prev = [None, None]
    yev_prev = None
    st_prev = None
    scan_hist = []
    ymm3_hist = []
    pend_ymm = None
    rep_mms = []
    rep_evs = {-1: ev}
    pend_rev = None
    dones = []
    for c in range(NDI):
        b = c % 2
        ldd = r.op("sync", lambda e, c=c, b=b: e.dma_start(
            out=s_dl[b][:], in_=deltaT_ap[128 * c:128 * (c + 1), :]), dma=True,
            waits=(ld_prev[b],))
        ldu = r.op("sync", lambda e, c=c, b=b: e.dma_start(
            out=s_ul[b][:], in_=uT_ap[128 * c:128 * (c + 1), :]), dma=True)
        ldr = r.op("sync", lambda e, c=c, b=b: e.dma_start(
            out=s_rl[b][:], in_=resT_ap[128 * c:128 * (c + 1), :]), dma=True)
        du = r.op("gpsimd", lambda e, b=b: e.tensor_tensor(
            s_du[b][:], s_dl[b][:], s_ul[b][:], ALU.mult), mark=True,
            waits=(ldd, ldu))
        for j in range(16):
            col = c * 16 + j
            jb = j % 2
            scan_g2 = scan_hist[-2] if len(scan_hist) >= 2 else None
            j_evs = []

            def mk_rev(i, n, col=col, jb=jb, scan_g2=scan_g2):
                tc = i % NTC
                bk = n % 3

                def emit(guard):
                    if i < NTC:
                        ev = r.op("scalar", lambda e: e.activation(
                            s_dA[jb][:, TC * tc:TC * (tc + 1)], p_sc[bk][:], AF.Exp,
                            scale=s_acol[:, col:col + 1]),
                            mark=True, waits=(guard, na, scan_g2))
                    else:
                        ev = r.op("vector", lambda e: e.scalar_tensor_tensor(
                            s_x[jb][:, TC * tc:TC * (tc + 1)], p_sc[bk][:], 1.0,
                            s_brep[:, TC * tc:TC * (tc + 1)], ALU.mult, ALU.mult),
                            mark=True, waits=(guard, scan_g2))
                    rep_evs[n] = ev
                    j_evs.append(ev)
                return emit

            for i in range(8):
                n = len(rep_mms)
                tc = i % NTC
                src = s_dl if i < NTC else s_du
                mm = r.op("tensor", lambda e, b=b, j=j, tc=tc, src=src, bk=n % 3: e.matmul(
                    p_sc[bk][:], s_rep[:, 128 * j:128 * (j + 1)],
                    src[b][:, TC * tc:TC * (tc + 1)], start=True, stop=True),
                    mark=True,
                    waits=(lds[0], ldd if i < NTC else du,
                           rep_evs.get(n - 3, rep_evs[-1] if n < 3 else None),
                           dd if n < 3 else None))
                rep_mms.append(mm)
                if pend_rev is not None:
                    pend_rev(mm)
                pend_rev = mk_rev(i, n)
            pend_rev(pe_done(ctx, rep_mms[-1]))
            pend_rev = None
            dA_last = j_evs[3]
            x_last = j_evs[7]
            scan = r.op("vector", lambda e, jb=jb: e.tensor_tensor_scan(
                s_h[jb][:], s_dA[jb][:], s_x[jb][:], 0.0, ALU.mult, ALU.add),
                mark=True, waits=(dA_last, x_last, hc_prev[jb]))
            scan_hist.append(scan)
            scan_prev = scan
            hc = r.op("gpsimd", lambda e, jb=jb: e.tensor_tensor(
                s_hc[jb][:], s_h[jb][:], s_crep[:], ALU.mult), mark=True,
                waits=(scan, yev_prev if j == 0 else None,
                       ymm3_hist[-1] if ymm3_hist else None))
            hc_prev[jb] = hc
            def emit_ymm(hcl, j_, jb_):
                ymm = []
                for tc in range(NTC):
                    ymm.append(r.op("tensor", lambda e, j_=j_, tc=tc, jb_=jb_: e.matmul(
                        p_y[tc][:], s_red[:, 128 * j_:128 * (j_ + 1)],
                        s_hc[jb_][:, TC * tc:TC * (tc + 1)],
                        start=(j_ == 0), stop=(j_ == 15)),
                        mark=(tc == NTC - 1 or j_ == 15), waits=(hcl, lds[1])))
                ymm3_hist.append(ymm[-1])
                return ymm
            if pend_ymm is not None:
                emit_ymm(*pend_ymm)
            pend_ymm = (hc, j, jb)
        ymm = emit_ymm(*pend_ymm)
        pend_ymm = None
        yevs = []
        for tc in range(NTC):
            g = pe_done(ctx, ymm[tc]) if tc == NTC - 1 else ymm[tc + 1]
            e1 = r.op("vector", lambda e, b=b, c=c, tc=tc: e.scalar_tensor_tensor(
                s_ep[:, TC * tc:TC * (tc + 1)], s_ul[b][:, TC * tc:TC * (tc + 1)],
                dcol[:, c:c + 1], p_y[tc][:], ALU.mult, ALU.add),
                mark=True, waits=(g, dd, yev_prev if tc == 0 else None))
            e2 = r.op("gpsimd", lambda e, b=b, tc=tc: e.tensor_tensor(
                s_yb[:, TC * tc:TC * (tc + 1)], s_ep[:, TC * tc:TC * (tc + 1)],
                s_rl[b][:, TC * tc:TC * (tc + 1)], ALU.mult),
                mark=True, waits=(e1, ldr, st_prev))
            yevs.append(e2)
        yev_prev = yevs[-1]
        st = r.op("sync", lambda e, c=c: e.dma_start(
            out=outs["ybT_bf"][128 * c:128 * (c + 1), :], in_=s_yb[:]),
            dma=True, waits=tuple(yevs))
        st_prev = st
        ld_prev[b] = st
        dones.append(st)
    return dones[-1:]


def phase_matmul_tok(ctx, pre, actT_ap, K, w_ap, N_out, out_ap, out_dt,
                     bias_ap=None, inject_ap=None, act=None, after=(),
                     w_pre_bf16=False):
    r = ctx.rec
    NK = K // 128
    NN = N_out // 512
    s_id = ctx.sb("s_ident")
    s_act = [ctx.sb(pre + f"a{k}", [128, NK * 128], BF16) for k in range(2)]
    s_w = [ctx.sb(pre + f"w{k}", [128, NK * 512], BF16) for k in range(2)]
    s_inj = ctx.sb(pre + "inj", [128, 512], BF16) if inject_ap is not None else None
    s_o = ctx.sb(pre + "o", [128, 512], out_dt)
    p_mm = [ctx.ps(pre + f"p{k}", [128, 512], F32) for k in range(2)]
    bias_row = None
    if bias_ap is not None:
        s_br = ctx.sb(pre + "br", [1, N_out], BF16)
        blb = r.op("gpsimd", lambda e: e.dma_start(out=s_br[:], in_=bias_ap[None, :]),
                   dma=True, waits=tuple(after))
        bias_row = s_br
    wrow = N_out
    inj_prev = [None]
    wl_prev = [None, None]
    al_prev = [None, None]
    ev_prev = [None, None]
    dones = []
    for n in range(NN):
        wb = n % 2
        wl = r.op("gpsimd", lambda e, n=n, wb=wb: e.dma_start(
            out=s_w[wb][:].rearrange("p (a c) -> p a c", a=NK),
            in_=dap(w_ap, 512 * n, [[wrow, 128], [128 * wrow, NK], [1, 512]])),
            dma=True, waits=(wl_prev[wb],) + tuple(after))
        for i in range(NT):
            ab = i % 2
            al = r.op("sync", lambda e, i=i, ab=ab: e.dma_start(
                out=s_act[ab][:].rearrange("p (a c) -> p a c", a=NK),
                in_=dap(actT_ap, 128 * i, [[T, 128], [128 * T, NK], [1, 128]])),
                dma=True, waits=(al_prev[ab],) + tuple(after))
            al_prev[ab] = al
            pb = (n * NT + i) % 2
            last_plain = bias_ap is None and inject_ap is None
            mms = []
            for k in range(NK):
                mms.append(r.op("tensor", lambda e, k=k, wb=wb, ab=ab, pb=pb,
                                lp=last_plain: e.matmul(
                    p_mm[pb][:], s_act[ab][:, 128 * k:128 * (k + 1)],
                    s_w[wb][:, 512 * k:512 * (k + 1)],
                    start=(k == 0), stop=(lp and k == NK - 1)),
                    mark=False, waits=(wl, al, ev_prev[pb]) if k == 0 else ()))
            if bias_ap is not None:
                r.op("tensor", lambda e, pb=pb, n=n: e.matmul(
                    p_mm[pb][:], ctx.sb("s_onesbf")[0:1, 0:128],
                    bias_row[:, 512 * n:512 * (n + 1)], start=False,
                    stop=inject_ap is None), mark=False, waits=(blb,))
            if inject_ap is not None:
                il = r.op("sync", lambda e, i=i, n=n: e.dma_start(
                    out=s_inj[:], in_=inject_ap[128 * i:128 * (i + 1), 512 * n:512 * (n + 1)]),
                    dma=True, waits=(inj_prev[0],))
                inj_prev[0] = r.op("tensor", lambda e, pb=pb: e.matmul(
                    p_mm[pb][:], s_id[:], s_inj[:], start=False, stop=True),
                    mark=True, waits=(il,))
            g = pe_done(ctx)
            ev = r.op("scalar", lambda e, pb=pb: e.activation(
                s_o[:], p_mm[pb][:], act if act is not None else AF.Identity),
                mark=True, waits=(g,))
            ev_prev[pb] = ev
            al_prev[ab] = ev
            dones.append(r.op("sync", lambda e, i=i, n=n: e.dma_start(
                out=out_ap[128 * i:128 * (i + 1), 512 * n:512 * (n + 1)],
                in_=s_o[:]), dma=True, waits=(ev,)))
        wl_prev[wb] = dones[-1]
    return dones[-1:]


def phase_gate(ctx, pre, xnT_ap, gate_wT_ap, gate_b_ap, combine_ap,
               combineT_ap, after=()):
    r = ctx.rec
    E, G = 16, 4
    s_gw = ctx.sb(pre + "gw", [128, ND * E], BF16)
    gl = r.op("gpsimd", lambda e: e.dma_start(
        out=s_gw[:].rearrange("p (a c) -> p a c", a=ND),
        in_=dap(gate_wT_ap, 0, [[E, 128], [128 * E, ND], [1, E]])),
        dma=True, waits=tuple(after))
    s_gb = ctx.sb(pre + "gb", [1, E], F32)
    gbl = r.op("sync", lambda e: e.dma_start(out=s_gb[:], in_=gate_b_ap[None, :]),
               dma=True, waits=tuple(after))
    gbr, gbd = bcast_row(ctx, pre + "gb", s_gb, E, F32, gbl)
    s_a = ctx.sb(pre + "act", [128, 128], BF16)
    s_sc = ctx.sb(pre + "sc", [128, E], F32)
    s_st = ctx.sb(pre + "sst", [128, 8], F32)
    s_orig = ctx.sb(pre + "orig", [128, E], F32)
    s_sb = ctx.sb(pre + "ssb", [128, E], F32)
    s_g2 = ctx.sb(pre + "g2", [128, G], F32)
    s_pr = ctx.sb(pre + "pr", [128, 6], F32)
    s_kg = ctx.sb(pre + "kg", [128, E], F32)
    s_cmb = ctx.sb(pre + "cmb", [128, E], F32)
    s_cmbb = ctx.sb(pre + "cmbb", [128, E], BF16)
    s_ct = ctx.sb(pre + "ct", [16, 128], BF16)
    p_g = ctx.ps(pre + "pg", [128, E], F32)
    p_t = ctx.ps(pre + "pt", [16, 128], BF16)
    s_id = ctx.sb("s_ident")
    PAIRS = [(0, 1), (0, 2), (0, 3), (1, 2), (1, 3), (2, 3)]
    prev = None
    dones = []
    for i in range(NT):
        em = chain_maker(r, gl, gbd, prev, *(after if i == 0 else ()))
        em("sync", lambda e, i=i: e.dma_start(
            out=s_a[:], in_=xnT_ap.tensor.ap()[0:0 + 128, 0:128]), dma=True) \
            if False else None
        for c in range(ND):
            ldc = em("sync", lambda e, i=i, c=c: e.dma_start(
                out=s_a[:], in_=xnT_ap[128 * c:128 * (c + 1), 128 * i:128 * (i + 1)]),
                dma=True)
            em("tensor", lambda e, c=c: e.matmul(
                p_g[:], s_a[:], s_gw[:, E * c:E * (c + 1)],
                start=(c == 0), stop=(c == ND - 1)))
        em("tensor", lambda e: e.matmul(
            ctx.ps("p_guard")[:], ctx.sb("s_onesbf")[0:1, 0:1],
            ctx.sb("s_onesbf")[0:1, 0:16], start=True, stop=True))
        em("scalar", lambda e: e.activation(
            s_sc[:], p_g[:], AF.Exp, accum_out=s_st[:, 0:1]))
        em("vector", lambda e: e.reciprocal(s_st[:, 1:2], s_st[:, 0:1]))
        em("vector", lambda e: e.tensor_scalar(
            s_orig[:], s_sc[:], s_st[:, 1:2], None, ALU.mult))
        em("vector", lambda e: e.tensor_add(s_sb[:], s_orig[:], gbr[:, 0:E]))
        for k, (a, b) in enumerate(PAIRS):
            em("vector", lambda e, k=k, a=a, b=b: e.tensor_tensor(
                s_pr[:, k:k + 1].to_broadcast([128, 1]) if False else s_pr[:, k:k + 1],
                s_sb[:, a::4][:, 0:1] if False else s_sb[:, a:a + 13:4][:, 0:0] if False else s_sb[:, a + 0:a + 1],
                s_sb[:, b:b + 1], ALU.add)) if False else None
        for g in range(G):
            for k, (a, b) in enumerate(PAIRS):
                em("vector", lambda e, g=g, k=k, a=a, b=b: e.tensor_tensor(
                    s_pr[:, k:k + 1], s_sb[:, 4 * g + a:4 * g + a + 1],
                    s_sb[:, 4 * g + b:4 * g + b + 1], ALU.add))
            em("vector", lambda e: e.reduce_max(s_st[:, 2:3], s_pr[:, 0:6], AX.X))
            em("vector", lambda e, g=g: e.tensor_copy(s_g2[:, g:g + 1], s_st[:, 2:3]))
        em("vector", lambda e: e.tensor_max(s_st[:, 3:4], s_g2[:, 0:1], s_g2[:, 1:2]))
        em("vector", lambda e: e.tensor_tensor(s_st[:, 4:5], s_g2[:, 0:1], s_g2[:, 1:2], ALU.min))
        em("vector", lambda e: e.tensor_max(s_st[:, 5:6], s_g2[:, 2:3], s_g2[:, 3:4]))
        em("vector", lambda e: e.tensor_tensor(s_st[:, 6:7], s_g2[:, 2:3], s_g2[:, 3:4], ALU.min))
        em("vector", lambda e: e.tensor_tensor(s_st[:, 7:8], s_st[:, 3:4], s_st[:, 5:6], ALU.min))
        em("vector", lambda e: e.tensor_max(s_st[:, 4:5], s_st[:, 4:5], s_st[:, 6:7]))
        em("vector", lambda e: e.tensor_max(s_st[:, 7:8], s_st[:, 7:8], s_st[:, 4:5]))
        em("vector", lambda e: e.tensor_scalar(
            s_g2[:], s_g2[:], s_st[:, 7:8], None, ALU.is_ge))
        for j in range(4):
            em("vector", lambda e, j=j: e.tensor_copy(
                AP(s_kg, j, [[E, 128], [4, 4]]), s_g2[:]))
        em("vector", lambda e: e.tensor_tensor(s_sb[:], s_sb[:], s_kg[:], ALU.mult))
        em("vector", lambda e: e.tensor_scalar(s_kg[:], s_kg[:], 1e9, -1e9, ALU.mult, ALU.add))
        em("vector", lambda e: e.tensor_tensor(s_sb[:], s_sb[:], s_kg[:], ALU.add))
        em("vector", lambda e: e.reduce_max(s_st[:, 2:3], s_sb[:], AX.X))
        em("vector", lambda e: e.tensor_scalar(
            s_kg[:], s_sb[:], s_st[:, 2:3], None, ALU.is_ge))
        em("vector", lambda e: e.scalar_tensor_tensor(
            s_kg[:], s_kg[:], -2e9, s_sb[:], ALU.mult, ALU.add))
        em("vector", lambda e: e.reduce_max(s_st[:, 3:4], s_kg[:], AX.X))
        em("vector", lambda e: e.tensor_scalar(
            s_kg[:], s_sb[:], s_st[:, 3:4], None, ALU.is_ge))
        em("vector", lambda e: e.tensor_tensor(s_cmb[:], s_orig[:], s_kg[:], ALU.mult))
        em("vector", lambda e: e.tensor_copy(s_cmbb[:], s_cmb[:]))
        st1 = em("sync", lambda e, i=i: e.dma_start(
            out=combine_ap[128 * i:128 * (i + 1), :], in_=s_cmb[:]), dma=True)
        em("tensor", lambda e: e.transpose(p_t[:], s_cmbb[:], s_id[:]))
        em("tensor", lambda e: e.matmul(
            ctx.ps("p_guard")[:], ctx.sb("s_onesbf")[0:1, 0:1],
            ctx.sb("s_onesbf")[0:1, 0:16], start=True, stop=True))
        em("scalar", lambda e: e.copy(s_ct[:], p_t[:]))
        prev = em("sync", lambda e, i=i: e.dma_start(
            out=combineT_ap[:, 128 * i:128 * (i + 1)], in_=s_ct[:]), dma=True)
        dones.append(prev)
    return dones[-1:]


def phase_moe_h(ctx, pre, xnT_ap, We1_ap, be1_ap, We3_ap, be3_ap,
                ws1_ap, bs1_ap, ws3_ap, bs3_ap, combineT_ap, hmoe_ap, after=()):
    r = ctx.rec
    E, IN = 16, 512
    TC = 512
    s_xn = ctx.sb(pre + "xn", [128, ND * T], BF16)
    xl = []
    for c in range(ND):
        xl.append(r.op("sync", lambda e, c=c: e.dma_start(
            out=s_xn[:, T * c:T * (c + 1)], in_=xnT_ap[128 * c:128 * (c + 1), :]),
            dma=True, waits=tuple(after)))
    s_w1 = [ctx.sb(pre + f"w1{k}", [128, ND * 1024], BF16) for k in range(2)]
    s_w3 = [ctx.sb(pre + f"w3{k}", [128, ND * 1024], BF16) for k in range(2)]
    s_cr = ctx.sb(pre + "cr", [128, T], F32)
    s_crow = ctx.sb(pre + "crow", [1, T], BF16)
    s_h1 = ctx.sb(pre + "h1", [128, TC], F32)
    s_hb = ctx.sb(pre + "hb", [128, TC], BF16)
    p_h = [ctx.ps(pre + f"ph{k}", [128, TC], F32) for k in range(2)]
    p_c = ctx.ps(pre + "pc", [128, TC], F32)

    wl_prev = [None, None]
    ev_prev = [None, None]
    crep_prev = None
    st_prev = None
    hb_prev = None
    dones = []
    for e_i in range(E + 1):
        shared = e_i == E
        wb = e_i % 2
        ninner = 2 * IN // 128 if shared else IN // 128
        w1ap, w3ap, b1ap, b3ap = ((ws1_ap, ws3_ap, bs1_ap, bs3_ap) if shared
                                  else (We1_ap, We3_ap, be1_ap, be3_ap))
        ncols = 1024 if shared else IN
        off = 0 if shared else e_i * D * IN
        wrow = 1024 if shared else IN
        wl1 = r.op("gpsimd", lambda e, wb=wb, w1ap=w1ap, off=off, wrow=wrow, ncols=ncols: e.dma_start(
            out=s_w1[wb][:, 0:ND * ncols].rearrange("p (a c) -> p a c", a=ND),
            in_=dap(w1ap, off, [[wrow, 128], [128 * wrow, ND], [1, ncols]])),
            dma=True, waits=(wl_prev[wb],))
        wl3 = r.op("gpsimd", lambda e, wb=wb, w3ap=w3ap, off=off, wrow=wrow, ncols=ncols: e.dma_start(
            out=s_w3[wb][:, 0:ND * ncols].rearrange("p (a c) -> p a c", a=ND),
            in_=dap(w3ap, off, [[wrow, 128], [128 * wrow, ND], [1, ncols]])),
            dma=True)
        b1v = b1ap if shared else b1ap.flatten()[e_i * IN:(e_i + 1) * IN]
        b3v = b3ap if shared else b3ap.flatten()[e_i * IN:(e_i + 1) * IN]
        nb1 = pre + ("bS1" if shared else f"b1_{e_i % 2}")
        nb3 = pre + ("bS3" if shared else f"b3_{e_i % 2}")
        b1c, b1d = vec2cols(ctx, nb1, b1v, 1024 if shared else IN, (st_prev,), psname=pre + "pv")
        b3c, b3d = vec2cols(ctx, nb3, b3v, 1024 if shared else IN, (st_prev,), psname=pre + "pv")
        if not shared:
            crl = r.op("sync", lambda e, e_i=e_i: e.dma_start(
                out=s_crow[:], in_=combineT_ap[e_i:e_i + 1, :]), dma=True,
                waits=(crep_prev,))
            crd = None
            for tc2 in range(4):
                cmm = r.op("tensor", lambda e, tc2=tc2: e.matmul(
                    p_c[:], ctx.sb("s_onesbf")[0:1, 0:128],
                    s_crow[:, TC * tc2:TC * (tc2 + 1)], start=True, stop=True),
                    mark=True, waits=(crl, crd))
                g = pe_done(ctx, cmm)
                crd = r.op("scalar", lambda e, tc2=tc2: e.copy(
                    s_cr[:, TC * tc2:TC * (tc2 + 1)], p_c[:]), mark=True, waits=(g,))
            crep = crd
        for ic in range(ninner):
            for tc in range(4):
                pb = (ic * 4 + tc) % 2
                mm1 = None
                for c in range(ND):
                    mm1 = r.op("tensor", lambda e, wb=wb, c=c, ic=ic, tc=tc, pb=pb, ncols=ncols: e.matmul(
                        p_h[pb][:], s_w1[wb][:, ncols * c + 128 * ic:ncols * c + 128 * (ic + 1)],
                        s_xn[:, T * c + TC * tc:T * c + TC * (tc + 1)],
                        start=(c == 0), stop=(c == ND - 1)),
                        mark=(c == ND - 1),
                        waits=(wl1, xl[-1], ev_prev[pb]) if c == 0 else ())
                g = pe_done(ctx, mm1)
                ev1 = r.op("scalar", lambda e, pb=pb, ic=ic, b1c=b1c: e.activation(
                    s_h1[:], p_h[pb][:], AF.Silu, bias=b1c[:, ic:ic + 1]),
                    mark=True, waits=(g, b1d, hb_prev))
                mm3 = None
                for c in range(ND):
                    mm3 = r.op("tensor", lambda e, wb=wb, c=c, ic=ic, tc=tc, pb=pb, ncols=ncols: e.matmul(
                        p_h[pb][:], s_w3[wb][:, ncols * c + 128 * ic:ncols * c + 128 * (ic + 1)],
                        s_xn[:, T * c + TC * tc:T * c + TC * (tc + 1)],
                        start=(c == 0), stop=(c == ND - 1)),
                        mark=(c == ND - 1),
                        waits=(wl3, ev1) if c == 0 else ())
                g3 = pe_done(ctx, mm3)
                s1 = r.op("vector", lambda e, pb=pb, ic=ic, b3c=b3c: e.scalar_tensor_tensor(
                    s_h1[:],
                    p_h[pb][:], b3c[:, ic:ic + 1], s_h1[:], ALU.add, ALU.mult),
                    mark=True, waits=(g3, b3d))
                if not shared:
                    s2 = r.op("vector", lambda e, tc=tc: e.tensor_tensor(
                        s_hb[:], s_h1[:], s_cr[:, TC * tc:TC * (tc + 1)], ALU.mult),
                        mark=True, waits=(s1, crep, st_prev))
                else:
                    s2 = r.op("vector", lambda e: e.tensor_copy(s_hb[:], s_h1[:]),
                              mark=True, waits=(s1, st_prev))
                hb_prev = s2
                row0 = (8192 if shared else e_i * IN) + 128 * ic
                st_prev = r.op("sync", lambda e, row0=row0, tc=tc: e.dma_start(
                    out=hmoe_ap[row0:row0 + 128, TC * tc:TC * (tc + 1)],
                    in_=s_hb[:]), dma=True, waits=(s2,))
                ev_prev[pb] = s1
                dones.append(st_prev)
        if not shared:
            crep_prev = s2
        wl_prev[wb] = st_prev
    return dones[-1:]


def phase_moe_out(ctx, pre, hmoe_ap, w2stack_ap, combineT_ap, be2_ap, bs2_ap,
                  x_ap, s0_ap, x1_ap, g0_dbg_ap=None, after=()):
    r = ctx.rec
    KTOT = 9216
    NKG = 18
    s_id = ctx.sb("s_ident")
    s_h = [ctx.sb(pre + f"h{k}", [128, 4 * 512], BF16) for k in range(2)]
    s_w = [ctx.sb(pre + f"w{k}", [128, 4 * 512], BF16) for k in range(2)]
    s_ct = ctx.sb(pre + "ct", [16, T], BF16)
    s_be2 = ctx.sb(pre + "be2", [16, D], BF16)
    s_bs2 = ctx.sb(pre + "bs2", [1, D], BF16)
    s_g = ctx.sb(pre + "g", [128, 512], BF16)
    s_x = ctx.sb(pre + "x", [128, 512], F32)
    s_s0 = ctx.sb(pre + "s0", [128, 512], BF16)
    s_sg = ctx.sb(pre + "sg", [128, 512], F32)
    p_y = [ctx.ps(pre + f"py{k}", [128, 512], F32) for k in range(4)]
    ctl = r.op("sync", lambda e: e.dma_start(out=s_ct[:], in_=combineT_ap), dma=True,
               waits=tuple(after))
    bel = r.op("gpsimd", lambda e: e.dma_start(out=s_be2[:], in_=be2_ap), dma=True,
               waits=tuple(after))
    bsl = r.op("gpsimd", lambda e: e.dma_start(out=s_bs2[:], in_=bs2_ap[None, :]), dma=True,
               waits=tuple(after))
    hl_prev = [None, None]
    ev_prev = [None] * 4
    st_prev = None
    dones = []
    for cpass in range(2):
        for sup in range(4):
            mml = [None] * 4
            for kg in range(NKG):
                b = kg % 2
                hl = r.op("sync", lambda e, kg=kg, sup=sup, b=b: e.dma_start(
                    out=s_h[b][:].rearrange("p (a c) -> p a c", a=4),
                    in_=dap(hmoe_ap, 512 * kg * T + 512 * sup, [[T, 128], [128 * T, 4], [1, 512]])),
                    dma=True, waits=(hl_prev[b],))
                wl = r.op("gpsimd", lambda e, kg=kg, cpass=cpass, b=b: e.dma_start(
                    out=s_w[b][:].rearrange("p (a c) -> p a c", a=4),
                    in_=dap(w2stack_ap, 512 * kg * D + 512 * cpass, [[D, 128], [128 * D, 4], [1, 512]])),
                    dma=True, waits=(hl_prev[b],))
                for kk in range(4):
                    for tt in range(4):
                        mml[tt] = r.op("tensor", lambda e, b=b, kk=kk, tt=tt, kg=kg: e.matmul(
                            p_y[tt][:], s_h[b][:, 512 * kk + 128 * tt:512 * kk + 128 * (tt + 1)],
                            s_w[b][:, 512 * kk:512 * (kk + 1)],
                            start=(kg == 0 and kk == 0), stop=False),
                            mark=(kk == 3),
                            waits=((hl, wl, ev_prev[tt]) if kk == 0 and kg == 0
                                   else (hl, wl) if kk == 0 else ()))
                hl_prev[b] = mml[3]
            for tt in range(4):
                i = sup * 4 + tt
                r.op("tensor", lambda e, i=i, tt=tt, cpass=cpass: e.matmul(
                    p_y[tt][:], s_ct[:, 128 * i:128 * (i + 1)],
                    s_be2[:, 512 * cpass:512 * (cpass + 1)], start=False, stop=False),
                    mark=False, waits=(ctl, bel))
                mmb = r.op("tensor", lambda e, tt=tt, cpass=cpass: e.matmul(
                    p_y[tt][:], ctx.sb("s_onesbf")[0:1, 0:128],
                    s_bs2[:, 512 * cpass:512 * (cpass + 1)], start=False, stop=True),
                    mark=True, waits=(bsl,))
                g = pe_done(ctx, mmb)
                ev = r.op("scalar", lambda e, tt=tt: e.activation(
                    s_g[:], p_y[tt][:], AF.Silu), mark=True, waits=(g, st_prev))
                ev_prev[tt] = ev
                lx = r.op("sync", lambda e, i=i, cpass=cpass: e.dma_start(
                    out=s_x[:], in_=x_ap[128 * i:128 * (i + 1), 512 * cpass:512 * (cpass + 1)]),
                    dma=True, waits=(st_prev,))
                ls = r.op("sync", lambda e, i=i, cpass=cpass: e.dma_start(
                    out=s_s0[:], in_=s0_ap[128 * i:128 * (i + 1), 512 * cpass:512 * (cpass + 1)]),
                    dma=True)
                if g0_dbg_ap is not None:
                    r.op("sync", lambda e, i=i, cpass=cpass: e.dma_start(
                        out=g0_dbg_ap[128 * i:128 * (i + 1), 512 * cpass:512 * (cpass + 1)],
                        in_=s_g[:]), dma=True, waits=(ev,))
                m1 = r.op("vector", lambda e: e.tensor_tensor(
                    s_sg[:], s_s0[:], s_g[:], ALU.mult), mark=True, waits=(ev, ls, st_prev))
                m2 = r.op("vector", lambda e: e.tensor_add(
                    s_sg[:], s_sg[:], s_x[:]), mark=True, waits=(m1, lx))
                st_prev = r.op("sync", lambda e, i=i, cpass=cpass: e.dma_start(
                    out=x1_ap[128 * i:128 * (i + 1), 512 * cpass:512 * (cpass + 1)],
                    in_=s_sg[:]), dma=True, waits=(m2,))
                dones.append(st_prev)
    return dones[-1:]


def stage_w2stack(ctx, pre, We2_ap, ws2_ap, w2stack_ap, after=()):
    r = ctx.rec
    labs = []
    for e_i in range(16):
        labs.append(r.op("gpsimd", lambda e, e_i=e_i: e.dma_start(
            out=dap(w2stack_ap, e_i * 512 * D, [[D, 512], [1, D]]),
            in_=dap(We2_ap, e_i * 512 * D, [[D, 512], [1, D]])),
            dma=True, waits=tuple(after)))
    labs.append(r.op("gpsimd", lambda e: e.dma_start(
        out=dap(w2stack_ap, 8192 * D, [[D, 1024], [1, D]]),
        in_=ws2_ap), dma=True, waits=tuple(after)))
    return labs


def build_l0(ctx, dbg=False):
    nc = ctx.nc
    x = ctx.din("x", [T, D], F32)
    gamma = ctx.din("L0_attn_ln_gamma", [D], F32)
    beta = ctx.din("L0_attn_ln_beta", [D], F32)
    W_in = ctx.din("L0_attn_W_in", [D, 2 * DI], F32)
    b_in = ctx.din("L0_attn_b_in", [2 * DI], F32)
    conv_w = ctx.din("L0_attn_conv_w", [KC, DI], F32)
    conv_b = ctx.din("L0_attn_conv_b", [DI], F32)
    W_x = ctx.din("L0_attn_W_x", [DI, 96], F32)
    W_dt = ctx.din("L0_attn_W_dt", [DTR, DI], F32)
    b_dt = ctx.din("L0_attn_b_dt", [DI], F32)
    A_colmaj = ctx.din("A_colmaj", [128, 256], F32)
    D_ap = ctx.din("L0_attn_D", [DI], F32)
    W_out = ctx.din("L0_attn_W_out", [DI, D], F32)
    b_out = ctx.din("L0_attn_b_out", [D], F32)
    gate_wT = ctx.din("gate_wT", [D, 16], F32)
    gate_b = ctx.din("L0_ffn_gate_b", [16], F32)
    We1 = ctx.din("L0_ffn_We1", [16, D, 512], F32)
    be1 = ctx.din("L0_ffn_be1", [16, 512], F32)
    We2 = ctx.din("L0_ffn_We2", [16, 512, D], F32)
    be2 = ctx.din("L0_ffn_be2", [16, D], F32)
    We3 = ctx.din("L0_ffn_We3", [16, D, 512], F32)
    be3 = ctx.din("L0_ffn_be3", [16, 512], F32)
    ws1 = ctx.din("L0_ffn_ws1", [D, D], F32)
    bs1 = ctx.din("L0_ffn_bs1", [D], F32)
    ws2 = ctx.din("L0_ffn_ws2", [D, D], F32)
    bs2 = ctx.din("L0_ffn_bs2", [D], F32)
    ws3 = ctx.din("L0_ffn_ws3", [D, D], F32)
    bs3 = ctx.din("L0_ffn_bs3", [D], F32)

    dkind = "ExternalOutput" if dbg else "Internal"
    xn_bf = ctx.dram("xn_bf", [T, D], BF16, kind=dkind).ap()
    xnT_bf = ctx.dram("xnT_bf", [D, T], BF16, kind=dkind).ap()
    xn2T_bf = ctx.dram("xn2T_bf", [D, T], BF16, kind=dkind).ap()
    uT = ctx.dram("uT", [DI, T], BF16, kind=dkind).ap()
    resT = ctx.dram("resT", [DI, T], BF16, kind=dkind).ap()
    deltaT = ctx.dram("deltaT", [DI, T], BF16, kind=dkind).ap()
    bc = ctx.dram("bc", [32, T], F32, kind=dkind).ap()
    ybT = ctx.dram("ybT", [DI, T], BF16, kind=dkind).ap()
    s0 = ctx.dram("s0", [T, D], BF16, kind=dkind).ap()
    combine = ctx.dram("combine", [T, 16], F32, kind=dkind).ap()
    combineT = ctx.dram("combineT", [16, T], BF16, kind=dkind).ap()
    hmoe = ctx.dram("hmoe", [9216, T], BF16, kind="Internal").ap()
    w2stack = ctx.dram("w2stack", [9216, D], BF16, kind="Internal").ap()
    g0 = ctx.dram("g0", [T, D], BF16, kind=dkind).ap() if dbg else None
    x1 = ctx.dout("x1", [T, D], F32)

    cs = load_consts(ctx)
    w2l = stage_w2stack(ctx, "w2s_", We2.flatten().rearrange("(a b) -> a b", b=D),
                        ws2, w2stack, after=tuple(cs["ld"]))
    d_ln = phase_ln(ctx, "ln_", x,
                    dict(xn_bf=xn_bf, xnT_bf=xnT_bf, xn2T_bf=xn2T_bf),
                    affine=(gamma, beta), after=tuple(cs["ld"]))
    ctx.free("ln_")
    d_mf = phase_mamba_front(ctx, "mf_", xn2T_bf, W_in, b_in, conv_w, conv_b,
                             W_x, W_dt, b_dt,
                             dict(uT_bf=uT, resT_bf=resT, deltaT_bf=deltaT, bc_f32=bc),
                             after=tuple(d_ln))
    ctx.free("mf_")
    d_sc = phase_scan(ctx, "sc_", deltaT, uT, resT, bc, A_colmaj, D_ap,
                      dict(ybT_bf=ybT), after=tuple(d_mf))
    ctx.free("sc_")
    d_wo = phase_matmul_tok(ctx, "wo_", ybT, DI, W_out, D, s0, BF16,
                            bias_ap=b_out, inject_ap=xn_bf, after=tuple(d_sc))
    ctx.free("wo_")
    d_g = phase_gate(ctx, "gt_", xnT_bf, gate_wT, gate_b, combine, combineT,
                     after=tuple(d_wo))
    ctx.free("gt_")
    d_mh = phase_moe_h(ctx, "mh_", xnT_bf,
                       We1.flatten().rearrange("(a b) -> a b", b=512), be1,
                       We3.flatten().rearrange("(a b) -> a b", b=512), be3,
                       ws1, bs1, ws3, bs3, combineT, hmoe,
                       after=tuple(d_g) + tuple(d_ln))
    ctx.free("mh_")
    d_mo = phase_moe_out(ctx, "mo_", hmoe, w2stack, combineT, be2, bs2,
                         x, s0, x1, g0_dbg_ap=g0,
                         after=tuple(d_mh) + tuple(d_wo) + tuple(w2l))
    ctx.free("mo_")
    return d_mo


def l0_host_inputs(z, core):
    im = {k: np.asarray(z[k]) for k in z if k.startswith("L0_")}
    im["x"] = np.ascontiguousarray(np.asarray(z["L0_attn_x"])[core])
    del im["L0_attn_x"]
    del im["L0_attn_A_log"]
    im["A_colmaj"] = a_colmaj_host(np.asarray(z["L0_attn_A_log"]))
    im["gate_wT"] = np.ascontiguousarray(np.asarray(z["L0_ffn_gate_w"]).T)
    del im["L0_ffn_gate_w"]
    im.update(host_constants())
    return im


def attn_host_consts():
    return {}


def phase_l1_qkv(ctx, pre, xn1T_ap, Wattn_ap, battn_ap, qkT_ap, v_ap, after=()):
    r = ctx.rec
    s_xn = ctx.sb(pre + "xn", [128, ND * T], BF16)
    xl = []
    for c in range(ND):
        xl.append(r.op("sync", lambda e, c=c: e.dma_start(
            out=s_xn[:, T * c:T * (c + 1)], in_=xn1T_ap[128 * c:128 * (c + 1), :]),
            dma=True, waits=tuple(after)))
    bcol, bd = vec2cols(ctx, pre + "bq", battn_ap[0:2048], 2048, tuple(after), psname=pre + "pv")
    bsc = r.op("vector", lambda e: e.tensor_scalar_mul(
        bcol[:, 8:16], bcol[:, 8:16], 0.125), mark=True, waits=(bd,))
    s_w = [ctx.sb(pre + f"w{k}", [128, ND * 128], BF16) for k in range(2)]
    s_o = ctx.sb(pre + "o", [128, 2048], BF16)
    p_m = [ctx.ps(pre + f"p{k}", [128, 512], F32) for k in range(2)]
    TC = 512
    wl_prev = [None, None]
    ev_prev = [None, None]
    st_prev = None
    for co in range(16):
        b = co % 2
        wl = r.op("gpsimd", lambda e, co=co, b=b: e.dma_start(
            out=s_w[b][:].rearrange("p (a c) -> p a c", a=ND),
            in_=dap(Wattn_ap, 128 * co, [[3072, 128], [128 * 3072, ND], [1, 128]])),
            dma=True, waits=(wl_prev[b],))
        evs = []
        for tc in range(4):
            pb = tc % 2
            mm = None
            for ci in range(ND):
                mm = r.op("tensor", lambda e, b=b, pb=pb, ci=ci, tc=tc: e.matmul(
                    p_m[pb][:], s_w[b][:, 128 * ci:128 * (ci + 1)],
                    s_xn[:, T * ci + TC * tc:T * ci + TC * (tc + 1)],
                    start=(ci == 0), stop=(ci == ND - 1)),
                    mark=(ci == ND - 1),
                    waits=(wl, xl[-1], ev_prev[pb]) if ci == 0 else ())
            g = pe_done(ctx, mm)
            ev = r.op("scalar", lambda e, pb=pb, co=co, tc=tc: e.activation(
                s_o[:, TC * tc:TC * (tc + 1)], p_m[pb][:], AF.Identity,
                bias=bcol[:, co:co + 1], scale=(0.125 if co >= 8 else 1.0)),
                mark=True, waits=(g, bsc, st_prev))
            ev_prev[pb] = ev
            evs.append(ev)
        st_prev = r.op("sync", lambda e, co=co: e.dma_start(
            out=qkT_ap[128 * co:128 * (co + 1), :], in_=s_o[:]),
            dma=True, waits=tuple(evs))
        wl_prev[b] = st_prev
    d_qk = st_prev
    s_wv = ctx.sb(pre + "wv", [128, ND * 512], BF16)
    wvl0 = r.op("gpsimd", lambda e: e.dma_start(
        out=s_wv[:].rearrange("p (a c) -> p a c", a=ND),
        in_=dap(Wattn_ap, 2048, [[3072, 128], [128 * 3072, ND], [1, 512]])),
        dma=True)
    s_bv = ctx.sb(pre + "bv", [1, 1024], BF16)
    bvl = r.op("gpsimd", lambda e: e.dma_start(
        out=s_bv[:], in_=battn_ap[None, 2048:3072]), dma=True)
    s_vo = ctx.sb(pre + "vo", [128, 512], BF16)
    p_v = [ctx.ps(pre + f"pvm{k}", [128, 512], F32) for k in range(2)]
    ev_prev = [None, None]
    st_prev = None
    wv_cur = wvl0
    for half in range(2):
        if half == 1:
            wv_cur = r.op("gpsimd", lambda e: e.dma_start(
                out=s_wv[:].rearrange("p (a c) -> p a c", a=ND),
                in_=dap(Wattn_ap, 2048 + 512, [[3072, 128], [128 * 3072, ND], [1, 512]])),
                dma=True, waits=(st_prev,))
        for i in range(NT):
            pb = i % 2
            mm = None
            for ci in range(ND):
                mm = r.op("tensor", lambda e, pb=pb, ci=ci, i=i: e.matmul(
                    p_v[pb][:], s_xn[:, T * ci + 128 * i:T * ci + 128 * (i + 1)],
                    s_wv[:, 512 * ci:512 * (ci + 1)],
                    start=(ci == 0), stop=False),
                    mark=False,
                    waits=(wv_cur, xl[-1], ev_prev[pb]) if ci == 0 else ())
            mmb = r.op("tensor", lambda e, pb=pb, half=half: e.matmul(
                p_v[pb][:], ctx.sb("s_onesbf")[0:1, 0:128], s_bv[:, 512 * half:512 * (half + 1)],
                start=False, stop=True), mark=True, waits=(bvl,))
            g = pe_done(ctx, mmb)
            ev = r.op("scalar", lambda e, pb=pb: e.copy(s_vo[:], p_v[pb][:]),
                      mark=True, waits=(g, st_prev))
            ev_prev[pb] = ev
            st_prev = r.op("sync", lambda e, i=i, half=half: e.dma_start(
                out=v_ap[128 * i:128 * (i + 1), 512 * half:512 * (half + 1)],
                in_=s_vo[:]), dma=True, waits=(ev,))
    return [d_qk, st_prev]


def phase_attn(ctx, pre, qkT_ap, v_ap, embkR_ap, embvRE_ap, Wproj_ap, bproj_ap,
               s1_ap, after=(), nheads=NH, dbg=None):
    r = ctx.rec
    nc = ctx.nc
    EB = 1664
    ER = 3072
    qkP = [ctx.dram(pre + f"qkP{k}", [2049 * EB], BF16).ap() for k in range(2)]
    edr = [ctx.dram(pre + f"edr{k}", [2050 * ER], BF16).ap() for k in range(2)]
    maskP_ap = ctx.din("c_maskP", [128, 1024], BF16)
    s_mask = ctx.sb(pre + "mask", [128, 1024], BF16)
    lm = r.op("sync", lambda e: e.dma_start(out=s_mask[:], in_=maskP_ap), dma=True,
              waits=tuple(after))
    s_z = ctx.sb(pre + "z", [128, ER], BF16)
    zd = r.op("vector", lambda e: e.memset(s_z[:], 0.0), mark=True, waits=tuple(after))
    zl = []
    for k in range(2):
        for blk in range(17):
            rows = 128 if 128 * (blk + 1) <= 2049 else 2049 - 128 * blk
            if rows <= 0:
                break
            zl.append(r.op("sync", lambda e, k=k, blk=blk, rows=rows: e.dma_start(
                out=dap(qkP[k], 128 * blk * EB, [[EB, rows], [1, EB]]),
                in_=s_z[0:rows, 0:EB]), dma=True, waits=(zd,)))
        for blk in range(17):
            rows = 128 if 128 * (blk + 1) <= 2050 else 2050 - 128 * blk
            if rows <= 0:
                break
            zl.append(r.op("sync", lambda e, k=k, blk=blk, rows=rows: e.dma_start(
                out=dap(edr[k], 128 * blk * ER, [[ER, rows], [1, ER]]),
                in_=s_z[0:rows, 0:ER]), dma=True, waits=(zd,)))
    s_ek = ctx.sb(pre + "ek", [128, 8 * 64], BF16)
    ekl = r.op("gpsimd", lambda e: e.dma_start(
        out=s_ek[:].rearrange("p (a c) -> p a c", a=8),
        in_=dap(embkR_ap, 0, [[64, 128], [128 * 64, 8], [1, 64]])), dma=True,
        waits=tuple(after))
    s_ekT = ctx.sb(pre + "ekT", [64, 1024], BF16)
    p_tr = ctx.ps(pre + "psA", [128, 512], F32)
    p_trb = ctx.ps(pre + "psB", [128, 512], F32)
    p_trc = ctx.ps(pre + "psC", [128, 512], F32)
    s_id = ctx.sb("s_ident")
    ekd = None
    for cchunk in range(8):
        mm = r.op("tensor", lambda e, c=cchunk: e.transpose(
            p_tr[:, 0:128].bitcast(BF16)[0:64, 0:128] if False else p_tr.bitcast(BF16)[0:64, 0:128],
            s_ek[:, 64 * c:64 * (c + 1)], s_id[:]), mark=True, waits=(ekl, ekd))
        g = pe_done(ctx, mm)
        ekd = r.op("scalar", lambda e, c=cchunk: e.copy(
            s_ekT[:, 128 * c:128 * (c + 1)], p_tr.bitcast(BF16)[0:64, 0:128]),
            mark=True, waits=(g,))
    s_ev = ctx.sb(pre + "ev", [128, 16 * 64], BF16)
    evl = r.op("gpsimd", lambda e: e.dma_start(
        out=s_ev[:].rearrange("p (a c) -> p a c", a=16),
        in_=dap(embvRE_ap, 0, [[64, 128], [128 * 64, 16], [1, 64]])), dma=True,
        waits=tuple(after))
    s_v = ctx.sb(pre + "v", [128, 16 * 1024], BF16)
    vl = r.op("sync", lambda e: e.dma_start(
        out=s_v[:].rearrange("p (a c) -> p a c", a=16),
        in_=dap(v_ap, 0, [[1024, 128], [128 * 1024, 16], [1, 1024]])), dma=True,
        waits=tuple(after))
    s_qh = [ctx.sb(pre + f"qh{k}", [64, T], BF16) for k in range(2)]
    s_kh = [ctx.sb(pre + f"kh{k}", [64, T], BF16) for k in range(2)]
    s_dq = [ctx.sb(pre + f"dq{k}", [128, NT], F32) for k in range(2)]
    s_dqn = ctx.sb(pre + "dqn", [128, NT], F32)
    s_dp = [ctx.sb(pre + f"dp{k}", [128, 8], F32) for k in range(4)]
    s_qp = [ctx.sb(pre + "qp", [128, NT * 1024], BF16)] * 2
    s_at2 = [ctx.sb(pre + f"at2{k}", [128, 4 * 1152], BF16) for k in range(2)]
    s_et = [ctx.sb(pre + f"et{k}", [128, 2048], BF16) for k in range(4)]
    s_eT = [ctx.sb(pre + f"eT{k}", [128, 2048], BF16) for k in range(2)]
    s_rT = [ctx.sb(pre + f"rT{k}", [128, 2048], BF16) for k in range(2)]
    s_yT = ctx.sb(pre + "yT", [128, 8 * 2048], BF16)
    p_y = ctx.ps(pre + "py", [64, 2048], F32)
    ydone_hist = {}
    qkP_hist = {}
    est_hist_map = {}
    est_tile_hist = {}
    ee_last_map = {}
    d_all = []
    pend_yev = None

    def emit_yev(h_, g_):
        yev = r.op("scalar", lambda e, h_=h_: e.activation(
            s_yT[64 * (h_ % 2):64 * (h_ % 2) + 64, 2048 * (h_ // 2):2048 * (h_ // 2 + 1)],
            p_y[:], AF.Identity), mark=True, waits=(g_, ydone_hist.get(h_ - 2)))
        ydone_hist[h_] = yev
        d_all.append(yev)

    for h in range(nheads):
        par = h % 2
        lq = r.op("sync", lambda e, h=h, par=par: e.dma_start(
            out=s_qh[par][:], in_=qkT_ap[64 * h:64 * (h + 1), :]), dma=True,
            waits=tuple(after) + (ee_last_map.get(h - 2),))
        lk = r.op("sync", lambda e, h=h, par=par: e.dma_start(
            out=s_kh[par][:], in_=qkT_ap[1024 + 64 * h:1024 + 64 * (h + 1), :]), dma=True)
        banksA = [p_tr, p_trb, p_trc]
        bank_free = {0: ee_last_map.get(h - 1), 1: ee_last_map.get(h - 1),
                     2: ee_last_map.get(h - 1)}
        qevs = []
        mmA_prev = None
        pend_e2 = None
        for i in range(NT):
            ba, bb = (2 * i) % 3, (2 * i + 1) % 3
            mmA = r.op("tensor", lambda e, i=i, par=par, ba=ba: e.matmul(
                banksA[ba][:], s_qh[par][:, 128 * i:128 * (i + 1)], s_ekT[:, 0:512],
                start=True, stop=True), mark=True,
                waits=(lq, ekd, bank_free[ba]))
            if pend_e2 is not None:
                pend_e2(mmA)
                pend_e2 = None
            mmB = r.op("tensor", lambda e, i=i, par=par, bb=bb: e.matmul(
                banksA[bb][:], s_qh[par][:, 128 * i:128 * (i + 1)], s_ekT[:, 512:1024],
                start=True, stop=True), mark=True, waits=(bank_free[bb],))
            dq = r.op("vector", lambda e, i=i, par=par, ba=ba: e.tensor_copy(
                s_dq[par][:, i:i + 1], banksA[ba][:, 0:1]), mark=True,
                waits=(mmB,) + ((ee_last_map.get(h - 2),) if i == 0 else ()))
            dqn = r.op("vector", lambda e, i=i, par=par: e.tensor_scalar_mul(
                s_dqn[:, i:i + 1], s_dq[par][:, i:i + 1], -1.0), mark=True, waits=(dq,))
            e1 = r.op("scalar", lambda e, i=i, par=par, ba=ba: e.activation(
                s_qp[par][:, 1024 * i:1024 * i + 512], banksA[ba][:], AF.Identity,
                bias=s_dqn[:, i:i + 1]), mark=True,
                waits=(dqn, mmB) + ((qkP_hist.get(h - 1),) if i == 0 else ()))
            bank_free[ba] = e1

            def mk_e2(i=i, par=par, bb=bb, dqn=dqn):
                def emit(guard):
                    e2 = r.op("scalar", lambda e: e.activation(
                        s_qp[par][:, 1024 * i + 512:1024 * (i + 1)], banksA[bb][:],
                        AF.Identity, bias=s_dqn[:, i:i + 1]), mark=True,
                        waits=(guard, dqn))
                    bank_free[bb] = e2
                    qevs.append(e2)
                return emit
            pend_e2 = mk_e2()
        pend_e2(pe_done(ctx, mmB))
        pend_e2 = None
        stq = r.op("sync", lambda e, par=par: e.dma_start(
            out=dap(qkP[par], 640, [[EB, 128], [128 * EB, NT], [1, 1024]]),
            in_=s_qp[par][:].rearrange("p (a c) -> p a c", a=NT)), dma=True,
            waits=tuple(qevs) + (qkP_hist.get(h - 2), zl[-1]))
        qkP_hist[h] = stq
        if pend_yev is not None:
            emit_yev(*pend_yev)
            pend_yev = None
        banks3 = [p_tr, p_trb, p_trc]
        last_ee = None
        est_hist = est_hist_map.get(h - 2)
        ee_by_cyc = {}
        cyc = 0
        pend_ev = None
        for grp in range(4):
            at2l = r.op("sync", lambda e, grp=grp, par=par: e.dma_start(
                out=s_at2[par][:].rearrange("p (a c) -> p a c", a=4),
                in_=dap(qkP[par], (4 * grp) * 128 * EB + 639,
                        [[EB - 1, 128], [128 * EB, 4], [1, 1152]])),
                dma=True, waits=(stq, est_hist), label=None)
            if dbg is not None and "at2" in dbg and h == 0 and grp == 1:
                r.op("sync", lambda e: e.dma_start(out=dbg["at2"], in_=s_at2[0][:]),
                     dma=True, waits=(at2l,))
            for sub in range(4):
                i = 4 * grp + sub
                idx = 2 * par + (i % 2)
                t0 = 128 * i
                nsc = (t0 + 128 + 511) // 512
                eevs = []
                for sci in range(nsc):
                    w = min(512, t0 + 128 - 512 * sci)
                    ps = banks3[cyc % 3]
                    mm = r.op("tensor", lambda e, i=i, sci=sci, w=w, ps=ps, par=par: e.matmul(
                        ps[:, 0:w], s_qh[par][:, 128 * i:128 * (i + 1)],
                        s_kh[par][:, 512 * sci:512 * sci + w], start=True, stop=False),
                        mark=True, waits=(lk, at2l, ee_by_cyc.get(cyc - 3)))
                    if pend_ev is not None:
                        pend_ev(mm)
                        pend_ev = None
                    s_lo = max(512 * sci, t0 - 1024)
                    if s_lo < 512 * sci + w:
                        co = s_lo - 512 * sci
                        jlo = s_lo - (t0 - 1024)
                        r.op("tensor", lambda e, i=i, sci=sci, w=w, co=co, jlo=jlo,
                             sub=sub, ps=ps, par=par: e.matmul(
                            ps[:, co:w], s_id[:],
                            s_at2[par][:, 1152 * sub + jlo:1152 * sub + jlo + (w - co)],
                            start=False, stop=False), mark=False)
                    last_is_mask = (512 * sci + w == t0 + 128)
                    if last_is_mask:
                        off = 512 - (t0 - 512 * sci)
                        mm2 = r.op("tensor", lambda e, off=off, w=w, ps=ps: e.matmul(
                            ps[:, 0:w], s_id[:], s_mask[:, off:off + w],
                            start=False, stop=True), mark=True, waits=(lm,))
                    else:
                        mm2 = r.op("tensor", lambda e, w=w, ps=ps: e.matmul(
                            ps[:, 0:w], s_id[0:1, 0:128], s_z[0:1, 0:w],
                            start=False, stop=True), mark=True, waits=(zd,))

                    def mk_pend(i=i, sci=sci, w=w, ps=ps, par=par, idx=idx,
                                cyc=cyc, sub_last=(sci == nsc - 1), eevs=eevs,
                                nsc=nsc, t0=t0, grp=grp):
                        def emit(guard):
                            nonlocal last_ee, est_hist
                            ee = r.op("scalar", lambda e: e.activation(
                                s_et[idx][:, 512 * sci:512 * sci + w], ps[:, 0:w],
                                AF.Exp, bias=s_dq[par][:, i:i + 1],
                                accum_out=s_dp[idx][:, sci:sci + 1]),
                                mark=True, waits=(guard, est_tile_hist.get(idx)))
                            ee_by_cyc[cyc] = ee
                            last_ee = ee
                            eevs.append(ee)
                            if sub_last:
                                sm = r.op("vector", lambda e: e.reduce_sum(
                                    s_dp[idx][:, 6:7], s_dp[idx][:, 0:nsc], AX.X),
                                    mark=True, waits=(eevs[-1],))
                                rc = r.op("vector", lambda e: e.reciprocal(
                                    s_dp[idx][:, 7:8], s_dp[idx][:, 6:7]),
                                    mark=True, waits=(sm,))
                                sc2 = r.op("vector", lambda e: e.tensor_scalar(
                                    s_et[idx][:, 0:t0 + 128], s_et[idx][:, 0:t0 + 128],
                                    s_dp[idx][:, 7:8], None, ALU.mult),
                                    mark=True, waits=(rc,))
                                est = r.op("sync", lambda e: e.dma_start(
                                    out=dap(edr[par], (1 + t0) * ER + 1024,
                                            [[ER, 128], [1, t0 + 128]]),
                                    in_=s_et[idx][:, 0:t0 + 128]), dma="e",
                                    waits=(sc2, ydone_hist.get(h - 2), zl[-1]))
                                est_tile_hist[idx] = est
                                est_hist = est
                        return emit
                    pend_ev = mk_pend()
                    cyc += 1
                if sub == 3 and pend_ev is not None:
                    pend_ev(pe_done(ctx, mm2))
                    pend_ev = None
        est_hist_map[h] = est_hist
        ee_last_map[h] = last_ee
        ymm_last = None
        ymm_hist = {}
        for sc in range(NT):
            wwin = 2048 - 128 * sc
            eTl = r.op("sync", lambda e, sc=sc, par=par, wwin=wwin: e.dma_start_transpose(
                s_eT[sc % 2][:, 0:wwin],
                dap(edr[par], (1 + 128 * sc) * ER + 1024 + 128 * sc, [[ER, wwin], [1, 128]])),
                dma="t", waits=(est_hist, ymm_hist.get(sc - 2),
                                ydone_hist.get(h - 1) if sc < 2 else None))
            for b in range(4):
                lo = max(128 * sc, 512 * b)
                hi = 512 * (b + 1)
                if lo >= hi:
                    continue
                ymm_last = r.op("tensor", lambda e, sc=sc, b=b, lo=lo, hi=hi, h=h: e.matmul(
                    p_y[:, lo:hi], s_v[:, 1024 * sc + 64 * h:1024 * sc + 64 * (h + 1)],
                    s_eT[sc % 2][:, lo - 128 * sc:hi - 128 * sc],
                    start=(sc == 0), stop=False), mark=True,
                    waits=(eTl, vl) + ((ydone_hist.get(h - 1),) if sc == 0 and b == 0 else ()))
            ymm_hist[sc] = ymm_last
        rT_hist = {}
        for J in range(16):
            rTl = r.op("sync", lambda e, J=J, par=par: e.dma_start_transpose(
                s_rT[J % 2][:], dap(edr[par], 128 * J + 2049, [[ER + 1, 2048], [1, 128]])),
                dma="t", waits=(est_hist, rT_hist.get(J - 2),
                                ydone_hist.get(h - 1) if J < 2 else None))
            for b in range(4):
                ymm_last = r.op("tensor", lambda e, J=J, b=b: e.matmul(
                    p_y[:, 512 * b:512 * (b + 1)], s_ev[:, 64 * J:64 * (J + 1)],
                    s_rT[J % 2][:, 512 * b:512 * (b + 1)],
                    start=False, stop=(J == 15 and b == 3)), mark=True,
                    waits=(rTl, evl))
            rT_hist[J] = ymm_last
        g = pe_done(ctx, ymm_last)
        pend_yev = (h, g)
    if pend_yev is not None:
        emit_yev(*pend_yev)
        pend_yev = None
    if dbg is not None:
        for nm, src_ap, rows, pitch in (("qkP0", qkP[0], 2049, EB), ("edr0", edr[0], 2050, ER)):
            if nm in dbg:
                r.op("sync", lambda e, nm=nm, src_ap=src_ap, rows=rows, pitch=pitch: e.dma_start(
                    out=dbg[nm], in_=dap(src_ap, 0, [[pitch, rows], [1, pitch]])),
                    dma=True, waits=(d_all[-1],))
        if "yT" in dbg:
            for c in range(ND):
                r.op("sync", lambda e, c=c: e.dma_start(
                    out=dbg["yT"][128 * c:128 * (c + 1), :],
                    in_=s_yT[:, 2048 * c:2048 * (c + 1)]), dma=True,
                    waits=(d_all[-1],))
    s_wp = ctx.sb(pre + "wp", [128, ND * 1024], BF16)
    wpl = r.op("gpsimd", lambda e: e.dma_start(
        out=s_wp[:].rearrange("p (a c) -> p a c", a=ND),
        in_=dap(Wproj_ap, 0, [[1024, 128], [128 * 1024, ND], [1, 1024]])),
        dma=True, waits=tuple(after))
    s_bp = ctx.sb(pre + "bp", [1, 1024], BF16)
    bpl = r.op("gpsimd", lambda e: e.dma_start(
        out=s_bp[:], in_=bproj_ap[None, :]), dma=True, waits=tuple(after))
    s_so = ctx.sb(pre + "so", [128, 1024], BF16)
    p_p = [p_tr, p_trb]
    ev_prev = [d_all[-1], d_all[-1]]
    dones = []
    for i in range(NT):
        for half in range(2):
            pb = half
            mm = None
            for c in range(ND):
                mm = r.op("tensor", lambda e, i=i, half=half, c=c, pb=pb: e.matmul(
                    p_p[pb][:], s_yT[:, 2048 * (c) + 128 * i:2048 * c + 128 * (i + 1)],
                    s_wp[:, 1024 * c + 512 * half:1024 * c + 512 * (half + 1)],
                    start=(c == 0), stop=False), mark=False,
                    waits=(wpl, d_all[-1], ev_prev[pb]) if c == 0 else ())
            mmb = r.op("tensor", lambda e, half=half, pb=pb: e.matmul(
                p_p[pb][:], ctx.sb("s_onesbf")[0:1, 0:128], s_bp[:, 512 * half:512 * (half + 1)],
                start=False, stop=True), mark=True, waits=(bpl,))
            g = pe_done(ctx, mmb)
            ev = r.op("scalar", lambda e, half=half, pb=pb: e.activation(
                s_so[:, 512 * half:512 * (half + 1)], p_p[pb][:], AF.Identity),
                mark=True, waits=(g,))
            ev_prev[pb] = ev
        dones.append(r.op("sync", lambda e, i=i: e.dma_start(
            out=s1_ap[128 * i:128 * (i + 1), :], in_=s_so[:]), dma=True,
            waits=(ev_prev[0], ev_prev[1])))
    return dones[-1:]


def phase_wq_multi(ctx, pre, jobs, after=()):
    r = ctx.rec
    st = []
    for k, job in enumerate(jobs):
        NI = job["nrow"] // 128
        NC = job["ncol"] // 128
        pj = pre + f"j{k}_"
        st.append(dict(job, NI=NI, NC=NC, prev=None, nrow=job["nrow"],
                       s_w=ctx.sb(pj + "w", [128, job["ncol"]], F32),
                       s_i=ctx.sb(pj + "i", [128, job["ncol"]], I32),
                       s_b=ctx.sb(pj + "b", [128, job["ncol"]], BF16),
                       s_st=ctx.sb(pj + "st", [128, 4], F32),
                       s_sc=ctx.sb(pj + "sc", [128, NI], F32),
                       s_tr=ctx.sb(pj + "tr", [128, job["ncol"]], BF16),
                       p_t=[ctx.ps(pj + f"pt{q}", [128, 128], BF16) for q in range(2)]))
    s_id = ctx.sb("s_ident")
    dones = []
    for ic in range(max(j["NI"] for j in st)):
        for j in st:
            if ic >= j["NI"]:
                continue
            s_w, s_i, s_b, s_stt = j["s_w"], j["s_i"], j["s_b"], j["s_st"]
            s_sc, s_tr, p_t = j["s_sc"], j["s_tr"], j["p_t"]
            em = chain_maker(r, j["prev"], *(after if ic == 0 else ()))
            em("sync", lambda e, j=j, ic=ic, s_w=s_w: e.dma_start(
                out=s_w[:], in_=j["w_ap"][128 * ic:128 * (ic + 1), :]),
                dma=j["stream"])
            em("vector", lambda e, s_w=s_w, s_stt=s_stt: e.reduce_max(
                s_stt[:, 0:1], s_w[:], AX.X, apply_absolute_value=True))
            em("vector", lambda e, s_stt=s_stt: e.reciprocal(s_stt[:, 1:2], s_stt[:, 0:1]))
            em("vector", lambda e, s_stt=s_stt: e.tensor_scalar_mul(
                s_stt[:, 2:3], s_stt[:, 1:2], 127.0))
            em("vector", lambda e, ic=ic, s_sc=s_sc, s_stt=s_stt: e.tensor_scalar_mul(
                s_sc[:, ic:ic + 1], s_stt[:, 0:1], 1.0 / 127.0))
            em("vector", lambda e, s_w=s_w, s_stt=s_stt: e.tensor_scalar(
                s_w[:], s_w[:], s_stt[:, 2:3], None, ALU.mult))
            em("vector", lambda e, s_w=s_w, s_i=s_i: e.tensor_copy(s_i[:], s_w[:]))
            if j["dequant"]:
                em("vector", lambda e, s_w=s_w, s_i=s_i: e.tensor_copy(s_w[:], s_i[:]))
                em("vector", lambda e, ic=ic, s_w=s_w, s_sc=s_sc: e.tensor_scalar(
                    s_w[:], s_w[:], s_sc[:, ic:ic + 1], None, ALU.mult))
                em("vector", lambda e, s_w=s_w, s_b=s_b: e.tensor_copy(s_b[:], s_w[:]))
            else:
                em("vector", lambda e, s_i=s_i, s_b=s_b: e.tensor_copy(s_b[:], s_i[:]))
            trs = []
            evs = []
            for c in range(j["NC"]):
                trs.append(em("tensor", lambda e, c=c, s_b=s_b, p_t=p_t: e.transpose(
                    p_t[c % 2][:], s_b[:, 128 * c:128 * (c + 1)], s_id[:]),
                    waits=(evs[c - 2],) if c >= 2 else ()))
                if c >= 1:
                    evs.append(r.op("scalar", lambda e, c=c, s_tr=s_tr, p_t=p_t: e.copy(
                        s_tr[:, 128 * (c - 1):128 * c], p_t[(c - 1) % 2][:]),
                        mark=True, waits=(trs[c],)))
            evs.append(r.op("scalar", lambda e, j=j, s_tr=s_tr, p_t=p_t: e.copy(
                s_tr[:, 128 * (j["NC"] - 1):128 * j["NC"]], p_t[(j["NC"] - 1) % 2][:]),
                mark=True, waits=(pe_done(ctx, trs[-1]),)))
            j["prev"] = r.op("sync", lambda e, j=j, ic=ic, s_tr=s_tr: e.dma_start(
                out=dap(j["wqT_ap"], 128 * ic,
                        [[j["nrow"], 128], [128 * j["nrow"], j["NC"]], [1, 128]]),
                in_=s_tr[:].rearrange("p (a c) -> p a c", a=j["NC"])),
                dma=j["stream"], waits=tuple(evs))
    for j in st:
        d = j["prev"]
        if j["scales_ap"] is not None:
            d = r.op("sync", lambda e, j=j: e.dma_start(
                out=j["scales_ap"], in_=j["s_sc"][:]), dma=j["stream"],
                waits=(j["prev"],))
        dones.append(d)
    return dones


def phase_wq_prep(ctx, pre, w_ap, nrow, ncol, wqT_ap, scales_ap=None,
                  dequant=False, after=(), stream=True):
    r = ctx.rec
    NI = nrow // 128
    NC = ncol // 128
    s_w = ctx.sb(pre + "w", [128, ncol], F32)
    s_i = ctx.sb(pre + "i", [128, ncol], I32)
    s_b = ctx.sb(pre + "b", [128, ncol], BF16)
    s_st = ctx.sb(pre + "st", [128, 4], F32)
    s_sc = ctx.sb(pre + "sc", [128, NI], F32)
    s_tr = ctx.sb(pre + "tr", [128, 128], BF16)
    p_t = ctx.ps(pre + "pt", [128, 128], BF16)
    s_id = ctx.sb("s_ident")
    prev = None
    dones = []
    for ic in range(NI):
        em = chain_maker(r, prev, *(after if ic == 0 else ()))
        em("sync", lambda e, ic=ic: e.dma_start(
            out=s_w[:], in_=w_ap[128 * ic:128 * (ic + 1), :]), dma=stream)
        em("vector", lambda e: e.reduce_max(s_st[:, 0:1], s_w[:], AX.X,
                                            apply_absolute_value=True))
        em("vector", lambda e: e.reciprocal(s_st[:, 1:2], s_st[:, 0:1]))
        em("vector", lambda e: e.tensor_scalar_mul(s_st[:, 2:3], s_st[:, 1:2], 127.0))
        em("vector", lambda e, ic=ic: e.tensor_scalar_mul(
            s_sc[:, ic:ic + 1], s_st[:, 0:1], 1.0 / 127.0))
        em("vector", lambda e: e.tensor_scalar(s_w[:], s_w[:], s_st[:, 2:3], None, ALU.mult))
        em("vector", lambda e: e.tensor_copy(s_i[:], s_w[:]))
        em("vector", lambda e: e.tensor_copy(s_w[:], s_i[:]))
        em("vector", lambda e: e.tensor_scalar(
            s_w[:], s_w[:], -128.0, 127.0, ALU.max, ALU.min))
        if dequant:
            em("vector", lambda e, ic=ic: e.tensor_scalar(
                s_w[:], s_w[:], s_sc[:, ic:ic + 1], None, ALU.mult))
        em("vector", lambda e: e.tensor_copy(s_b[:], s_w[:]))
        for c in range(NC):
            em("tensor", lambda e, c=c: e.transpose(
                p_t[:], s_b[:, 128 * c:128 * (c + 1)], s_id[:]))
            em("tensor", lambda e: e.matmul(
                ctx.ps("p_guard")[:], ctx.sb("s_onesbf")[0:1, 0:1],
                ctx.sb("s_onesbf")[0:1, 0:16], start=True, stop=True))
            em("scalar", lambda e: e.copy(s_tr[:], p_t[:]))
            prev = em("sync", lambda e, ic=ic, c=c: e.dma_start(
                out=wqT_ap[128 * c:128 * (c + 1), 128 * ic:128 * (ic + 1)],
                in_=s_tr[:]), dma=stream)
        dones.append(prev)
    if scales_ap is not None:
        dones.append(r.op("sync", lambda e: e.dma_start(
            out=scales_ap, in_=s_sc[:]), dma=stream, waits=(prev,)))
    return dones[-1:]


def phase_absmax(ctx, pre, x_ap, rows, cols, out_ap, after=()):
    r = ctx.rec
    s_x = ctx.sb(pre + "x", [128, cols], F32)
    s_m = ctx.sb(pre + "m", [128, 2], F32)
    s_r = ctx.sb(pre + "r", [1, 128], F32)
    prev = None
    init = None
    for i in range(rows // 128):
        em = chain_maker(r, prev, *(after if i == 0 else ()))
        em("sync", lambda e, i=i: e.dma_start(
            out=s_x[:], in_=x_ap[128 * i:128 * (i + 1), :]), dma=True)
        em("vector", lambda e: e.reduce_max(s_m[:, 1:2], s_x[:], AX.X,
                                            apply_absolute_value=True))
        if i == 0:
            prev = em("vector", lambda e: e.tensor_copy(s_m[:, 0:1], s_m[:, 1:2]))
        else:
            prev = em("vector", lambda e: e.tensor_max(s_m[:, 0:1], s_m[:, 0:1], s_m[:, 1:2]))
    em = chain_maker(r, prev)
    em("sync", lambda e: e.dma_start(out=s_r[:], in_=AP(s_m, 0, [[2, 128], [1, 1]]),
                                     allow_slow_non_contiguous=True), dma=True)
    em("vector", lambda e: e.reduce_max(s_m[0:1, 1:2], s_r[:], AX.X))
    return [em("sync", lambda e: e.dma_start(out=out_ap[None, :], in_=s_m[0:1, 1:2]), dma=True)]


def scalar_prep(ctx, pre, clip_ap, lmax_ap, nmax, after=()):
    r = ctx.rec
    s_l = ctx.sb(pre + "l", [1, 16], F32)
    s_c = ctx.sb(pre + "c", [1, 4], F32)
    s_rep = ctx.sb(pre + "rep", [128, 2], F32)
    p_b = ctx.ps(pre + "pb", [128, 16], F32)
    l1 = r.op("sync", lambda e: e.dma_start(out=s_l[:, 0:nmax], in_=lmax_ap[None, :]),
              dma=True, waits=tuple(after))
    l2 = r.op("sync", lambda e: e.dma_start(out=s_c[:, 0:1], in_=clip_ap[None, :]), dma=True)
    em = chain_maker(r, l1, l2)
    em("vector", lambda e: e.reduce_max(s_c[:, 1:2], s_l[:, 0:nmax], AX.X))
    em("scalar", lambda e: e.activation(s_c[:, 2:3], s_c[:, 0:1], AF.Sigmoid))
    em("vector", lambda e: e.tensor_tensor(s_c[:, 1:2], s_c[:, 1:2], s_c[:, 2:3], ALU.mult))
    em("vector", lambda e: e.tensor_scalar_mul(s_c[:, 2:3], s_c[:, 1:2], 1.0 / 127.0))
    em("vector", lambda e: e.reciprocal(s_c[:, 3:4], s_c[:, 2:3]))
    mm1 = em("tensor", lambda e: e.matmul(
        p_b[:, 0:2], ctx.sb("s_onesf")[:, 0:128], s_c[:, 2:4], start=True, stop=True))
    em("tensor", lambda e: e.matmul(
        ctx.ps("p_guard")[:], ctx.sb("s_onesbf")[0:1, 0:1],
        ctx.sb("s_onesbf")[0:1, 0:16], start=True, stop=True))
    d = em("scalar", lambda e: e.copy(s_rep[:], p_b[:, 0:2]))
    return s_rep, [d]


def phase_ffn_a(ctx, pre, xn1_ap, clip_x_ap, lmax_ap, nmax, wqgT_ap, wquT_ap,
                swg_ap, swu_ap, hF_ap, lmaxh_ap, after=()):
    r = ctx.rec
    NI = IFF // 128
    s_rep, d_s = scalar_prep(ctx, pre + "s", clip_x_ap, lmax_ap, nmax, after)
    s_x = ctx.sb(pre + "x", [128, D], F32)
    s_i = ctx.sb(pre + "i", [128, D], I32)
    s_b = ctx.sb(pre + "xb", [128, D], BF16)
    s_xqT = ctx.sb(pre + "xqT", [128, ND * T], BF16)
    s_tr = ctx.sb(pre + "tr", [128, 128], BF16)
    p_t = ctx.ps(pre + "pt", [128, 128], BF16)
    s_id = ctx.sb("s_ident")
    prev = None
    for i in range(NT):
        em = chain_maker(r, prev, *d_s, *(after if i == 0 else ()))
        em("sync", lambda e, i=i: e.dma_start(
            out=s_x[:], in_=xn1_ap[128 * i:128 * (i + 1), :]), dma=True)
        em("vector", lambda e: e.tensor_scalar(s_x[:], s_x[:], s_rep[:, 1:2], None, ALU.mult))
        em("vector", lambda e: e.tensor_copy(s_i[:], s_x[:]))
        em("vector", lambda e: e.tensor_copy(s_x[:], s_i[:]))
        em("vector", lambda e: e.tensor_scalar(
            s_x[:], s_x[:], -128.0, 127.0, ALU.max, ALU.min))
        em("vector", lambda e: e.tensor_copy(s_b[:], s_x[:]))
        for c in range(ND):
            em("tensor", lambda e, c=c: e.transpose(
                p_t[:], s_b[:, 128 * c:128 * (c + 1)], s_id[:]))
            em("tensor", lambda e: e.matmul(
                ctx.ps("p_guard")[:], ctx.sb("s_onesbf")[0:1, 0:1],
                ctx.sb("s_onesbf")[0:1, 0:16], start=True, stop=True))
            prev = em("scalar", lambda e, c=c, i=i: e.copy(
                s_xqT[:, T * c + 128 * i:T * c + 128 * (i + 1)], p_t[:]))
    d_xq = prev
    s_swg = ctx.sb(pre + "swg", [128, NI], F32)
    s_swu = ctx.sb(pre + "swu", [128, NI], F32)
    lg = r.op("sync", lambda e: e.dma_start(out=s_swg[:, 0:NI], in_=swg_ap), dma=True,
              waits=tuple(after))
    lu = r.op("sync", lambda e: e.dma_start(out=s_swu[:, 0:NI], in_=swu_ap), dma=True)
    sg1 = r.op("vector", lambda e: e.tensor_scalar(
        s_swg[:], s_swg[:], s_rep[:, 0:1], None, ALU.mult),
        mark=True, waits=(lg,) + tuple(d_s))
    sg2 = r.op("vector", lambda e: e.tensor_scalar(
        s_swu[:], s_swu[:], s_rep[:, 0:1], None, ALU.mult), mark=True, waits=(lu, sg1))
    s_wg = [ctx.sb(pre + f"wg{k}", [128, ND * 128], BF16) for k in range(2)]
    s_wu = [ctx.sb(pre + f"wu{k}", [128, ND * 128], BF16) for k in range(2)]
    s_h1 = ctx.sb(pre + "h1", [128, 512], F32)
    s_h = ctx.sb(pre + "h", [128, T], F32)
    s_mh = ctx.sb(pre + "mh", [128, 4], F32)
    s_mr = ctx.sb(pre + "mr", [1, 128], F32)
    p_h = [ctx.ps(pre + f"ph{k}", [128, 512], F32) for k in range(2)]
    wl_prev = [None, None]
    ev_prev = [None, None]
    st_prev = None
    mx_prev = None
    s2_prev = None
    for ic in range(NI):
        b = ic % 2
        wlg = r.op("gpsimd", lambda e, ic=ic, b=b: e.dma_start(
            out=s_wg[b][:].rearrange("p (a c) -> p a c", a=ND),
            in_=dap(wqgT_ap, 128 * ic, [[IFF, 128], [128 * IFF, ND], [1, 128]])),
            dma=True, waits=(wl_prev[b],) + tuple(after))
        wlu = r.op("gpsimd", lambda e, ic=ic, b=b: e.dma_start(
            out=s_wu[b][:].rearrange("p (a c) -> p a c", a=ND),
            in_=dap(wquT_ap, 128 * ic, [[IFF, 128], [128 * IFF, ND], [1, 128]])),
            dma=True, waits=tuple(after) if ic == 0 else ())
        for tc in range(4):
            pb = tc % 2
            mm1 = None
            for c in range(ND):
                mm1 = r.op("tensor", lambda e, b=b, c=c, ic=ic, tc=tc, pb=pb: e.matmul(
                    p_h[pb][:], s_wg[b][:, 128 * c:128 * (c + 1)],
                    s_xqT[:, T * c + 512 * tc:T * c + 512 * (tc + 1)],
                    start=(c == 0), stop=(c == ND - 1)), mark=(c == ND - 1),
                    waits=(wlg, d_xq, ev_prev[pb]) if c == 0 else ())
            g = pe_done(ctx, mm1)
            e1 = r.op("scalar", lambda e, pb=pb, ic=ic: e.activation(
                s_h1[:], p_h[pb][:], AF.Silu, scale=s_swg[:, ic:ic + 1]),
                mark=True, waits=(g, sg2, s2_prev))
            mm3 = None
            for c in range(ND):
                mm3 = r.op("tensor", lambda e, b=b, c=c, ic=ic, tc=tc, pb=pb: e.matmul(
                    p_h[pb][:], s_wu[b][:, 128 * c:128 * (c + 1)],
                    s_xqT[:, T * c + 512 * tc:T * c + 512 * (tc + 1)],
                    start=(c == 0), stop=(c == ND - 1)), mark=(c == ND - 1),
                    waits=(wlu, e1) if c == 0 else ())
            g3 = pe_done(ctx, mm3)
            s2 = r.op("vector", lambda e, pb=pb, ic=ic, tc=tc: e.scalar_tensor_tensor(
                s_h[:, 512 * tc:512 * (tc + 1)], p_h[pb][:], s_swu[:, ic:ic + 1],
                s_h1[:], ALU.mult, ALU.mult), mark=True, waits=(g3, sg2, st_prev))
            ev_prev[pb] = s2
            s2_prev = s2
        mx = r.op("vector", lambda e, ic=ic: e.reduce_max(
            s_mh[:, 1:2], s_h[:], AX.X, apply_absolute_value=True), mark=True)
        if ic == 0:
            mx_prev = r.op("vector", lambda e: e.tensor_copy(s_mh[:, 0:1], s_mh[:, 1:2]),
                           mark=True, waits=(mx,))
        else:
            mx_prev = r.op("vector", lambda e: e.tensor_max(
                s_mh[:, 0:1], s_mh[:, 0:1], s_mh[:, 1:2]), mark=True, waits=(mx, mx_prev))
        st_prev = r.op("sync", lambda e, ic=ic: e.dma_start(
            out=hF_ap[128 * ic:128 * (ic + 1), :], in_=s_h[:]), dma=True, waits=(mx,))
        wl_prev[b] = st_prev
    em = chain_maker(r, mx_prev, st_prev)
    em("sync", lambda e: e.dma_start(out=s_mr[:], in_=AP(s_mh, 0, [[4, 128], [1, 1]]),
                                     allow_slow_non_contiguous=True), dma=True)
    em("vector", lambda e: e.reduce_max(s_mh[0:1, 1:2], s_mr[:], AX.X))
    return [em("sync", lambda e: e.dma_start(out=lmaxh_ap[None, :], in_=s_mh[0:1, 1:2]), dma=True)]


def phase_ffn_b(ctx, pre, hF_ap, clip_d_ap, lmaxh_ap, nmax, wdTs_ap,
                x1_ap, s1_ap, out_ap, after=(), dbg_hq=None, dbg_g=None):
    r = ctx.rec
    NI = IFF // 128
    s_rep, d_s = scalar_prep(ctx, pre + "s", clip_d_ap, lmaxh_ap, nmax, after)
    s_h = ctx.sb(pre + "h", [128, T], F32)
    s_i = ctx.sb(pre + "i", [128, T], I32)
    s_hq2 = [ctx.sb(pre + f"hq{k}", [128, 6 * T], BF16) for k in range(4)]
    prev = None
    for ic in range(NI):
        em = chain_maker(r, prev, *d_s, *(after if ic == 0 else ()))
        em("sync", lambda e, ic=ic: e.dma_start(
            out=s_h[:], in_=hF_ap[128 * ic:128 * (ic + 1), :]), dma=True)
        em("vector", lambda e: e.tensor_scalar(s_h[:], s_h[:], s_rep[:, 1:2], None, ALU.mult))
        em("vector", lambda e: e.tensor_copy(s_i[:], s_h[:]))
        em("vector", lambda e: e.tensor_copy(s_h[:], s_i[:]))
        em("vector", lambda e: e.tensor_scalar(
            s_h[:], s_h[:], -128.0, 127.0, ALU.max, ALU.min))
        prev = em("vector", lambda e, ic=ic: e.tensor_copy(
            s_hq2[ic // 6][:, T * (ic % 6):T * (ic % 6 + 1)], s_h[:]))
    d_hq = prev
    if dbg_hq is not None:
        for k in range(4):
            w_ = min(6 * T, 22 * T - k * 6 * T)
            d_hq = r.op("sync", lambda e, k=k, w_=w_: e.dma_start(
                out=dbg_hq[:, 6 * T * k:6 * T * k + w_], in_=s_hq2[k][:, 0:w_]),
                dma=True, waits=(prev,))
    s_wd2 = [ctx.sb(pre + f"wd{k}", [128, 11 * 1024], BF16) for k in range(2)]
    wl = None
    for k in range(2):
        wl = r.op("gpsimd", lambda e, k=k: e.dma_start(
            out=s_wd2[k][:].rearrange("p (a c) -> p a c", a=11),
            in_=dap(wdTs_ap, 11 * k * 128 * 1024, [[1024, 128], [128 * 1024, 11], [1, 1024]])),
            dma=True, waits=tuple(after))
    s_g = [ctx.sb(pre + f"g{k}", [128, 512], BF16) for k in range(2)]
    s_x = ctx.sb(pre + "x", [128, 512], F32)
    s_s1 = ctx.sb(pre + "s1", [128, 512], BF16)
    s_o = ctx.sb(pre + "o", [128, 512], F32)
    p_d = [ctx.ps(pre + f"pd{k}", [128, 512], F32) for k in range(2)]
    ev_prev = [None, None]
    m1_prev = [None, None]
    st_prev = None
    dones = []
    for i in range(NT):
        for half in range(2):
            pb = half
            mm = None
            for ic in range(NI):
                mm = r.op("tensor", lambda e, i=i, half=half, ic=ic, pb=pb: e.matmul(
                    p_d[pb][:], s_hq2[ic // 6][:, T * (ic % 6) + 128 * i:T * (ic % 6) + 128 * (i + 1)],
                    s_wd2[ic // 11][:, 1024 * (ic % 11) + 512 * half:1024 * (ic % 11) + 512 * (half + 1)],
                    start=(ic == 0), stop=(ic == NI - 1)), mark=(ic == NI - 1),
                    waits=(wl, d_hq, ev_prev[pb]) if ic == 0 else ())
            g = pe_done(ctx, mm)
            ev = r.op("scalar", lambda e, pb=pb: e.activation(
                s_g[pb][:], p_d[pb][:], AF.Silu, scale=s_rep[:, 0:1]),
                mark=True, waits=(g, m1_prev[pb]) + tuple(d_s))
            ev_prev[pb] = ev
            lx = r.op("sync", lambda e, i=i, half=half: e.dma_start(
                out=s_x[:], in_=x1_ap[128 * i:128 * (i + 1), 512 * half:512 * (half + 1)]),
                dma=True, waits=(st_prev,))
            ls = r.op("sync", lambda e, i=i, half=half: e.dma_start(
                out=s_s1[:], in_=s1_ap[128 * i:128 * (i + 1), 512 * half:512 * (half + 1)]),
                dma=True)
            m1 = r.op("vector", lambda e, pb=pb: e.tensor_tensor(s_o[:], s_s1[:], s_g[pb][:], ALU.mult),
                      mark=True, waits=(ev, ls, st_prev))
            m1_prev[pb] = m1
            m2 = r.op("vector", lambda e: e.tensor_add(s_o[:], s_o[:], s_x[:]),
                      mark=True, waits=(m1, lx))
            st_prev = r.op("sync", lambda e, i=i, half=half: e.dma_start(
                out=out_ap[128 * i:128 * (i + 1), 512 * half:512 * (half + 1)],
                in_=s_o[:]), dma=True, waits=(m2,))
            dones.append(st_prev)
    return dones[-1:]


def finalize(ctx, *labels):
    r = ctx.rec
    s_fin = ctx.sb("s_fin", [1, 4], F32)
    return r.op("vector", lambda e: e.tensor_copy(s_fin[:, 0:1], s_fin[:, 1:2]),
                mark=True, waits=tuple(l for l in labels if l is not None))


def build_launch_a():
    ctx = Ctx()
    d_l0 = build_l0(ctx)
    x1 = ctx.dram("x1", [T, D], F32, kind="ExternalOutput").ap()
    xn1 = ctx.dram("xn1", [T, D], F32, kind="ExternalOutput").ap()
    xn1T = ctx.dram("xn1T", [D, T], BF16, kind="Internal").ap()
    lmax1 = ctx.dram("lmax1", [1], F32, kind="ExternalOutput").ap()
    Wattn = ctx.din("Wattn", [D, 3 * D], F32)
    battn = ctx.din("battn", [3 * D], F32)
    Wproj = ctx.din("Wproj", [D, D], F32)
    bproj = ctx.din("bproj", [D], F32)
    embkR = ctx.din("embkR", [NR, HD], F32)
    embvRE = ctx.din("embvRE", [T, HD], F32)
    qkT = ctx.dram("qkT", [2048, T], BF16, kind="Internal").ap()
    v = ctx.dram("v", [T, D], BF16, kind="Internal").ap()
    s1 = ctx.dram("s1", [T, D], BF16, kind="ExternalOutput").ap()

    d_ln = phase_ln(ctx, "l1n_", x1, dict(xn_f32=xn1, xnT_bf=xn1T),
                    affine=None, after=tuple(d_l0))
    ctx.free("l1n_")
    d_qk = phase_l1_qkv(ctx, "qk_", xn1T, Wattn, battn, qkT, v,
                        after=tuple(d_ln))
    ctx.free("qk_")
    d_am = phase_absmax(ctx, "am_", xn1, T, D, lmax1, after=tuple(d_qk))
    ctx.free("am_")
    d_at = phase_attn(ctx, "at_", qkT, v, embkR, embvRE, Wproj, bproj, s1,
                      after=tuple(d_qk) + tuple(d_am))
    finalize(ctx, *(tuple(d_at) + tuple(d_am)))
    return ctx.build()


def build_launch_b():
    ctx = Ctx()
    xn1 = ctx.din("xn1", [T, D], F32)
    lmax8 = ctx.din("lmax8", [8], F32)
    clip_x = ctx.din("clip_x", [1], F32)
    wg = ctx.din("w_gate", [IFF, D], F32)
    wu = ctx.din("w_up", [IFF, D], F32)
    wd = ctx.din("w_down", [D, IFF], F32)
    wqgT = ctx.dram("wqgT", [D, IFF], BF16).ap()
    wquT = ctx.dram("wquT", [D, IFF], BF16).ap()
    wdTs = ctx.dram("wdTs", [IFF, D], BF16, kind="ExternalOutput").ap()
    swg = ctx.dram("swg", [128, IFF // 128], F32).ap()
    swu = ctx.dram("swu", [128, IFF // 128], F32).ap()
    hF = ctx.dram("hF", [IFF, T], F32, kind="ExternalOutput").ap()
    lmaxh = ctx.dram("lmaxh", [1], F32, kind="ExternalOutput").ap()
    cs = load_consts(ctx)
    dq = phase_wq_multi(ctx, "q_", [
        dict(w_ap=wg, nrow=IFF, ncol=D, wqT_ap=wqgT, scales_ap=swg,
             dequant=False, stream="g"),
        dict(w_ap=wu, nrow=IFF, ncol=D, wqT_ap=wquT, scales_ap=swu,
             dequant=False, stream="u"),
        dict(w_ap=wd, nrow=D, ncol=IFF, wqT_ap=wdTs, scales_ap=None,
             dequant=True, stream="w"),
    ], after=tuple(cs["ld"]))
    ctx.free("q_")
    d4 = phase_ffn_a(ctx, "fa_", xn1, clip_x, lmax8, 8, wqgT, wquT, swg, swu,
                     hF, lmaxh, after=tuple(dq))
    finalize(ctx, *(tuple(d4) + tuple(dq)))
    return ctx.build()


def build_launch_c():
    ctx = Ctx()
    hF = ctx.din("hF", [IFF, T], F32)
    lmaxh8 = ctx.din("lmaxh8", [8], F32)
    wdTs = ctx.din("wdTs", [IFF, D], BF16)
    x1 = ctx.din("x1", [T, D], F32)
    s1 = ctx.din("s1", [T, D], BF16)
    clip_d = ctx.din("clip_down", [1], F32)
    out = ctx.dout("out", [T, D], F32)
    cs = load_consts(ctx)
    d5 = phase_ffn_b(ctx, "fb_", hF, clip_d, lmaxh8, 8, wdTs, x1, s1, out,
                     after=tuple(cs["ld"]))
    finalize(ctx, *tuple(d5))
    return ctx.build()


def l1a_host_inputs(z):
    embk = np.asarray(z["L1_attn_embk"])
    embv = np.asarray(z["L1_attn_embv"])
    return dict(
        Wattn=np.asarray(z["L1_attn_Wattn"]),
        battn=np.asarray(z["L1_attn_battn"]),
        Wproj=np.asarray(z["L1_attn_Wproj"]),
        bproj=np.asarray(z["L1_attn_bproj"]),
        embkR=np.ascontiguousarray(embk[::-1]),
        embvRE=np.ascontiguousarray(embv[np.minimum(2047 - np.arange(2048), 1023)]),
    )


def kernel(**inputs):
    z = {k: np.asarray(v) for k, v in inputs.items()}
    hc = host_constants()
    l1a = l1a_host_inputs(z)
    ncA = build_launch_a()
    imsA = []
    for b in range(8):
        im = l0_host_inputs(z, b)
        im.update(l1a)
        imsA.append(im)
    ra = run_bass_kernel_spmd(ncA, imsA, core_ids=list(range(8))).results

    lmax8 = np.ascontiguousarray(
        np.concatenate([np.asarray(ra[b]["lmax1"]) for b in range(8)]))
    ncB = build_launch_b()
    imsB = [dict(xn1=np.asarray(ra[b]["xn1"]), lmax8=lmax8,
                 clip_x=z["L1_ffn_clip_x"], w_gate=z["L1_ffn_w_gate"],
                 w_up=z["L1_ffn_w_up"], w_down=z["L1_ffn_w_down"], **hc)
            for b in range(8)]
    rb = run_bass_kernel_spmd(ncB, imsB, core_ids=list(range(8))).results

    lmaxh8 = np.ascontiguousarray(
        np.concatenate([np.asarray(rb[b]["lmaxh"]) for b in range(8)]))
    ncC = build_launch_c()
    imsC = [dict(hF=np.asarray(rb[b]["hF"]), lmaxh8=lmaxh8,
                 wdTs=np.asarray(rb[b]["wdTs"]), x1=np.asarray(ra[b]["x1"]),
                 s1=np.asarray(ra[b]["s1"]), clip_down=z["L1_ffn_clip_down"],
                 **hc)
            for b in range(8)]
    rc = run_bass_kernel_spmd(ncC, imsC, core_ids=list(range(8))).results
    return np.stack([np.asarray(rc[b]["out"]) for b in range(8)], 0).astype(np.float32)


# version 06 (final)
# speedup vs baseline: 1.2968x; 1.0098x over previous
import contextlib
import numpy as np
import ml_dtypes
import concourse.bass as bass
import concourse.mybir as mybir
from concourse.ap import AP
from concourse.bass_utils import run_bass_kernel_spmd

F32 = mybir.dt.float32
BF16 = mybir.dt.bfloat16
I32 = mybir.dt.int32
AF = mybir.ActivationFunctionType
ALU = mybir.AluOpType
AX = mybir.AxisListType
BF = ml_dtypes.bfloat16

T = 2048
D = 1024
DI = 2048
NS = 16
DTR = 64
KC = 4
NT = T // 128
ND = D // 128
NDI = DI // 128
EPS = 1e-5
NH = 16
HD = 64
NR = 1024
IFF = 2816
QMAX = 127.0

ENGINES = ("sync", "tensor", "scalar", "vector", "gpsimd")
SELF_SYNC = False
SELF_SYNC_ENGINES = {'vector', 'gpsimd', 'scalar', 'tensor'}
CVALS = [EPS, 1.0, 0.5, -0.5, -1.0, 2.0, 0.0, -1e9, 127.0, -128.0, 0.125]


class Op:
    __slots__ = ("eng", "fn", "sem", "inc", "waits", "label")

    def __init__(self, eng, fn, sem, inc, waits, label):
        self.eng, self.fn, self.sem, self.inc, self.waits, self.label = (
            eng, fn, sem, inc, waits, label)


class Rec:
    def __init__(self):
        self.ops = []
        self.cnt = {}
        self.marks = {}
        self._auto = 0

    def op(self, eng, fn, *, dma=False, mark=False, waits=(), label=None):
        if dma:
            stream = dma if isinstance(dma, str) else "d"
            sem = f"{eng}.{stream}"
            inc = 16
        else:
            sem = eng + "_c"
            inc = 1 if mark else 0
        wl = []
        if dma:
            prev = self.cnt.get(sem, 0)
            if prev:
                wl.append((sem, prev))
        if inc:
            v = self.cnt.get(sem, 0) + inc
            self.cnt[sem] = v
            if label is None:
                label = f"_a{self._auto}"
                self._auto += 1
            self.marks[label] = (eng, sem, v)
        for w in waits:
            if w is None:
                continue
            weng, wsem, wval = self.marks[w]
            if (weng == eng and not SELF_SYNC and eng not in SELF_SYNC_ENGINES
                    and "." not in wsem):
                continue
            wl.append((wsem, wval))
        self.ops.append(Op(eng, fn, sem if inc else None, inc, wl, label))
        return label

    def barrier_label(self, eng_needed=ENGINES):
        out = []
        for label, (eng, sem, v) in self.marks.items():
            if self.cnt.get(sem) == v:
                out.append(label)
        return out


def play(rec, nc):
    with contextlib.ExitStack() as stack:
        sems = {}
        for name in rec.cnt:
            sems[name] = stack.enter_context(nc.semaphore(name))
        per_eng = {e: [] for e in ENGINES}
        for op in rec.ops:
            per_eng[op.eng].append(op)
        with nc.Block() as blk:
            for eng in ENGINES:
                ops = per_eng[eng]
                if not ops:
                    continue

                def mk(ops):
                    def f(e):
                        waited = {}
                        for op in ops:
                            for (wsem, wval) in op.waits:
                                if waited.get(wsem, -1) >= wval:
                                    continue
                                e.wait_ge(sems[wsem], wval)
                                waited[wsem] = wval
                            ins = op.fn(e)
                            if op.inc:
                                ins.then_inc(sems[op.sem], op.inc)
                    return f
                getattr(blk, eng)(mk(ops))


class Ctx:

    def __init__(self):
        self.nc = bass.Bass("TRN2", target_bir_lowering=False, debug=False)
        self.rec = Rec()
        self._stack = contextlib.ExitStack()
        self._sbuf = {}
        self._psum = {}
        self._dram = {}

    def sb(self, name, shape=None, dt=None):
        if name not in self._sbuf:
            assert shape is not None, name
            cm = self.nc.sbuf_tensor(name, shape, dt)
            self._sbuf[name] = (cm.__enter__(), cm)
        return self._sbuf[name][0]

    def ps(self, name, shape=None, dt=F32):
        if name not in self._psum:
            assert shape is not None, name
            cm = self.nc.psum_tensor(name, shape, dt)
            self._psum[name] = (cm.__enter__(), cm)
        return self._psum[name][0]

    def free(self, prefix):
        for pool in (self._sbuf, self._psum):
            for name in reversed([n for n in pool if n.startswith(prefix)]):
                pool[name][1].__exit__(None, None, None)
                del pool[name]

    def dram(self, name, shape, dt, kind="Internal"):
        if name not in self._dram:
            self._dram[name] = self.nc.dram_tensor(name, shape, dt, kind=kind)
        return self._dram[name]

    def din(self, name, shape, dt=F32):
        return self.dram(name, shape, dt, kind="ExternalInput").ap()

    def dout(self, name, shape, dt=F32):
        return self.dram(name, shape, dt, kind="ExternalOutput").ap()

    def build(self):
        play(self.rec, self.nc)
        self._stack.close()
        return self.nc


def host_constants():
    c = {}
    c["c_ident_bf"] = np.eye(128, dtype=BF)
    c["c_ident_f"] = np.eye(128, dtype=np.float32)
    c["c_ones_f"] = np.ones((1, 128), np.float32)
    c["c_ones_bf"] = np.ones((1, 128), BF)
    pat = np.zeros((128, 16, 128), BF)
    for j in range(16):
        for ds in range(8):
            pat[8 * j + ds, j, ds * 16:ds * 16 + 16] = 1
    c["c_rep_pat"] = np.ascontiguousarray(pat.reshape(128, 16 * 128))
    red = np.zeros((128, 16, 128), BF)
    for j in range(16):
        for ds in range(8):
            red[ds * 16:ds * 16 + 16, j, 8 * j + ds] = 1
    c["c_red_pat"] = np.ascontiguousarray(red.reshape(128, 16 * 128))
    c["c_cvec"] = np.tile(np.asarray(CVALS, np.float32)[None, :], (128, 1))
    bcp = np.zeros((16, 128), BF)
    for ds in range(8):
        for n in range(16):
            bcp[n, ds * 16 + n] = 1
    c["c_bc_pat"] = bcp
    m = np.zeros((128, 1024), np.float32)
    for p in range(128):
        m[p, p + 513:] = -1e9
    c["c_maskP"] = m.astype(BF)
    return c


def load_consts(ctx):
    r = ctx.rec
    ident = ctx.din("c_ident_bf", [128, 128], BF16)
    identf = ctx.din("c_ident_f", [128, 128], F32)
    ones_f = ctx.din("c_ones_f", [1, 128], F32)
    ones_bf = ctx.din("c_ones_bf", [1, 128], BF16)
    s_id = ctx.sb("s_ident", [128, 128], BF16)
    s_idf = ctx.sb("s_identf", [128, 128], F32)
    s_onesf = ctx.sb("s_onesf", [1, 128], F32)
    s_onesbf = ctx.sb("s_onesbf", [1, 128], BF16)
    ctx.ps("p_guard", [1, 16], F32)
    cvec = ctx.din("c_cvec", [128, len(CVALS)], F32)
    s_cvec = ctx.sb("s_cvec", [128, len(CVALS)], F32)
    l0 = r.op("sync", lambda e: e.dma_start(out=s_cvec[:], in_=cvec), dma=True)
    for j, v in enumerate(CVALS):
        ctx.nc.const_aps.register(F32, v, s_cvec[:, j:j + 1])
    l1 = r.op("sync", lambda e: e.dma_start(out=s_id[:], in_=ident), dma=True)
    l1b = r.op("sync", lambda e: e.dma_start(out=s_idf[:], in_=identf), dma=True)
    l2 = r.op("sync", lambda e: e.dma_start(out=s_onesf[:], in_=ones_f), dma=True)
    l3 = r.op("sync", lambda e: e.dma_start(out=s_onesbf[:], in_=ones_bf), dma=True)
    return dict(ident=s_id, identf=s_idf, onesf=s_onesf, onesbf=s_onesbf,
                ld=[l0, l1, l1b, l2, l3])


def pe_done(ctx, *labs):
    r = ctx.rec
    pg = ctx.ps("p_guard", [1, 16], F32)
    ob = ctx.sb("s_onesbf")
    return r.op("tensor", lambda e: e.matmul(
        pg[:], ob[0:1, 0:1], ob[0:1, 0:16], start=True, stop=True),
        mark=True, waits=labs)


def dap(base_ap, offset, axes):
    return AP(base_ap.tensor, offset, [list(a) for a in axes])


def bcast_row(ctx, pre, row_ap, width, out_dt, ld_label, n_parts=128):
    r = ctx.rec
    c = ctx.sb("s_onesf")
    p = ctx.ps(pre + "_pb", [128, 512], F32)
    out = ctx.sb(pre + "_rep", [128, width], out_dt)
    done = None
    for c0 in range(0, width, 512):
        w = min(512, width - c0)
        mm = r.op("tensor", lambda e, c0=c0, w=w: e.matmul(
            p[:, 0:w], c[:, 0:n_parts], row_ap[:, c0:c0 + w],
            start=True, stop=True), mark=True, waits=(ld_label, done))
        g = pe_done(ctx, mm)
        done = r.op("scalar", lambda e, c0=c0, w=w: e.copy(
            out[:, c0:c0 + w], p[:, 0:w]), mark=True, waits=(g,))
    return out, done



def chain_maker(r, *init):
    state = {"last": None, "init": tuple(x for x in init if x)}

    def em(eng, fn, *, dma=False, waits=()):
        w = state["init"] + (state["last"],) + tuple(waits)
        state["init"] = ()
        lab = r.op(eng, fn, dma=dma, mark=True, waits=w)
        state["last"] = lab
        return lab
    return em


def phase_ln(ctx, pre, x_ap, outs, affine=None, after=()):
    r = ctx.rec
    s_x = [ctx.sb(pre + f"x{k}", [128, D], F32) for k in range(2)]
    s_sq = ctx.sb(pre + "sq", [128, D], F32)
    s_af = ctx.sb(pre + "af", [128, D], F32) if affine else None
    s_st = ctx.sb(pre + "st", [128, 12], F32)
    s_xn = ctx.sb(pre + "xn", [128, D], F32)
    s_xnb = ctx.sb(pre + "xnb", [128, D], BF16)
    s_x2b = ctx.sb(pre + "x2b", [128, D], BF16) if affine else None
    s_tr = ctx.sb(pre + "tr", [128, 512], BF16)
    s_id = ctx.sb("s_ident")
    p_tr = ctx.ps(pre + "ptr", [128, 512], BF16)

    grep = brep = None
    pend = []
    if affine:
        gamma_ap, beta_ap = affine
        s_grow = ctx.sb(pre + "grow", [1, D], F32)
        s_brow = ctx.sb(pre + "brow", [1, D], F32)
        lg = r.op("sync", lambda e: e.dma_start(out=s_grow[:], in_=gamma_ap[None, :]), dma=True)
        lb = r.op("sync", lambda e: e.dma_start(out=s_brow[:], in_=beta_ap[None, :]), dma=True)
        grep, gd = bcast_row(ctx, pre + "g", s_grow, D, F32, lg)
        brep, bd = bcast_row(ctx, pre + "b", s_brow, D, F32, lb)
        pend = [gd, bd]

    dones = []
    ld_prev = None
    ld_cur = r.op("sync", lambda e: e.dma_start(out=s_x[0][:], in_=x_ap[0:128, :]),
                  dma=True, waits=tuple(after))
    last_tile = None
    for i in range(NT):
        x = s_x[i % 2]
        em = chain_maker(r, ld_cur, last_tile, *(pend if i == 0 else []))
        if i + 1 < NT:
            ld_nxt = r.op("sync", lambda e, i=i: e.dma_start(
                out=s_x[(i + 1) % 2][:], in_=x_ap[128 * (i + 1):128 * (i + 2), :]),
                dma=True, waits=(ld_prev, last_tile))
        em("scalar", lambda e, x=x: e.activation(s_sq[:], x[:], AF.Square, accum_out=s_st[:, 1:2]))
        em("vector", lambda e, x=x: e.reduce_sum(s_st[:, 0:1], x[:], AX.X))
        em("vector", lambda e: e.tensor_scalar_mul(s_st[:, 2:3], s_st[:, 0:1], 1.0 / D))
        em("vector", lambda e: e.tensor_scalar_mul(s_st[:, 6:7], s_st[:, 1:2], 1.0 / D))
        em("vector", lambda e: e.tensor_tensor(s_st[:, 7:8], s_st[:, 2:3], s_st[:, 2:3], ALU.mult))
        em("vector", lambda e: e.tensor_tensor(s_st[:, 3:4], s_st[:, 6:7], s_st[:, 7:8], ALU.subtract))
        em("scalar", lambda e: e.activation(s_st[:, 4:5], s_st[:, 3:4], AF.Ln, bias=EPS))
        em("scalar", lambda e: e.activation(s_st[:, 5:6], s_st[:, 4:5], AF.Exp, scale=-0.5))
        xnd = em("vector", lambda e, x=x: e.tensor_scalar(
            s_xn[:], x[:], s_st[:, 2:3], s_st[:, 5:6], ALU.subtract, ALU.mult))
        em("vector", lambda e: e.tensor_copy(s_xnb[:], s_xn[:]))
        if outs.get("xn_f32") is not None:
            em("sync", lambda e, i=i: e.dma_start(
                out=outs["xn_f32"][128 * i:128 * (i + 1), :], in_=s_xn[:]), dma=True)
        if outs.get("xn_bf") is not None:
            em("sync", lambda e, i=i: e.dma_start(
                out=outs["xn_bf"][128 * i:128 * (i + 1), :], in_=s_xnb[:]), dma=True)
        if affine:
            em("scalar", lambda e: e.activation(s_st[:, 8:9], s_st[:, 4:5], AF.Exp, scale=-1.0))
            em("vector", lambda e: e.tensor_tensor(s_st[:, 9:10], s_st[:, 3:4], s_st[:, 8:9], ALU.mult))
            em("scalar", lambda e: e.activation(s_st[:, 10:11], s_st[:, 9:10], AF.Ln, bias=EPS))
            em("scalar", lambda e: e.activation(s_st[:, 11:12], s_st[:, 10:11], AF.Exp, scale=-0.5))
            em("vector", lambda e: e.tensor_scalar(s_af[:], s_xn[:], s_st[:, 11:12], None, ALU.mult))
            em("vector", lambda e: e.tensor_tensor(s_af[:], s_af[:], grep[:], ALU.mult))
            em("vector", lambda e: e.tensor_add(s_af[:], s_af[:], brep[:]))
            em("vector", lambda e: e.tensor_copy(s_x2b[:], s_af[:]))
        for (key, src) in (("xnT_bf", s_xnb), ("xn2T_bf", s_x2b)):
            if outs.get(key) is None:
                continue
            for c in range(0, 8, 4):
                for cc in range(c, c + 4):
                    em("tensor", lambda e, src=src, cc=cc: e.transpose(
                        p_tr[:, 128 * (cc % 4):128 * (cc % 4 + 1)],
                        src[:, 128 * cc:128 * (cc + 1)], s_id[:]))
                em("tensor", lambda e: e.matmul(
                    ctx.ps("p_guard")[:], ctx.sb("s_onesbf")[0:1, 0:1],
                    ctx.sb("s_onesbf")[0:1, 0:16], start=True, stop=True))
                em("scalar", lambda e: e.copy(s_tr[:], p_tr[:]))
                em("sync", lambda e, key=key, i=i, c=c: e.dma_start(
                    out=dap(outs[key], 128 * c * T + 128 * i,
                            [[T, 128], [128 * T, 4], [1, 128]]),
                    in_=s_tr[:].rearrange("p (a t) -> p a t", a=4)), dma=True)
        last_tile = em("vector", lambda e: e.tensor_copy(s_st[:, 0:1], s_st[:, 0:1]))
        dones.append(last_tile)
        ld_prev = ld_cur
        if i + 1 < NT:
            ld_cur = ld_nxt
    return dones[-1:]


def vec2cols(ctx, pre, vec_ap, n, ld_waits=(), dt_in=F32, psname=None):
    r = ctx.rec
    nch = n // 128
    s_row = ctx.sb(pre + "row", [1, n], dt_in)
    s_col = ctx.sb(pre + "col", [128, nch], dt_in)
    s_id = ctx.sb("s_identf")
    p = ctx.ps(psname or (pre + "pv"), [128, 16], F32)
    ld = r.op("sync", lambda e: e.dma_start(out=s_row[:], in_=vec_ap[None, :]),
              dma=True, waits=ld_waits)
    done = None
    for j in range(nch):
        mm = r.op("tensor", lambda e, j=j: e.transpose(
            p[:, j % 16:j % 16 + 1], s_row[:, 128 * j:128 * (j + 1)], s_id[0:1, 0:1]),
            mark=True, waits=(ld, done))
        g = pe_done(ctx, mm)
        done = r.op("scalar", lambda e, j=j: e.copy(
            s_col[:, j:j + 1], p[:, j % 16:j % 16 + 1]), mark=True, waits=(g,))
    return s_col, done


def wcast_load(ctx, name, w_ap, rows, row0, cols, col0, shape, waits=()):
    r = ctx.rec
    wrow = w_ap.shape[1]
    if rows < 128:
        t = ctx.sb(name, [rows, cols], BF16)
        lab = r.op("gpsimd", lambda e: e.dma_start(
            out=t[:], in_=dap(w_ap, row0 * wrow + col0, [[wrow, rows], [1, cols]])),
            dma=True, waits=waits)
        return t, lab
    nci = rows // 128
    t = ctx.sb(name, [128, nci * cols], BF16)
    lab = r.op("gpsimd", lambda e: e.dma_start(
        out=t[:].rearrange("p (a c) -> p a c", a=nci),
        in_=dap(w_ap, row0 * wrow + col0, [[wrow, 128], [128 * wrow, nci], [1, cols]])),
        dma=True, waits=waits)
    return t, lab


def phase_mamba_front(ctx, pre, xn2T_ap, W_in_ap, b_in_ap, conv_w_ap, conv_b_ap,
                      W_x_ap, W_dt_ap, b_dt_ap, outs, after=()):
    r = ctx.rec
    TC = 512
    NTC = T // TC
    bin_u, d1 = vec2cols(ctx, pre + "biu", b_in_ap[0:DI], DI, after, psname=pre + "pv")
    bin_r, d2 = vec2cols(ctx, pre + "bir", b_in_ap[DI:2 * DI], DI, psname=pre + "pv")
    cb, d3 = vec2cols(ctx, pre + "cb", conv_b_ap, DI, psname=pre + "pv")
    bdt, d4 = vec2cols(ctx, pre + "bdt", b_dt_ap, DI, psname=pre + "pv")
    cw = []
    for k in range(KC):
        cwk, d5 = vec2cols(ctx, pre + f"cw{k}", conv_w_ap[k, :], DI, psname=pre + "pv")
        cw.append(cwk)
    vdone = (d1, d2, d3, d4, d5)

    s_xn2 = ctx.sb(pre + "xn2", [128, ND * T], BF16)
    xl = []
    for c in range(ND):
        xl.append(r.op("sync", lambda e, c=c: e.dma_start(
            out=s_xn2[:, T * c:T * (c + 1)], in_=xn2T_ap[128 * c:128 * (c + 1), :]),
            dma=True, waits=tuple(after)))
    s_w = [ctx.sb(pre + f"w{k}", [128, 8 * 128], BF16) for k in range(2)]
    p_mm = [ctx.ps(pre + f"pm{k}", [128, 512], F32) for k in range(2)]
    s_u = ctx.sb(pre + "u", [128, T + KC - 1], F32)
    s_acc = ctx.sb(pre + "acc", [128, T], F32)
    s_ob = ctx.sb(pre + "ob", [128, T], BF16)
    zpad = r.op("vector", lambda e: e.memset(s_u[:, 0:KC - 1], 0.0), mark=True, waits=tuple(after))

    wl_prev = [None, None]
    ev_prev = [None, None]
    st_prev = None
    u_stores = {}
    NCO = 2 * DI // 128
    for co in range(NCO):
        b = co % 2
        wl = r.op("gpsimd", lambda e, co=co, b=b: e.dma_start(
            out=s_w[b][:].rearrange("p (a c) -> p a c", a=8),
            in_=dap(W_in_ap, 128 * co, [[4096, 128], [128 * 4096, 8], [1, 128]])),
            dma=True, waits=(wl_prev[b],))
        is_u = co < NDI
        evs = []
        for tc in range(NTC):
            pb = tc % 2
            mms = []
            for ci in range(ND):
                mms.append(r.op("tensor", lambda e, b=b, pb=pb, ci=ci, tc=tc: e.matmul(
                    p_mm[pb][:], s_w[b][:, 128 * ci:128 * (ci + 1)],
                    s_xn2[:, T * ci + TC * tc: T * ci + TC * (tc + 1)],
                    start=(ci == 0), stop=(ci == ND - 1)),
                    mark=(ci == ND - 1),
                    waits=(wl, xl[-1], ev_prev[pb]) if ci == 0 else ()))
            g = pe_done(ctx, mms[-1])
            if is_u:
                ev = r.op("scalar", lambda e, pb=pb, tc=tc, co=co: e.activation(
                    s_u[:, KC - 1 + TC * tc: KC - 1 + TC * (tc + 1)], p_mm[pb][:],
                    AF.Identity, bias=bin_u[:, co:co + 1]),
                    mark=True, waits=(g, vdone[0], zpad))
            else:
                ev = r.op("scalar", lambda e, pb=pb, tc=tc, co=co: e.activation(
                    s_ob[:, TC * tc: TC * (tc + 1)], p_mm[pb][:],
                    AF.Silu, bias=bin_r[:, co - NDI:co - NDI + 1]),
                    mark=True, waits=(g, vdone[1], st_prev))
            ev_prev[pb] = ev
            evs.append(ev)
        if is_u:
            c0 = r.op("vector", lambda e, co=co: e.tensor_scalar(
                s_acc[:], s_u[:, 0:T], cw[0][:, co:co + 1], None, ALU.mult),
                mark=True, waits=(evs[-1], vdone[2], st_prev))
            for k in range(1, KC):
                c0 = r.op("vector", lambda e, k=k, co=co: e.scalar_tensor_tensor(
                    s_acc[:], s_u[:, k:T + k], cw[k][:, co:co + 1], s_acc[:],
                    ALU.mult, ALU.add), mark=True, waits=(c0,))
            sl = r.op("scalar", lambda e, co=co: e.activation(
                s_ob[:, 0:T], s_acc[:], AF.Silu, bias=cb[:, co:co + 1]),
                mark=True, waits=(c0,))
            st = r.op("sync", lambda e, co=co: e.dma_start(
                out=outs["uT_bf"][128 * co:128 * (co + 1), :], in_=s_ob[:, 0:T]),
                dma=True, waits=(sl,))
            u_stores[co] = st
        else:
            st = r.op("sync", lambda e, co=co: e.dma_start(
                out=outs["resT_bf"][128 * (co - NDI):128 * (co - NDI + 1), :],
                in_=s_ob[:, 0:T]), dma=True, waits=(evs[-1],))
        st_prev = st
        wl_prev[b] = st if not is_u else st
    s_wx, wxl = wcast_load(ctx, pre + "wx", W_x_ap, DI, 0, 96, 0, None, waits=())
    s_wdt, wdtl = wcast_load(ctx, pre + "wdt", W_dt_ap, 64, 0, DI, 0, None, waits=())
    s_ut = [ctx.sb(pre + f"ut{k}", [128, T], BF16) for k in range(2)]
    s_xd = ctx.sb(pre + "xd", [96, T], F32)
    s_xdb = ctx.sb(pre + "xdb", [64, T], BF16)
    p_xd = ctx.ps(pre + "pxd", [96, 512], F32)
    ul_prev = [None, None]
    ev = None
    for tc in range(NTC):
        mms = []
        for di in range(NDI):
            b = di % 2
            ul = r.op("sync", lambda e, di=di, tc=tc, b=b: e.dma_start(
                out=s_ut[b][:, 0:512], in_=outs["uT_bf"][128 * di:128 * (di + 1), TC * tc:TC * (tc + 1)]),
                dma=True, waits=(ul_prev[b], u_stores[di]))
            mms.append(r.op("tensor", lambda e, b=b, di=di, tc=tc: e.matmul(
                p_xd[:], s_wx[:, 96 * di:96 * (di + 1)], s_ut[b][:, 0:512],
                start=(di == 0), stop=(di == NDI - 1)),
                mark=True, waits=(ul, wxl) + ((ev,) if di == 0 else ())))
            ul_prev[b] = mms[-1]
        g = pe_done(ctx, mms[-1])
        ev = r.op("scalar", lambda e, tc=tc: e.copy(
            s_xd[:, TC * tc:TC * (tc + 1)], p_xd[:]), mark=True, waits=(g,))
    bc = r.op("sync", lambda e: e.dma_start(
        out=outs["bc_f32"], in_=s_xd[64:96, :]), dma=True, waits=(ev,))
    dtb = r.op("vector", lambda e: e.tensor_copy(s_xdb[:], s_xd[0:64, :]), mark=True, waits=(ev,))
    p_dt = [ctx.ps(pre + f"pdt{k}", [128, 512], F32) for k in range(2)]
    s_z = ctx.sb(pre + "z", [128, 512], F32)
    s_a = ctx.sb(pre + "a", [128, 512], F32)
    s_db = ctx.sb(pre + "db", [128, 512], BF16)
    ev2 = [None, None]
    x5 = None
    dones = []
    for di in range(NDI):
        for tc in range(NTC):
            pb = (di * NTC + tc) % 2
            mm = r.op("tensor", lambda e, di=di, tc=tc, pb=pb: e.matmul(
                p_dt[pb][:], s_wdt[:, 128 * di:128 * (di + 1)],
                s_xdb[:, TC * tc:TC * (tc + 1)], start=True, stop=True),
                mark=True, waits=(dtb, wdtl, ev2[pb]))
            g = pe_done(ctx, mm)
            z = r.op("scalar", lambda e, pb=pb, di=di: e.activation(
                s_z[:], p_dt[pb][:], AF.Identity, bias=bdt[:, di:di + 1]),
                mark=True, waits=(g, vdone[3], x5, dones[-1] if dones else None))
            ev2[pb] = z
            a = r.op("scalar", lambda e: e.activation(s_a[:], s_z[:], AF.Abs), mark=True, waits=(z,))
            x1 = r.op("scalar", lambda e: e.activation(s_a[:], s_a[:], AF.Exp, scale=-1.0), mark=True, waits=(a,))
            x2 = r.op("scalar", lambda e: e.activation(s_a[:], s_a[:], AF.Ln, bias=1.0), mark=True, waits=(x1,))
            x3 = r.op("vector", lambda e: e.tensor_single_scalar(s_z[:], s_z[:], 0.0, ALU.max), mark=True, waits=(x2,))
            x4 = r.op("vector", lambda e: e.tensor_add(s_z[:], s_z[:], s_a[:]), mark=True, waits=(x3,))
            x5 = r.op("vector", lambda e: e.tensor_copy(s_db[:], s_z[:]), mark=True, waits=(x4,))
            dones.append(r.op("sync", lambda e, di=di, tc=tc: e.dma_start(
                out=outs["deltaT_bf"][128 * di:128 * (di + 1), TC * tc:TC * (tc + 1)],
                in_=s_db[:]), dma=True, waits=(x5,)))
    return dones[-1:] + [bc]


def a_colmaj_host(A_log):
    A = np.asarray(A_log).reshape(16, 16, 8, NS)
    return np.ascontiguousarray(A.transpose(2, 3, 0, 1).reshape(128, 256))


def phase_scan(ctx, pre, deltaT_ap, uT_ap, resT_ap, bc_ap, A_colmaj_ap, D_ap,
               outs, after=()):
    r = ctx.rec
    TC = 512
    NTC = T // TC

    rep_pat_ap = ctx.din("c_rep_pat", [128, 2048], BF16)
    red_pat_ap = ctx.din("c_red_pat", [128, 2048], BF16)
    bc_pat_ap = ctx.din("c_bc_pat", [16, 128], BF16)
    s_rep = ctx.sb(pre + "reppat", [128, 2048], BF16)
    s_red = ctx.sb(pre + "redpat", [128, 2048], BF16)
    s_bcp = ctx.sb(pre + "bcpat", [16, 128], BF16)
    lds = [r.op("sync", lambda e: e.dma_start(out=s_rep[:], in_=rep_pat_ap), dma=True, waits=tuple(after)),
           r.op("sync", lambda e: e.dma_start(out=s_red[:], in_=red_pat_ap), dma=True),
           r.op("sync", lambda e: e.dma_start(out=s_bcp[:], in_=bc_pat_ap), dma=True)]
    s_acol = ctx.sb(pre + "acol", [128, 256], F32)
    p_sc = [ctx.ps(pre + f"psc{k}", [128, 512], F32) for k in range(3)]
    la = r.op("sync", lambda e: e.dma_start(out=s_acol[:], in_=A_colmaj_ap), dma=True)
    ea = r.op("scalar", lambda e: e.activation(s_acol[:], s_acol[:], AF.Exp), mark=True, waits=(la,))
    na = r.op("vector", lambda e: e.tensor_scalar_mul(s_acol[:], s_acol[:], -1.0), mark=True, waits=(ea,))
    dcol, dd = vec2cols(ctx, pre + "dcol", D_ap, DI, psname=pre + "psc2")

    s_bf = ctx.sb(pre + "bf", [16, T], F32)
    s_cf = ctx.sb(pre + "cf", [16, T], F32)
    s_bb = ctx.sb(pre + "bb", [16, T], BF16)
    s_cb = ctx.sb(pre + "cb2", [16, T], BF16)
    s_brep = ctx.sb(pre + "brep", [128, T], F32)
    s_crep = ctx.sb(pre + "crep", [128, T], F32)
    p_bc = p_sc[0]
    lb_ = r.op("sync", lambda e: e.dma_start(out=s_bf[:], in_=bc_ap[0:16, :]), dma=True)
    lc_ = r.op("sync", lambda e: e.dma_start(out=s_cf[:], in_=bc_ap[16:32, :]), dma=True)
    cb1 = r.op("vector", lambda e: e.tensor_copy(s_bb[:], s_bf[:]), mark=True, waits=(lb_,))
    cb2 = r.op("vector", lambda e: e.tensor_copy(s_cb[:], s_cf[:]), mark=True, waits=(lc_,))
    ev = None
    for (srcb, dst, cbl) in ((s_bb, s_brep, cb1), (s_cb, s_crep, cb2)):
        for tc in range(NTC):
            mm = r.op("tensor", lambda e, srcb=srcb, tc=tc: e.matmul(
                p_bc[:], s_bcp[:], srcb[:, TC * tc:TC * (tc + 1)],
                start=True, stop=True), mark=True, waits=(cbl, lds[2], ev))
            g = pe_done(ctx, mm)
            ev = r.op("scalar", lambda e, dst=dst, tc=tc: e.copy(
                dst[:, TC * tc:TC * (tc + 1)], p_bc[:]), mark=True, waits=(g,))

    s_dl = [ctx.sb(pre + f"dl{k}", [128, T], BF16) for k in range(2)]
    s_ul = [ctx.sb(pre + f"ul{k}", [128, T], BF16) for k in range(2)]
    s_rl = [ctx.sb(pre + f"rl{k}", [128, T], BF16) for k in range(2)]
    s_du = [ctx.sb(pre + f"du{k}", [128, T], BF16) for k in range(2)]
    s_dA = [ctx.sb(pre + f"dA{k}", [128, T], F32) for k in range(2)]
    s_x = [ctx.sb(pre + f"x{k}", [128, T], BF16) for k in range(2)]
    s_h = [ctx.sb(pre + f"h{k}", [128, T], F32) for k in range(2)]
    s_hc = [ctx.sb(pre + f"hc{k}", [128, T], BF16) for k in range(2)]
    s_ep = ctx.sb(pre + "ep", [128, T], F32)
    s_yb = ctx.sb(pre + "yb", [128, T], BF16)
    p_y = [ctx.ps(pre + f"py{k}", [128, 512], F32) for k in range(4)]

    ld_prev = [None, None]
    scan_prev = None
    hc_prev = [None, None]
    yev_prev = None
    st_prev = None
    scan_hist = []
    ymm3_hist = []
    pend_ymm = None
    rep_mms = []
    rep_evs = {-1: ev}
    pend_rev = None
    dones = []
    for c in range(NDI):
        b = c % 2
        ldd = r.op("sync", lambda e, c=c, b=b: e.dma_start(
            out=s_dl[b][:], in_=deltaT_ap[128 * c:128 * (c + 1), :]), dma=True,
            waits=(ld_prev[b],))
        ldu = r.op("sync", lambda e, c=c, b=b: e.dma_start(
            out=s_ul[b][:], in_=uT_ap[128 * c:128 * (c + 1), :]), dma=True)
        ldr = r.op("sync", lambda e, c=c, b=b: e.dma_start(
            out=s_rl[b][:], in_=resT_ap[128 * c:128 * (c + 1), :]), dma=True)
        du = r.op("gpsimd", lambda e, b=b: e.tensor_tensor(
            s_du[b][:], s_dl[b][:], s_ul[b][:], ALU.mult), mark=True,
            waits=(ldd, ldu))
        for j in range(16):
            col = c * 16 + j
            jb = j % 2
            scan_g2 = scan_hist[-2] if len(scan_hist) >= 2 else None
            j_evs = []

            def mk_rev(i, n, col=col, jb=jb, scan_g2=scan_g2):
                tc = i % NTC
                bk = n % 3

                def emit(guard):
                    if i < NTC:
                        ev = r.op("scalar", lambda e: e.activation(
                            s_dA[jb][:, TC * tc:TC * (tc + 1)], p_sc[bk][:], AF.Exp,
                            scale=s_acol[:, col:col + 1]),
                            mark=True, waits=(guard, na, scan_g2))
                    else:
                        ev = r.op("vector", lambda e: e.scalar_tensor_tensor(
                            s_x[jb][:, TC * tc:TC * (tc + 1)], p_sc[bk][:], 1.0,
                            s_brep[:, TC * tc:TC * (tc + 1)], ALU.mult, ALU.mult),
                            mark=True, waits=(guard, scan_g2))
                    rep_evs[n] = ev
                    j_evs.append(ev)
                return emit

            for i in range(8):
                n = len(rep_mms)
                tc = i % NTC
                src = s_dl if i < NTC else s_du
                mm = r.op("tensor", lambda e, b=b, j=j, tc=tc, src=src, bk=n % 3: e.matmul(
                    p_sc[bk][:], s_rep[:, 128 * j:128 * (j + 1)],
                    src[b][:, TC * tc:TC * (tc + 1)], start=True, stop=True),
                    mark=True,
                    waits=(lds[0], ldd if i < NTC else du,
                           rep_evs.get(n - 3, rep_evs[-1] if n < 3 else None),
                           dd if n < 3 else None))
                rep_mms.append(mm)
                if pend_rev is not None:
                    pend_rev(mm)
                pend_rev = mk_rev(i, n)
            pend_rev(pe_done(ctx, rep_mms[-1]))
            pend_rev = None
            dA_last = j_evs[3]
            x_last = j_evs[7]
            scan = r.op("vector", lambda e, jb=jb: e.tensor_tensor_scan(
                s_h[jb][:], s_dA[jb][:], s_x[jb][:], 0.0, ALU.mult, ALU.add),
                mark=True, waits=(dA_last, x_last, hc_prev[jb]))
            scan_hist.append(scan)
            scan_prev = scan
            hc = r.op("gpsimd", lambda e, jb=jb: e.tensor_tensor(
                s_hc[jb][:], s_h[jb][:], s_crep[:], ALU.mult), mark=True,
                waits=(scan, yev_prev if j == 0 else None,
                       ymm3_hist[-1] if ymm3_hist else None))
            hc_prev[jb] = hc
            def emit_ymm(hcl, j_, jb_):
                ymm = []
                for tc in range(NTC):
                    ymm.append(r.op("tensor", lambda e, j_=j_, tc=tc, jb_=jb_: e.matmul(
                        p_y[tc][:], s_red[:, 128 * j_:128 * (j_ + 1)],
                        s_hc[jb_][:, TC * tc:TC * (tc + 1)],
                        start=(j_ == 0), stop=(j_ == 15)),
                        mark=(tc == NTC - 1 or j_ == 15), waits=(hcl, lds[1])))
                ymm3_hist.append(ymm[-1])
                return ymm
            if pend_ymm is not None:
                emit_ymm(*pend_ymm)
            pend_ymm = (hc, j, jb)
        ymm = emit_ymm(*pend_ymm)
        pend_ymm = None
        yevs = []
        for tc in range(NTC):
            g = pe_done(ctx, ymm[tc]) if tc == NTC - 1 else ymm[tc + 1]
            e1 = r.op("vector", lambda e, b=b, c=c, tc=tc: e.scalar_tensor_tensor(
                s_ep[:, TC * tc:TC * (tc + 1)], s_ul[b][:, TC * tc:TC * (tc + 1)],
                dcol[:, c:c + 1], p_y[tc][:], ALU.mult, ALU.add),
                mark=True, waits=(g, dd, yev_prev if tc == 0 else None))
            e2 = r.op("gpsimd", lambda e, b=b, tc=tc: e.tensor_tensor(
                s_yb[:, TC * tc:TC * (tc + 1)], s_ep[:, TC * tc:TC * (tc + 1)],
                s_rl[b][:, TC * tc:TC * (tc + 1)], ALU.mult),
                mark=True, waits=(e1, ldr, st_prev))
            yevs.append(e2)
        yev_prev = yevs[-1]
        st = r.op("sync", lambda e, c=c: e.dma_start(
            out=outs["ybT_bf"][128 * c:128 * (c + 1), :], in_=s_yb[:]),
            dma=True, waits=tuple(yevs))
        st_prev = st
        ld_prev[b] = st
        dones.append(st)
    return dones[-1:]


def phase_matmul_tok(ctx, pre, actT_ap, K, w_ap, N_out, out_ap, out_dt,
                     bias_ap=None, inject_ap=None, act=None, after=(),
                     w_pre_bf16=False):
    r = ctx.rec
    NK = K // 128
    NN = N_out // 512
    s_id = ctx.sb("s_ident")
    s_act = ctx.sb(pre + "a", [128, NK * T], BF16)
    s_w = [ctx.sb(pre + f"w{k}", [128, NK * 512], BF16) for k in range(2)]
    s_inj = [ctx.sb(pre + f"inj{k}", [128, 512], BF16) for k in range(2)] \
        if inject_ap is not None else None
    s_o = [ctx.sb(pre + f"o{k}", [128, 512], out_dt) for k in range(2)]
    p_mm = [ctx.ps(pre + f"p{k}", [128, 512], F32) for k in range(2)]
    bias_row = None
    if bias_ap is not None:
        s_br = ctx.sb(pre + "br", [1, N_out], BF16)
        blb = r.op("gpsimd", lambda e: e.dma_start(out=s_br[:], in_=bias_ap[None, :]),
                   dma=True, waits=tuple(after))
        bias_row = s_br
    al = None
    for k in range(NK):
        al = r.op("sync", lambda e, k=k: e.dma_start(
            out=s_act[:, T * k:T * (k + 1)], in_=actT_ap[128 * k:128 * (k + 1), :]),
            dma=True, waits=tuple(after) if k == 0 else ())
    wrow = N_out
    inj_prev = [None, None]
    wl_prev = [None, None]
    ev_prev = [None, None]
    st_prev = [None, None]
    dones = []
    for n in range(NN):
        wb = n % 2
        wl = r.op("gpsimd", lambda e, n=n, wb=wb: e.dma_start(
            out=s_w[wb][:].rearrange("p (a c) -> p a c", a=NK),
            in_=dap(w_ap, 512 * n, [[wrow, 128], [128 * wrow, NK], [1, 512]])),
            dma=True, waits=(wl_prev[wb],) + tuple(after))
        for i in range(NT):
            pb = (n * NT + i) % 2
            last_plain = bias_ap is None and inject_ap is None
            mms = []
            for k in range(NK):
                mms.append(r.op("tensor", lambda e, k=k, wb=wb, pb=pb, i=i,
                                lp=last_plain: e.matmul(
                    p_mm[pb][:], s_act[:, T * k + 128 * i:T * k + 128 * (i + 1)],
                    s_w[wb][:, 512 * k:512 * (k + 1)],
                    start=(k == 0), stop=(lp and k == NK - 1)),
                    mark=False, waits=(wl, al, ev_prev[pb]) if k == 0 else ()))
            if bias_ap is not None:
                r.op("tensor", lambda e, pb=pb, n=n: e.matmul(
                    p_mm[pb][:], ctx.sb("s_onesbf")[0:1, 0:128],
                    bias_row[:, 512 * n:512 * (n + 1)], start=False,
                    stop=inject_ap is None), mark=False, waits=(blb,))
            if inject_ap is not None:
                il = r.op("sync", lambda e, i=i, n=n, pb=pb: e.dma_start(
                    out=s_inj[pb][:], in_=inject_ap[128 * i:128 * (i + 1), 512 * n:512 * (n + 1)]),
                    dma=True, waits=(inj_prev[pb],))
                inj_prev[pb] = r.op("tensor", lambda e, pb=pb: e.matmul(
                    p_mm[pb][:], s_id[:], s_inj[pb][:], start=False, stop=True),
                    mark=True, waits=(il,))
            g = pe_done(ctx)
            ev = r.op("scalar", lambda e, pb=pb: e.activation(
                s_o[pb][:], p_mm[pb][:], act if act is not None else AF.Identity),
                mark=True, waits=(g, st_prev[pb]))
            ev_prev[pb] = ev
            st_prev[pb] = r.op("sync", lambda e, i=i, n=n, pb=pb: e.dma_start(
                out=out_ap[128 * i:128 * (i + 1), 512 * n:512 * (n + 1)],
                in_=s_o[pb][:]), dma=True, waits=(ev,))
            dones.append(st_prev[pb])
        wl_prev[wb] = dones[-1]
    return dones[-1:]


def phase_gate(ctx, pre, xnT_ap, gate_wT_ap, gate_b_ap, combine_ap,
               combineT_ap, after=()):
    r = ctx.rec
    E, G = 16, 4
    s_gw = ctx.sb(pre + "gw", [128, ND * E], BF16)
    gl = r.op("gpsimd", lambda e: e.dma_start(
        out=s_gw[:].rearrange("p (a c) -> p a c", a=ND),
        in_=dap(gate_wT_ap, 0, [[E, 128], [128 * E, ND], [1, E]])),
        dma=True, waits=tuple(after))
    s_gb = ctx.sb(pre + "gb", [1, E], F32)
    gbl = r.op("sync", lambda e: e.dma_start(out=s_gb[:], in_=gate_b_ap[None, :]),
               dma=True, waits=tuple(after))
    gbr, gbd = bcast_row(ctx, pre + "gb", s_gb, E, F32, gbl)
    s_a = ctx.sb(pre + "act", [128, 128], BF16)
    s_sc = ctx.sb(pre + "sc", [128, E], F32)
    s_st = ctx.sb(pre + "sst", [128, 8], F32)
    s_orig = ctx.sb(pre + "orig", [128, E], F32)
    s_sb = ctx.sb(pre + "ssb", [128, E], F32)
    s_g2 = ctx.sb(pre + "g2", [128, G], F32)
    s_pr = ctx.sb(pre + "pr", [128, 6], F32)
    s_kg = ctx.sb(pre + "kg", [128, E], F32)
    s_cmb = ctx.sb(pre + "cmb", [128, E], F32)
    s_cmbb = ctx.sb(pre + "cmbb", [128, E], BF16)
    s_ct = ctx.sb(pre + "ct", [16, 128], BF16)
    p_g = ctx.ps(pre + "pg", [128, E], F32)
    p_t = ctx.ps(pre + "pt", [16, 128], BF16)
    s_id = ctx.sb("s_ident")
    PAIRS = [(0, 1), (0, 2), (0, 3), (1, 2), (1, 3), (2, 3)]
    prev = None
    dones = []
    for i in range(NT):
        em = chain_maker(r, gl, gbd, prev, *(after if i == 0 else ()))
        em("sync", lambda e, i=i: e.dma_start(
            out=s_a[:], in_=xnT_ap.tensor.ap()[0:0 + 128, 0:128]), dma=True) \
            if False else None
        for c in range(ND):
            ldc = em("sync", lambda e, i=i, c=c: e.dma_start(
                out=s_a[:], in_=xnT_ap[128 * c:128 * (c + 1), 128 * i:128 * (i + 1)]),
                dma=True)
            em("tensor", lambda e, c=c: e.matmul(
                p_g[:], s_a[:], s_gw[:, E * c:E * (c + 1)],
                start=(c == 0), stop=(c == ND - 1)))
        em("tensor", lambda e: e.matmul(
            ctx.ps("p_guard")[:], ctx.sb("s_onesbf")[0:1, 0:1],
            ctx.sb("s_onesbf")[0:1, 0:16], start=True, stop=True))
        em("scalar", lambda e: e.activation(
            s_sc[:], p_g[:], AF.Exp, accum_out=s_st[:, 0:1]))
        em("vector", lambda e: e.reciprocal(s_st[:, 1:2], s_st[:, 0:1]))
        em("vector", lambda e: e.tensor_scalar(
            s_orig[:], s_sc[:], s_st[:, 1:2], None, ALU.mult))
        em("vector", lambda e: e.tensor_add(s_sb[:], s_orig[:], gbr[:, 0:E]))
        for k, (a, b) in enumerate(PAIRS):
            em("vector", lambda e, k=k, a=a, b=b: e.tensor_tensor(
                s_pr[:, k:k + 1].to_broadcast([128, 1]) if False else s_pr[:, k:k + 1],
                s_sb[:, a::4][:, 0:1] if False else s_sb[:, a:a + 13:4][:, 0:0] if False else s_sb[:, a + 0:a + 1],
                s_sb[:, b:b + 1], ALU.add)) if False else None
        for g in range(G):
            for k, (a, b) in enumerate(PAIRS):
                em("vector", lambda e, g=g, k=k, a=a, b=b: e.tensor_tensor(
                    s_pr[:, k:k + 1], s_sb[:, 4 * g + a:4 * g + a + 1],
                    s_sb[:, 4 * g + b:4 * g + b + 1], ALU.add))
            em("vector", lambda e: e.reduce_max(s_st[:, 2:3], s_pr[:, 0:6], AX.X))
            em("vector", lambda e, g=g: e.tensor_copy(s_g2[:, g:g + 1], s_st[:, 2:3]))
        em("vector", lambda e: e.tensor_max(s_st[:, 3:4], s_g2[:, 0:1], s_g2[:, 1:2]))
        em("vector", lambda e: e.tensor_tensor(s_st[:, 4:5], s_g2[:, 0:1], s_g2[:, 1:2], ALU.min))
        em("vector", lambda e: e.tensor_max(s_st[:, 5:6], s_g2[:, 2:3], s_g2[:, 3:4]))
        em("vector", lambda e: e.tensor_tensor(s_st[:, 6:7], s_g2[:, 2:3], s_g2[:, 3:4], ALU.min))
        em("vector", lambda e: e.tensor_tensor(s_st[:, 7:8], s_st[:, 3:4], s_st[:, 5:6], ALU.min))
        em("vector", lambda e: e.tensor_max(s_st[:, 4:5], s_st[:, 4:5], s_st[:, 6:7]))
        em("vector", lambda e: e.tensor_max(s_st[:, 7:8], s_st[:, 7:8], s_st[:, 4:5]))
        em("vector", lambda e: e.tensor_scalar(
            s_g2[:], s_g2[:], s_st[:, 7:8], None, ALU.is_ge))
        for j in range(4):
            em("vector", lambda e, j=j: e.tensor_copy(
                AP(s_kg, j, [[E, 128], [4, 4]]), s_g2[:]))
        em("vector", lambda e: e.tensor_tensor(s_sb[:], s_sb[:], s_kg[:], ALU.mult))
        em("vector", lambda e: e.tensor_scalar(s_kg[:], s_kg[:], 1e9, -1e9, ALU.mult, ALU.add))
        em("vector", lambda e: e.tensor_tensor(s_sb[:], s_sb[:], s_kg[:], ALU.add))
        em("vector", lambda e: e.reduce_max(s_st[:, 2:3], s_sb[:], AX.X))
        em("vector", lambda e: e.tensor_scalar(
            s_kg[:], s_sb[:], s_st[:, 2:3], None, ALU.is_ge))
        em("vector", lambda e: e.scalar_tensor_tensor(
            s_kg[:], s_kg[:], -2e9, s_sb[:], ALU.mult, ALU.add))
        em("vector", lambda e: e.reduce_max(s_st[:, 3:4], s_kg[:], AX.X))
        em("vector", lambda e: e.tensor_scalar(
            s_kg[:], s_sb[:], s_st[:, 3:4], None, ALU.is_ge))
        em("vector", lambda e: e.tensor_tensor(s_cmb[:], s_orig[:], s_kg[:], ALU.mult))
        em("vector", lambda e: e.tensor_copy(s_cmbb[:], s_cmb[:]))
        st1 = em("sync", lambda e, i=i: e.dma_start(
            out=combine_ap[128 * i:128 * (i + 1), :], in_=s_cmb[:]), dma=True)
        em("tensor", lambda e: e.transpose(p_t[:], s_cmbb[:], s_id[:]))
        em("tensor", lambda e: e.matmul(
            ctx.ps("p_guard")[:], ctx.sb("s_onesbf")[0:1, 0:1],
            ctx.sb("s_onesbf")[0:1, 0:16], start=True, stop=True))
        em("scalar", lambda e: e.copy(s_ct[:], p_t[:]))
        prev = em("sync", lambda e, i=i: e.dma_start(
            out=combineT_ap[:, 128 * i:128 * (i + 1)], in_=s_ct[:]), dma=True)
        dones.append(prev)
    return dones[-1:]


def phase_moe_h(ctx, pre, xnT_ap, We1_ap, be1_ap, We3_ap, be3_ap,
                ws1_ap, bs1_ap, ws3_ap, bs3_ap, combineT_ap, hmoe_ap, after=()):
    r = ctx.rec
    E, IN = 16, 512
    TC = 512
    s_xn = ctx.sb(pre + "xn", [128, ND * T], BF16)
    xl = []
    for c in range(ND):
        xl.append(r.op("sync", lambda e, c=c: e.dma_start(
            out=s_xn[:, T * c:T * (c + 1)], in_=xnT_ap[128 * c:128 * (c + 1), :]),
            dma=True, waits=tuple(after)))
    s_w1 = [ctx.sb(pre + f"w1{k}", [128, ND * 1024], BF16) for k in range(2)]
    s_w3 = [ctx.sb(pre + f"w3{k}", [128, ND * 1024], BF16) for k in range(2)]
    s_cr = ctx.sb(pre + "cr", [128, T], F32)
    s_crow = ctx.sb(pre + "crow", [1, T], BF16)
    s_h1 = ctx.sb(pre + "h1", [128, TC], F32)
    s_hb = ctx.sb(pre + "hb", [128, TC], BF16)
    p_h = [ctx.ps(pre + f"ph{k}", [128, TC], F32) for k in range(2)]
    p_c = ctx.ps(pre + "pc", [128, TC], F32)

    wl_prev = [None, None]
    ev_prev = [None, None]
    crep_prev = None
    st_prev = None
    hb_prev = None
    dones = []
    for e_i in range(E + 1):
        shared = e_i == E
        wb = e_i % 2
        ninner = 2 * IN // 128 if shared else IN // 128
        w1ap, w3ap, b1ap, b3ap = ((ws1_ap, ws3_ap, bs1_ap, bs3_ap) if shared
                                  else (We1_ap, We3_ap, be1_ap, be3_ap))
        ncols = 1024 if shared else IN
        off = 0 if shared else e_i * D * IN
        wrow = 1024 if shared else IN
        wl1 = r.op("gpsimd", lambda e, wb=wb, w1ap=w1ap, off=off, wrow=wrow, ncols=ncols: e.dma_start(
            out=s_w1[wb][:, 0:ND * ncols].rearrange("p (a c) -> p a c", a=ND),
            in_=dap(w1ap, off, [[wrow, 128], [128 * wrow, ND], [1, ncols]])),
            dma=True, waits=(wl_prev[wb],))
        wl3 = r.op("gpsimd", lambda e, wb=wb, w3ap=w3ap, off=off, wrow=wrow, ncols=ncols: e.dma_start(
            out=s_w3[wb][:, 0:ND * ncols].rearrange("p (a c) -> p a c", a=ND),
            in_=dap(w3ap, off, [[wrow, 128], [128 * wrow, ND], [1, ncols]])),
            dma=True)
        b1v = b1ap if shared else b1ap.flatten()[e_i * IN:(e_i + 1) * IN]
        b3v = b3ap if shared else b3ap.flatten()[e_i * IN:(e_i + 1) * IN]
        nb1 = pre + ("bS1" if shared else f"b1_{e_i % 2}")
        nb3 = pre + ("bS3" if shared else f"b3_{e_i % 2}")
        b1c, b1d = vec2cols(ctx, nb1, b1v, 1024 if shared else IN, (st_prev,), psname=pre + "pv")
        b3c, b3d = vec2cols(ctx, nb3, b3v, 1024 if shared else IN, (st_prev,), psname=pre + "pv")
        if not shared:
            crl = r.op("sync", lambda e, e_i=e_i: e.dma_start(
                out=s_crow[:], in_=combineT_ap[e_i:e_i + 1, :]), dma=True,
                waits=(crep_prev,))
            crd = None
            for tc2 in range(4):
                cmm = r.op("tensor", lambda e, tc2=tc2: e.matmul(
                    p_c[:], ctx.sb("s_onesbf")[0:1, 0:128],
                    s_crow[:, TC * tc2:TC * (tc2 + 1)], start=True, stop=True),
                    mark=True, waits=(crl, crd))
                g = pe_done(ctx, cmm)
                crd = r.op("scalar", lambda e, tc2=tc2: e.copy(
                    s_cr[:, TC * tc2:TC * (tc2 + 1)], p_c[:]), mark=True, waits=(g,))
            crep = crd
        for ic in range(ninner):
            for tc in range(4):
                pb = (ic * 4 + tc) % 2
                mm1 = None
                for c in range(ND):
                    mm1 = r.op("tensor", lambda e, wb=wb, c=c, ic=ic, tc=tc, pb=pb, ncols=ncols: e.matmul(
                        p_h[pb][:], s_w1[wb][:, ncols * c + 128 * ic:ncols * c + 128 * (ic + 1)],
                        s_xn[:, T * c + TC * tc:T * c + TC * (tc + 1)],
                        start=(c == 0), stop=(c == ND - 1)),
                        mark=(c == ND - 1),
                        waits=(wl1, xl[-1], ev_prev[pb]) if c == 0 else ())
                g = pe_done(ctx, mm1)
                ev1 = r.op("scalar", lambda e, pb=pb, ic=ic, b1c=b1c: e.activation(
                    s_h1[:], p_h[pb][:], AF.Silu, bias=b1c[:, ic:ic + 1]),
                    mark=True, waits=(g, b1d, hb_prev))
                mm3 = None
                for c in range(ND):
                    mm3 = r.op("tensor", lambda e, wb=wb, c=c, ic=ic, tc=tc, pb=pb, ncols=ncols: e.matmul(
                        p_h[pb][:], s_w3[wb][:, ncols * c + 128 * ic:ncols * c + 128 * (ic + 1)],
                        s_xn[:, T * c + TC * tc:T * c + TC * (tc + 1)],
                        start=(c == 0), stop=(c == ND - 1)),
                        mark=(c == ND - 1),
                        waits=(wl3, ev1) if c == 0 else ())
                g3 = pe_done(ctx, mm3)
                s1 = r.op("vector", lambda e, pb=pb, ic=ic, b3c=b3c: e.scalar_tensor_tensor(
                    s_h1[:],
                    p_h[pb][:], b3c[:, ic:ic + 1], s_h1[:], ALU.add, ALU.mult),
                    mark=True, waits=(g3, b3d))
                if not shared:
                    s2 = r.op("vector", lambda e, tc=tc: e.tensor_tensor(
                        s_hb[:], s_h1[:], s_cr[:, TC * tc:TC * (tc + 1)], ALU.mult),
                        mark=True, waits=(s1, crep, st_prev))
                else:
                    s2 = r.op("vector", lambda e: e.tensor_copy(s_hb[:], s_h1[:]),
                              mark=True, waits=(s1, st_prev))
                hb_prev = s2
                row0 = (8192 if shared else e_i * IN) + 128 * ic
                st_prev = r.op("sync", lambda e, row0=row0, tc=tc: e.dma_start(
                    out=hmoe_ap[row0:row0 + 128, TC * tc:TC * (tc + 1)],
                    in_=s_hb[:]), dma=True, waits=(s2,))
                ev_prev[pb] = s1
                dones.append(st_prev)
        if not shared:
            crep_prev = s2
        wl_prev[wb] = st_prev
    return dones[-1:]


def phase_moe_out(ctx, pre, hmoe_ap, w2stack_ap, combineT_ap, be2_ap, bs2_ap,
                  x_ap, s0_ap, x1_ap, g0_dbg_ap=None, after=()):
    r = ctx.rec
    KTOT = 9216
    NKG = 18
    s_id = ctx.sb("s_ident")
    s_h = [ctx.sb(pre + f"h{k}", [128, 4 * 512], BF16) for k in range(2)]
    s_w = [ctx.sb(pre + f"w{k}", [128, 4 * 512], BF16) for k in range(2)]
    s_ct = ctx.sb(pre + "ct", [16, T], BF16)
    s_be2 = ctx.sb(pre + "be2", [16, D], BF16)
    s_bs2 = ctx.sb(pre + "bs2", [1, D], BF16)
    s_g = ctx.sb(pre + "g", [128, 512], BF16)
    s_x = ctx.sb(pre + "x", [128, 512], F32)
    s_s0 = ctx.sb(pre + "s0", [128, 512], BF16)
    s_sg = ctx.sb(pre + "sg", [128, 512], F32)
    p_y = [ctx.ps(pre + f"py{k}", [128, 512], F32) for k in range(4)]
    ctl = r.op("sync", lambda e: e.dma_start(out=s_ct[:], in_=combineT_ap), dma=True,
               waits=tuple(after))
    bel = r.op("gpsimd", lambda e: e.dma_start(out=s_be2[:], in_=be2_ap), dma=True,
               waits=tuple(after))
    bsl = r.op("gpsimd", lambda e: e.dma_start(out=s_bs2[:], in_=bs2_ap[None, :]), dma=True,
               waits=tuple(after))
    hl_prev = [None, None]
    ev_prev = [None] * 4
    st_prev = None
    dones = []
    for cpass in range(2):
        for sup in range(4):
            mml = [None] * 4
            for kg in range(NKG):
                b = kg % 2
                hl = r.op("sync", lambda e, kg=kg, sup=sup, b=b: e.dma_start(
                    out=s_h[b][:].rearrange("p (a c) -> p a c", a=4),
                    in_=dap(hmoe_ap, 512 * kg * T + 512 * sup, [[T, 128], [128 * T, 4], [1, 512]])),
                    dma=True, waits=(hl_prev[b],))
                wl = r.op("gpsimd", lambda e, kg=kg, cpass=cpass, b=b: e.dma_start(
                    out=s_w[b][:].rearrange("p (a c) -> p a c", a=4),
                    in_=dap(w2stack_ap, 512 * kg * D + 512 * cpass, [[D, 128], [128 * D, 4], [1, 512]])),
                    dma=True, waits=(hl_prev[b],))
                for kk in range(4):
                    for tt in range(4):
                        mml[tt] = r.op("tensor", lambda e, b=b, kk=kk, tt=tt, kg=kg: e.matmul(
                            p_y[tt][:], s_h[b][:, 512 * kk + 128 * tt:512 * kk + 128 * (tt + 1)],
                            s_w[b][:, 512 * kk:512 * (kk + 1)],
                            start=(kg == 0 and kk == 0), stop=False),
                            mark=(kk == 3),
                            waits=((hl, wl, ev_prev[tt]) if kk == 0 and kg == 0
                                   else (hl, wl) if kk == 0 else ()))
                hl_prev[b] = mml[3]
            for tt in range(4):
                i = sup * 4 + tt
                r.op("tensor", lambda e, i=i, tt=tt, cpass=cpass: e.matmul(
                    p_y[tt][:], s_ct[:, 128 * i:128 * (i + 1)],
                    s_be2[:, 512 * cpass:512 * (cpass + 1)], start=False, stop=False),
                    mark=False, waits=(ctl, bel))
                mmb = r.op("tensor", lambda e, tt=tt, cpass=cpass: e.matmul(
                    p_y[tt][:], ctx.sb("s_onesbf")[0:1, 0:128],
                    s_bs2[:, 512 * cpass:512 * (cpass + 1)], start=False, stop=True),
                    mark=True, waits=(bsl,))
                g = pe_done(ctx, mmb)
                ev = r.op("scalar", lambda e, tt=tt: e.activation(
                    s_g[:], p_y[tt][:], AF.Silu), mark=True, waits=(g, st_prev))
                ev_prev[tt] = ev
                lx = r.op("sync", lambda e, i=i, cpass=cpass: e.dma_start(
                    out=s_x[:], in_=x_ap[128 * i:128 * (i + 1), 512 * cpass:512 * (cpass + 1)]),
                    dma=True, waits=(st_prev,))
                ls = r.op("sync", lambda e, i=i, cpass=cpass: e.dma_start(
                    out=s_s0[:], in_=s0_ap[128 * i:128 * (i + 1), 512 * cpass:512 * (cpass + 1)]),
                    dma=True)
                if g0_dbg_ap is not None:
                    r.op("sync", lambda e, i=i, cpass=cpass: e.dma_start(
                        out=g0_dbg_ap[128 * i:128 * (i + 1), 512 * cpass:512 * (cpass + 1)],
                        in_=s_g[:]), dma=True, waits=(ev,))
                m1 = r.op("vector", lambda e: e.tensor_tensor(
                    s_sg[:], s_s0[:], s_g[:], ALU.mult), mark=True, waits=(ev, ls, st_prev))
                m2 = r.op("vector", lambda e: e.tensor_add(
                    s_sg[:], s_sg[:], s_x[:]), mark=True, waits=(m1, lx))
                st_prev = r.op("sync", lambda e, i=i, cpass=cpass: e.dma_start(
                    out=x1_ap[128 * i:128 * (i + 1), 512 * cpass:512 * (cpass + 1)],
                    in_=s_sg[:]), dma=True, waits=(m2,))
                dones.append(st_prev)
    return dones[-1:]


def stage_w2stack(ctx, pre, We2_ap, ws2_ap, w2stack_ap, after=()):
    r = ctx.rec
    labs = []
    for e_i in range(16):
        labs.append(r.op("gpsimd", lambda e, e_i=e_i: e.dma_start(
            out=dap(w2stack_ap, e_i * 512 * D, [[D, 512], [1, D]]),
            in_=dap(We2_ap, e_i * 512 * D, [[D, 512], [1, D]])),
            dma=True, waits=tuple(after)))
    labs.append(r.op("gpsimd", lambda e: e.dma_start(
        out=dap(w2stack_ap, 8192 * D, [[D, 1024], [1, D]]),
        in_=ws2_ap), dma=True, waits=tuple(after)))
    return labs


def build_l0(ctx, dbg=False):
    nc = ctx.nc
    x = ctx.din("x", [T, D], F32)
    gamma = ctx.din("L0_attn_ln_gamma", [D], F32)
    beta = ctx.din("L0_attn_ln_beta", [D], F32)
    W_in = ctx.din("L0_attn_W_in", [D, 2 * DI], F32)
    b_in = ctx.din("L0_attn_b_in", [2 * DI], F32)
    conv_w = ctx.din("L0_attn_conv_w", [KC, DI], F32)
    conv_b = ctx.din("L0_attn_conv_b", [DI], F32)
    W_x = ctx.din("L0_attn_W_x", [DI, 96], F32)
    W_dt = ctx.din("L0_attn_W_dt", [DTR, DI], F32)
    b_dt = ctx.din("L0_attn_b_dt", [DI], F32)
    A_colmaj = ctx.din("A_colmaj", [128, 256], F32)
    D_ap = ctx.din("L0_attn_D", [DI], F32)
    W_out = ctx.din("L0_attn_W_out", [DI, D], F32)
    b_out = ctx.din("L0_attn_b_out", [D], F32)
    gate_wT = ctx.din("gate_wT", [D, 16], F32)
    gate_b = ctx.din("L0_ffn_gate_b", [16], F32)
    We1 = ctx.din("L0_ffn_We1", [16, D, 512], F32)
    be1 = ctx.din("L0_ffn_be1", [16, 512], F32)
    We2 = ctx.din("L0_ffn_We2", [16, 512, D], F32)
    be2 = ctx.din("L0_ffn_be2", [16, D], F32)
    We3 = ctx.din("L0_ffn_We3", [16, D, 512], F32)
    be3 = ctx.din("L0_ffn_be3", [16, 512], F32)
    ws1 = ctx.din("L0_ffn_ws1", [D, D], F32)
    bs1 = ctx.din("L0_ffn_bs1", [D], F32)
    ws2 = ctx.din("L0_ffn_ws2", [D, D], F32)
    bs2 = ctx.din("L0_ffn_bs2", [D], F32)
    ws3 = ctx.din("L0_ffn_ws3", [D, D], F32)
    bs3 = ctx.din("L0_ffn_bs3", [D], F32)

    dkind = "ExternalOutput" if dbg else "Internal"
    xn_bf = ctx.dram("xn_bf", [T, D], BF16, kind=dkind).ap()
    xnT_bf = ctx.dram("xnT_bf", [D, T], BF16, kind=dkind).ap()
    xn2T_bf = ctx.dram("xn2T_bf", [D, T], BF16, kind=dkind).ap()
    uT = ctx.dram("uT", [DI, T], BF16, kind=dkind).ap()
    resT = ctx.dram("resT", [DI, T], BF16, kind=dkind).ap()
    deltaT = ctx.dram("deltaT", [DI, T], BF16, kind=dkind).ap()
    bc = ctx.dram("bc", [32, T], F32, kind=dkind).ap()
    ybT = ctx.dram("ybT", [DI, T], BF16, kind=dkind).ap()
    s0 = ctx.dram("s0", [T, D], BF16, kind=dkind).ap()
    combine = ctx.dram("combine", [T, 16], F32, kind=dkind).ap()
    combineT = ctx.dram("combineT", [16, T], BF16, kind=dkind).ap()
    hmoe = ctx.dram("hmoe", [9216, T], BF16, kind="Internal").ap()
    w2stack = ctx.dram("w2stack", [9216, D], BF16, kind="Internal").ap()
    g0 = ctx.dram("g0", [T, D], BF16, kind=dkind).ap() if dbg else None
    x1 = ctx.dout("x1", [T, D], F32)

    cs = load_consts(ctx)
    w2l = stage_w2stack(ctx, "w2s_", We2.flatten().rearrange("(a b) -> a b", b=D),
                        ws2, w2stack, after=tuple(cs["ld"]))
    d_ln = phase_ln(ctx, "ln_", x,
                    dict(xn_bf=xn_bf, xnT_bf=xnT_bf, xn2T_bf=xn2T_bf),
                    affine=(gamma, beta), after=tuple(cs["ld"]))
    ctx.free("ln_")
    d_mf = phase_mamba_front(ctx, "mf_", xn2T_bf, W_in, b_in, conv_w, conv_b,
                             W_x, W_dt, b_dt,
                             dict(uT_bf=uT, resT_bf=resT, deltaT_bf=deltaT, bc_f32=bc),
                             after=tuple(d_ln))
    ctx.free("mf_")
    d_sc = phase_scan(ctx, "sc_", deltaT, uT, resT, bc, A_colmaj, D_ap,
                      dict(ybT_bf=ybT), after=tuple(d_mf))
    ctx.free("sc_")
    d_wo = phase_matmul_tok(ctx, "wo_", ybT, DI, W_out, D, s0, BF16,
                            bias_ap=b_out, inject_ap=xn_bf, after=tuple(d_sc))
    ctx.free("wo_")
    d_g = phase_gate(ctx, "gt_", xnT_bf, gate_wT, gate_b, combine, combineT,
                     after=tuple(d_wo))
    ctx.free("gt_")
    d_mh = phase_moe_h(ctx, "mh_", xnT_bf,
                       We1.flatten().rearrange("(a b) -> a b", b=512), be1,
                       We3.flatten().rearrange("(a b) -> a b", b=512), be3,
                       ws1, bs1, ws3, bs3, combineT, hmoe,
                       after=tuple(d_g) + tuple(d_ln))
    ctx.free("mh_")
    d_mo = phase_moe_out(ctx, "mo_", hmoe, w2stack, combineT, be2, bs2,
                         x, s0, x1, g0_dbg_ap=g0,
                         after=tuple(d_mh) + tuple(d_wo) + tuple(w2l))
    ctx.free("mo_")
    return d_mo


def l0_host_inputs(z, core):
    im = {k: np.asarray(z[k]) for k in z if k.startswith("L0_")}
    im["x"] = np.ascontiguousarray(np.asarray(z["L0_attn_x"])[core])
    del im["L0_attn_x"]
    del im["L0_attn_A_log"]
    im["A_colmaj"] = a_colmaj_host(np.asarray(z["L0_attn_A_log"]))
    im["gate_wT"] = np.ascontiguousarray(np.asarray(z["L0_ffn_gate_w"]).T)
    del im["L0_ffn_gate_w"]
    im.update(host_constants())
    return im


def attn_host_consts():
    return {}


def phase_l1_qkv(ctx, pre, xn1T_ap, Wattn_ap, battn_ap, qkT_ap, v_ap, after=()):
    r = ctx.rec
    s_xn = ctx.sb(pre + "xn", [128, ND * T], BF16)
    xl = []
    for c in range(ND):
        xl.append(r.op("sync", lambda e, c=c: e.dma_start(
            out=s_xn[:, T * c:T * (c + 1)], in_=xn1T_ap[128 * c:128 * (c + 1), :]),
            dma=True, waits=tuple(after)))
    bcol, bd = vec2cols(ctx, pre + "bq", battn_ap[0:2048], 2048, tuple(after), psname=pre + "pv")
    bsc = r.op("vector", lambda e: e.tensor_scalar_mul(
        bcol[:, 8:16], bcol[:, 8:16], 0.125), mark=True, waits=(bd,))
    s_w = [ctx.sb(pre + f"w{k}", [128, ND * 128], BF16) for k in range(2)]
    s_o = ctx.sb(pre + "o", [128, 2048], BF16)
    p_m = [ctx.ps(pre + f"p{k}", [128, 512], F32) for k in range(2)]
    TC = 512
    wl_prev = [None, None]
    ev_prev = [None, None]
    st_prev = None
    for co in range(16):
        b = co % 2
        wl = r.op("gpsimd", lambda e, co=co, b=b: e.dma_start(
            out=s_w[b][:].rearrange("p (a c) -> p a c", a=ND),
            in_=dap(Wattn_ap, 128 * co, [[3072, 128], [128 * 3072, ND], [1, 128]])),
            dma=True, waits=(wl_prev[b],))
        evs = []
        for tc in range(4):
            pb = tc % 2
            mm = None
            for ci in range(ND):
                mm = r.op("tensor", lambda e, b=b, pb=pb, ci=ci, tc=tc: e.matmul(
                    p_m[pb][:], s_w[b][:, 128 * ci:128 * (ci + 1)],
                    s_xn[:, T * ci + TC * tc:T * ci + TC * (tc + 1)],
                    start=(ci == 0), stop=(ci == ND - 1)),
                    mark=(ci == ND - 1),
                    waits=(wl, xl[-1], ev_prev[pb]) if ci == 0 else ())
            g = pe_done(ctx, mm)
            ev = r.op("scalar", lambda e, pb=pb, co=co, tc=tc: e.activation(
                s_o[:, TC * tc:TC * (tc + 1)], p_m[pb][:], AF.Identity,
                bias=bcol[:, co:co + 1], scale=(0.125 if co >= 8 else 1.0)),
                mark=True, waits=(g, bsc, st_prev))
            ev_prev[pb] = ev
            evs.append(ev)
        st_prev = r.op("sync", lambda e, co=co: e.dma_start(
            out=qkT_ap[128 * co:128 * (co + 1), :], in_=s_o[:]),
            dma=True, waits=tuple(evs))
        wl_prev[b] = st_prev
    d_qk = st_prev
    s_wv = ctx.sb(pre + "wv", [128, ND * 512], BF16)
    wvl0 = r.op("gpsimd", lambda e: e.dma_start(
        out=s_wv[:].rearrange("p (a c) -> p a c", a=ND),
        in_=dap(Wattn_ap, 2048, [[3072, 128], [128 * 3072, ND], [1, 512]])),
        dma=True)
    s_bv = ctx.sb(pre + "bv", [1, 1024], BF16)
    bvl = r.op("gpsimd", lambda e: e.dma_start(
        out=s_bv[:], in_=battn_ap[None, 2048:3072]), dma=True)
    s_vo = ctx.sb(pre + "vo", [128, 512], BF16)
    p_v = [ctx.ps(pre + f"pvm{k}", [128, 512], F32) for k in range(2)]
    ev_prev = [None, None]
    st_prev = None
    wv_cur = wvl0
    for half in range(2):
        if half == 1:
            wv_cur = r.op("gpsimd", lambda e: e.dma_start(
                out=s_wv[:].rearrange("p (a c) -> p a c", a=ND),
                in_=dap(Wattn_ap, 2048 + 512, [[3072, 128], [128 * 3072, ND], [1, 512]])),
                dma=True, waits=(st_prev,))
        for i in range(NT):
            pb = i % 2
            mm = None
            for ci in range(ND):
                mm = r.op("tensor", lambda e, pb=pb, ci=ci, i=i: e.matmul(
                    p_v[pb][:], s_xn[:, T * ci + 128 * i:T * ci + 128 * (i + 1)],
                    s_wv[:, 512 * ci:512 * (ci + 1)],
                    start=(ci == 0), stop=False),
                    mark=False,
                    waits=(wv_cur, xl[-1], ev_prev[pb]) if ci == 0 else ())
            mmb = r.op("tensor", lambda e, pb=pb, half=half: e.matmul(
                p_v[pb][:], ctx.sb("s_onesbf")[0:1, 0:128], s_bv[:, 512 * half:512 * (half + 1)],
                start=False, stop=True), mark=True, waits=(bvl,))
            g = pe_done(ctx, mmb)
            ev = r.op("scalar", lambda e, pb=pb: e.copy(s_vo[:], p_v[pb][:]),
                      mark=True, waits=(g, st_prev))
            ev_prev[pb] = ev
            st_prev = r.op("sync", lambda e, i=i, half=half: e.dma_start(
                out=v_ap[128 * i:128 * (i + 1), 512 * half:512 * (half + 1)],
                in_=s_vo[:]), dma=True, waits=(ev,))
    return [d_qk, st_prev]


def phase_attn(ctx, pre, qkT_ap, v_ap, embkR_ap, embvRE_ap, Wproj_ap, bproj_ap,
               s1_ap, after=(), nheads=NH, dbg=None):
    r = ctx.rec
    nc = ctx.nc
    EB = 1664
    ER = 3072
    qkP = [ctx.dram(pre + f"qkP{k}", [2049 * EB], BF16).ap() for k in range(2)]
    edr = [ctx.dram(pre + f"edr{k}", [2050 * ER], BF16).ap() for k in range(2)]
    maskP_ap = ctx.din("c_maskP", [128, 1024], BF16)
    s_mask = ctx.sb(pre + "mask", [128, 1024], BF16)
    lm = r.op("sync", lambda e: e.dma_start(out=s_mask[:], in_=maskP_ap), dma=True,
              waits=tuple(after))
    s_z = ctx.sb(pre + "z", [128, ER], BF16)
    zd = r.op("vector", lambda e: e.memset(s_z[:], 0.0), mark=True, waits=tuple(after))
    zl = []
    for k in range(2):
        for blk in range(17):
            rows = 128 if 128 * (blk + 1) <= 2049 else 2049 - 128 * blk
            if rows <= 0:
                break
            zl.append(r.op("sync", lambda e, k=k, blk=blk, rows=rows: e.dma_start(
                out=dap(qkP[k], 128 * blk * EB, [[EB, rows], [1, EB]]),
                in_=s_z[0:rows, 0:EB]), dma=True, waits=(zd,)))
        for blk in range(17):
            rows = 128 if 128 * (blk + 1) <= 2050 else 2050 - 128 * blk
            if rows <= 0:
                break
            zl.append(r.op("sync", lambda e, k=k, blk=blk, rows=rows: e.dma_start(
                out=dap(edr[k], 128 * blk * ER, [[ER, rows], [1, ER]]),
                in_=s_z[0:rows, 0:ER]), dma=True, waits=(zd,)))
    s_ek = ctx.sb(pre + "ek", [128, 8 * 64], BF16)
    ekl = r.op("gpsimd", lambda e: e.dma_start(
        out=s_ek[:].rearrange("p (a c) -> p a c", a=8),
        in_=dap(embkR_ap, 0, [[64, 128], [128 * 64, 8], [1, 64]])), dma=True,
        waits=tuple(after))
    s_ekT = ctx.sb(pre + "ekT", [64, 1024], BF16)
    p_tr = ctx.ps(pre + "psA", [128, 512], F32)
    p_trb = ctx.ps(pre + "psB", [128, 512], F32)
    p_trc = ctx.ps(pre + "psC", [128, 512], F32)
    s_id = ctx.sb("s_ident")
    ekd = None
    for cchunk in range(8):
        mm = r.op("tensor", lambda e, c=cchunk: e.transpose(
            p_tr[:, 0:128].bitcast(BF16)[0:64, 0:128] if False else p_tr.bitcast(BF16)[0:64, 0:128],
            s_ek[:, 64 * c:64 * (c + 1)], s_id[:]), mark=True, waits=(ekl, ekd))
        g = pe_done(ctx, mm)
        ekd = r.op("scalar", lambda e, c=cchunk: e.copy(
            s_ekT[:, 128 * c:128 * (c + 1)], p_tr.bitcast(BF16)[0:64, 0:128]),
            mark=True, waits=(g,))
    s_ev = ctx.sb(pre + "ev", [128, 16 * 64], BF16)
    evl = r.op("gpsimd", lambda e: e.dma_start(
        out=s_ev[:].rearrange("p (a c) -> p a c", a=16),
        in_=dap(embvRE_ap, 0, [[64, 128], [128 * 64, 16], [1, 64]])), dma=True,
        waits=tuple(after))
    s_v = ctx.sb(pre + "v", [128, 16 * 1024], BF16)
    vl = r.op("sync", lambda e: e.dma_start(
        out=s_v[:].rearrange("p (a c) -> p a c", a=16),
        in_=dap(v_ap, 0, [[1024, 128], [128 * 1024, 16], [1, 1024]])), dma=True,
        waits=tuple(after))
    s_qh = [ctx.sb(pre + f"qh{k}", [64, T], BF16) for k in range(2)]
    s_kh = [ctx.sb(pre + f"kh{k}", [64, T], BF16) for k in range(2)]
    s_dq = [ctx.sb(pre + f"dq{k}", [128, NT], F32) for k in range(2)]
    s_dqn = ctx.sb(pre + "dqn", [128, NT], F32)
    s_dp = [ctx.sb(pre + f"dp{k}", [128, 8], F32) for k in range(4)]
    s_qp = [ctx.sb(pre + "qp", [128, NT * 1024], BF16)] * 2
    s_at2 = [ctx.sb(pre + f"at2{k}", [128, 4 * 1152], BF16) for k in range(2)]
    s_et = [ctx.sb(pre + f"et{k}", [128, 2048], BF16) for k in range(4)]
    s_eT = [ctx.sb(pre + f"eT{k}", [128, 2048], BF16) for k in range(2)]
    s_rT = [ctx.sb(pre + f"rT{k}", [128, 2048], BF16) for k in range(2)]
    s_yT = ctx.sb(pre + "yT", [128, 8 * 2048], BF16)
    p_y = ctx.ps(pre + "py", [64, 2048], F32)
    ydone_hist = {}
    qkP_hist = {}
    est_hist_map = {}
    est_tile_hist = {}
    ee_last_map = {}
    d_all = []
    pend_yev = None

    def emit_yev(h_, g_):
        yev = r.op("scalar", lambda e, h_=h_: e.activation(
            s_yT[64 * (h_ % 2):64 * (h_ % 2) + 64, 2048 * (h_ // 2):2048 * (h_ // 2 + 1)],
            p_y[:], AF.Identity), mark=True, waits=(g_, ydone_hist.get(h_ - 2)))
        ydone_hist[h_] = yev
        d_all.append(yev)

    for h in range(nheads):
        par = h % 2
        lq = r.op("sync", lambda e, h=h, par=par: e.dma_start(
            out=s_qh[par][:], in_=qkT_ap[64 * h:64 * (h + 1), :]), dma=True,
            waits=tuple(after) + (ee_last_map.get(h - 2),))
        lk = r.op("sync", lambda e, h=h, par=par: e.dma_start(
            out=s_kh[par][:], in_=qkT_ap[1024 + 64 * h:1024 + 64 * (h + 1), :]), dma=True)
        banksA = [p_tr, p_trb, p_trc]
        bank_free = {0: ee_last_map.get(h - 1), 1: ee_last_map.get(h - 1),
                     2: ee_last_map.get(h - 1)}
        qevs = []
        mmA_prev = None
        pend_e2 = None
        for i in range(NT):
            ba, bb = (2 * i) % 3, (2 * i + 1) % 3
            mmA = r.op("tensor", lambda e, i=i, par=par, ba=ba: e.matmul(
                banksA[ba][:], s_qh[par][:, 128 * i:128 * (i + 1)], s_ekT[:, 0:512],
                start=True, stop=True), mark=True,
                waits=(lq, ekd, bank_free[ba]))
            if pend_e2 is not None:
                pend_e2(mmA)
                pend_e2 = None
            mmB = r.op("tensor", lambda e, i=i, par=par, bb=bb: e.matmul(
                banksA[bb][:], s_qh[par][:, 128 * i:128 * (i + 1)], s_ekT[:, 512:1024],
                start=True, stop=True), mark=True, waits=(bank_free[bb],))
            dq = r.op("vector", lambda e, i=i, par=par, ba=ba: e.tensor_copy(
                s_dq[par][:, i:i + 1], banksA[ba][:, 0:1]), mark=True,
                waits=(mmB,) + ((ee_last_map.get(h - 2),) if i == 0 else ()))
            dqn = r.op("vector", lambda e, i=i, par=par: e.tensor_scalar_mul(
                s_dqn[:, i:i + 1], s_dq[par][:, i:i + 1], -1.0), mark=True, waits=(dq,))
            e1 = r.op("scalar", lambda e, i=i, par=par, ba=ba: e.activation(
                s_qp[par][:, 1024 * i:1024 * i + 512], banksA[ba][:], AF.Identity,
                bias=s_dqn[:, i:i + 1]), mark=True,
                waits=(dqn, mmB) + ((qkP_hist.get(h - 1),) if i == 0 else ()))
            bank_free[ba] = e1

            def mk_e2(i=i, par=par, bb=bb, dqn=dqn):
                def emit(guard):
                    e2 = r.op("scalar", lambda e: e.activation(
                        s_qp[par][:, 1024 * i + 512:1024 * (i + 1)], banksA[bb][:],
                        AF.Identity, bias=s_dqn[:, i:i + 1]), mark=True,
                        waits=(guard, dqn))
                    bank_free[bb] = e2
                    qevs.append(e2)
                return emit
            pend_e2 = mk_e2()
        pend_e2(pe_done(ctx, mmB))
        pend_e2 = None
        stq = r.op("sync", lambda e, par=par: e.dma_start(
            out=dap(qkP[par], 640, [[EB, 128], [128 * EB, NT], [1, 1024]]),
            in_=s_qp[par][:].rearrange("p (a c) -> p a c", a=NT)), dma=True,
            waits=tuple(qevs) + (qkP_hist.get(h - 2), zl[-1]))
        qkP_hist[h] = stq
        if pend_yev is not None:
            emit_yev(*pend_yev)
            pend_yev = None
        banks3 = [p_tr, p_trb, p_trc]
        last_ee = None
        est_hist = est_hist_map.get(h - 2)
        ee_by_cyc = {}
        cyc = 0
        pend_ev = None
        for grp in range(4):
            at2l = r.op("sync", lambda e, grp=grp, par=par: e.dma_start(
                out=s_at2[par][:].rearrange("p (a c) -> p a c", a=4),
                in_=dap(qkP[par], (4 * grp) * 128 * EB + 639,
                        [[EB - 1, 128], [128 * EB, 4], [1, 1152]])),
                dma=True, waits=(stq, est_hist), label=None)
            if dbg is not None and "at2" in dbg and h == 0 and grp == 1:
                r.op("sync", lambda e: e.dma_start(out=dbg["at2"], in_=s_at2[0][:]),
                     dma=True, waits=(at2l,))
            for sub in range(4):
                i = 4 * grp + sub
                idx = 2 * par + (i % 2)
                t0 = 128 * i
                nsc = (t0 + 128 + 511) // 512
                eevs = []
                for sci in range(nsc):
                    w = min(512, t0 + 128 - 512 * sci)
                    ps = banks3[cyc % 3]
                    mm = r.op("tensor", lambda e, i=i, sci=sci, w=w, ps=ps, par=par: e.matmul(
                        ps[:, 0:w], s_qh[par][:, 128 * i:128 * (i + 1)],
                        s_kh[par][:, 512 * sci:512 * sci + w], start=True, stop=False),
                        mark=True, waits=(lk, at2l, ee_by_cyc.get(cyc - 3)))
                    if pend_ev is not None:
                        pend_ev(mm)
                        pend_ev = None
                    s_lo = max(512 * sci, t0 - 1024)
                    if s_lo < 512 * sci + w:
                        co = s_lo - 512 * sci
                        jlo = s_lo - (t0 - 1024)
                        r.op("tensor", lambda e, i=i, sci=sci, w=w, co=co, jlo=jlo,
                             sub=sub, ps=ps, par=par: e.matmul(
                            ps[:, co:w], s_id[:],
                            s_at2[par][:, 1152 * sub + jlo:1152 * sub + jlo + (w - co)],
                            start=False, stop=False), mark=False)
                    last_is_mask = (512 * sci + w == t0 + 128)
                    if last_is_mask:
                        off = 512 - (t0 - 512 * sci)
                        mm2 = r.op("tensor", lambda e, off=off, w=w, ps=ps: e.matmul(
                            ps[:, 0:w], s_id[:], s_mask[:, off:off + w],
                            start=False, stop=True), mark=True, waits=(lm,))
                    else:
                        mm2 = r.op("tensor", lambda e, w=w, ps=ps: e.matmul(
                            ps[:, 0:w], s_id[0:1, 0:128], s_z[0:1, 0:w],
                            start=False, stop=True), mark=True, waits=(zd,))

                    def mk_pend(i=i, sci=sci, w=w, ps=ps, par=par, idx=idx,
                                cyc=cyc, sub_last=(sci == nsc - 1), eevs=eevs,
                                nsc=nsc, t0=t0, grp=grp):
                        def emit(guard):
                            nonlocal last_ee, est_hist
                            ee = r.op("scalar", lambda e: e.activation(
                                s_et[idx][:, 512 * sci:512 * sci + w], ps[:, 0:w],
                                AF.Exp, bias=s_dq[par][:, i:i + 1],
                                accum_out=s_dp[idx][:, sci:sci + 1]),
                                mark=True, waits=(guard, est_tile_hist.get(idx)))
                            ee_by_cyc[cyc] = ee
                            last_ee = ee
                            eevs.append(ee)
                            if sub_last:
                                sm = r.op("vector", lambda e: e.reduce_sum(
                                    s_dp[idx][:, 6:7], s_dp[idx][:, 0:nsc], AX.X),
                                    mark=True, waits=(eevs[-1],))
                                rc = r.op("vector", lambda e: e.reciprocal(
                                    s_dp[idx][:, 7:8], s_dp[idx][:, 6:7]),
                                    mark=True, waits=(sm,))
                                sc2 = r.op("vector", lambda e: e.tensor_scalar(
                                    s_et[idx][:, 0:t0 + 128], s_et[idx][:, 0:t0 + 128],
                                    s_dp[idx][:, 7:8], None, ALU.mult),
                                    mark=True, waits=(rc,))
                                est = r.op("sync", lambda e: e.dma_start(
                                    out=dap(edr[par], (1 + t0) * ER + 1024,
                                            [[ER, 128], [1, t0 + 128]]),
                                    in_=s_et[idx][:, 0:t0 + 128]), dma="e",
                                    waits=(sc2, ydone_hist.get(h - 2), zl[-1]))
                                est_tile_hist[idx] = est
                                est_hist = est
                        return emit
                    pend_ev = mk_pend()
                    cyc += 1
                if sub == 3 and pend_ev is not None:
                    pend_ev(pe_done(ctx, mm2))
                    pend_ev = None
        est_hist_map[h] = est_hist
        ee_last_map[h] = last_ee
        ymm_last = None
        ymm_hist = {}
        for sc in range(NT):
            wwin = 2048 - 128 * sc
            eTl = r.op("sync", lambda e, sc=sc, par=par, wwin=wwin: e.dma_start_transpose(
                s_eT[sc % 2][:, 0:wwin],
                dap(edr[par], (1 + 128 * sc) * ER + 1024 + 128 * sc, [[ER, wwin], [1, 128]])),
                dma="t", waits=(est_hist, ymm_hist.get(sc - 2),
                                ydone_hist.get(h - 1) if sc < 2 else None))
            for b in range(4):
                lo = max(128 * sc, 512 * b)
                hi = 512 * (b + 1)
                if lo >= hi:
                    continue
                ymm_last = r.op("tensor", lambda e, sc=sc, b=b, lo=lo, hi=hi, h=h: e.matmul(
                    p_y[:, lo:hi], s_v[:, 1024 * sc + 64 * h:1024 * sc + 64 * (h + 1)],
                    s_eT[sc % 2][:, lo - 128 * sc:hi - 128 * sc],
                    start=(sc == 0), stop=False), mark=True,
                    waits=(eTl, vl) + ((ydone_hist.get(h - 1),) if sc == 0 and b == 0 else ()))
            ymm_hist[sc] = ymm_last
        rT_hist = {}
        for J in range(16):
            rTl = r.op("sync", lambda e, J=J, par=par: e.dma_start_transpose(
                s_rT[J % 2][:], dap(edr[par], 128 * J + 2049, [[ER + 1, 2048], [1, 128]])),
                dma="t", waits=(est_hist, rT_hist.get(J - 2),
                                ydone_hist.get(h - 1) if J < 2 else None))
            for b in range(4):
                ymm_last = r.op("tensor", lambda e, J=J, b=b: e.matmul(
                    p_y[:, 512 * b:512 * (b + 1)], s_ev[:, 64 * J:64 * (J + 1)],
                    s_rT[J % 2][:, 512 * b:512 * (b + 1)],
                    start=False, stop=(J == 15 and b == 3)), mark=True,
                    waits=(rTl, evl))
            rT_hist[J] = ymm_last
        g = pe_done(ctx, ymm_last)
        pend_yev = (h, g)
    if pend_yev is not None:
        emit_yev(*pend_yev)
        pend_yev = None
    if dbg is not None:
        for nm, src_ap, rows, pitch in (("qkP0", qkP[0], 2049, EB), ("edr0", edr[0], 2050, ER)):
            if nm in dbg:
                r.op("sync", lambda e, nm=nm, src_ap=src_ap, rows=rows, pitch=pitch: e.dma_start(
                    out=dbg[nm], in_=dap(src_ap, 0, [[pitch, rows], [1, pitch]])),
                    dma=True, waits=(d_all[-1],))
        if "yT" in dbg:
            for c in range(ND):
                r.op("sync", lambda e, c=c: e.dma_start(
                    out=dbg["yT"][128 * c:128 * (c + 1), :],
                    in_=s_yT[:, 2048 * c:2048 * (c + 1)]), dma=True,
                    waits=(d_all[-1],))
    s_wp = ctx.sb(pre + "wp", [128, ND * 1024], BF16)
    wpl = r.op("gpsimd", lambda e: e.dma_start(
        out=s_wp[:].rearrange("p (a c) -> p a c", a=ND),
        in_=dap(Wproj_ap, 0, [[1024, 128], [128 * 1024, ND], [1, 1024]])),
        dma=True, waits=tuple(after))
    s_bp = ctx.sb(pre + "bp", [1, 1024], BF16)
    bpl = r.op("gpsimd", lambda e: e.dma_start(
        out=s_bp[:], in_=bproj_ap[None, :]), dma=True, waits=tuple(after))
    s_so = ctx.sb(pre + "so", [128, 1024], BF16)
    p_p = [p_tr, p_trb]
    ev_prev = [d_all[-1], d_all[-1]]
    dones = []
    for i in range(NT):
        for half in range(2):
            pb = half
            mm = None
            for c in range(ND):
                mm = r.op("tensor", lambda e, i=i, half=half, c=c, pb=pb: e.matmul(
                    p_p[pb][:], s_yT[:, 2048 * (c) + 128 * i:2048 * c + 128 * (i + 1)],
                    s_wp[:, 1024 * c + 512 * half:1024 * c + 512 * (half + 1)],
                    start=(c == 0), stop=False), mark=False,
                    waits=(wpl, d_all[-1], ev_prev[pb]) if c == 0 else ())
            mmb = r.op("tensor", lambda e, half=half, pb=pb: e.matmul(
                p_p[pb][:], ctx.sb("s_onesbf")[0:1, 0:128], s_bp[:, 512 * half:512 * (half + 1)],
                start=False, stop=True), mark=True, waits=(bpl,))
            g = pe_done(ctx, mmb)
            ev = r.op("scalar", lambda e, half=half, pb=pb: e.activation(
                s_so[:, 512 * half:512 * (half + 1)], p_p[pb][:], AF.Identity),
                mark=True, waits=(g,))
            ev_prev[pb] = ev
        dones.append(r.op("sync", lambda e, i=i: e.dma_start(
            out=s1_ap[128 * i:128 * (i + 1), :], in_=s_so[:]), dma=True,
            waits=(ev_prev[0], ev_prev[1])))
    return dones[-1:]


def phase_wq_multi(ctx, pre, jobs, after=()):
    r = ctx.rec
    st = []
    for k, job in enumerate(jobs):
        NI = job["nrow"] // 128
        NC = job["ncol"] // 128
        pj = pre + f"j{k}_"
        st.append(dict(job, NI=NI, NC=NC, prev=None, nrow=job["nrow"],
                       s_w=ctx.sb(pj + "w", [128, job["ncol"]], F32),
                       s_i=ctx.sb(pj + "i", [128, job["ncol"]], I32),
                       s_b=ctx.sb(pj + "b", [128, job["ncol"]], BF16),
                       s_st=ctx.sb(pj + "st", [128, 4], F32),
                       s_sc=ctx.sb(pj + "sc", [128, NI], F32),
                       s_tr=ctx.sb(pj + "tr", [128, job["ncol"]], BF16),
                       p_t=[ctx.ps(pj + f"pt{q}", [128, 128], BF16) for q in range(2)]))
    s_id = ctx.sb("s_ident")
    dones = []
    for ic in range(max(j["NI"] for j in st)):
        for j in st:
            if ic >= j["NI"]:
                continue
            s_w, s_i, s_b, s_stt = j["s_w"], j["s_i"], j["s_b"], j["s_st"]
            s_sc, s_tr, p_t = j["s_sc"], j["s_tr"], j["p_t"]
            em = chain_maker(r, j["prev"], *(after if ic == 0 else ()))
            em("sync", lambda e, j=j, ic=ic, s_w=s_w: e.dma_start(
                out=s_w[:], in_=j["w_ap"][128 * ic:128 * (ic + 1), :]),
                dma=j["stream"])
            em("vector", lambda e, s_w=s_w, s_stt=s_stt: e.reduce_max(
                s_stt[:, 0:1], s_w[:], AX.X, apply_absolute_value=True))
            em("vector", lambda e, s_stt=s_stt: e.reciprocal(s_stt[:, 1:2], s_stt[:, 0:1]))
            em("vector", lambda e, s_stt=s_stt: e.tensor_scalar_mul(
                s_stt[:, 2:3], s_stt[:, 1:2], 127.0))
            em("vector", lambda e, ic=ic, s_sc=s_sc, s_stt=s_stt: e.tensor_scalar_mul(
                s_sc[:, ic:ic + 1], s_stt[:, 0:1], 1.0 / 127.0))
            em("vector", lambda e, s_w=s_w, s_stt=s_stt: e.tensor_scalar(
                s_w[:], s_w[:], s_stt[:, 2:3], None, ALU.mult))
            em("vector", lambda e, s_w=s_w, s_i=s_i: e.tensor_copy(s_i[:], s_w[:]))
            if j["dequant"]:
                em("vector", lambda e, s_w=s_w, s_i=s_i: e.tensor_copy(s_w[:], s_i[:]))
                em("vector", lambda e, ic=ic, s_w=s_w, s_sc=s_sc: e.tensor_scalar(
                    s_w[:], s_w[:], s_sc[:, ic:ic + 1], None, ALU.mult))
                em("vector", lambda e, s_w=s_w, s_b=s_b: e.tensor_copy(s_b[:], s_w[:]))
            else:
                em("vector", lambda e, s_i=s_i, s_b=s_b: e.tensor_copy(s_b[:], s_i[:]))
            trs = []
            evs = []
            for c in range(j["NC"]):
                trs.append(em("tensor", lambda e, c=c, s_b=s_b, p_t=p_t: e.transpose(
                    p_t[c % 2][:], s_b[:, 128 * c:128 * (c + 1)], s_id[:]),
                    waits=(evs[c - 2],) if c >= 2 else ()))
                if c >= 1:
                    evs.append(r.op("scalar", lambda e, c=c, s_tr=s_tr, p_t=p_t: e.copy(
                        s_tr[:, 128 * (c - 1):128 * c], p_t[(c - 1) % 2][:]),
                        mark=True, waits=(trs[c],)))
            evs.append(r.op("scalar", lambda e, j=j, s_tr=s_tr, p_t=p_t: e.copy(
                s_tr[:, 128 * (j["NC"] - 1):128 * j["NC"]], p_t[(j["NC"] - 1) % 2][:]),
                mark=True, waits=(pe_done(ctx, trs[-1]),)))
            j["prev"] = r.op("sync", lambda e, j=j, ic=ic, s_tr=s_tr: e.dma_start(
                out=dap(j["wqT_ap"], 128 * ic,
                        [[j["nrow"], 128], [128 * j["nrow"], j["NC"]], [1, 128]]),
                in_=s_tr[:].rearrange("p (a c) -> p a c", a=j["NC"])),
                dma=j["stream"], waits=tuple(evs))
    for j in st:
        d = j["prev"]
        if j["scales_ap"] is not None:
            d = r.op("sync", lambda e, j=j: e.dma_start(
                out=j["scales_ap"], in_=j["s_sc"][:]), dma=j["stream"],
                waits=(j["prev"],))
        dones.append(d)
    return dones


def phase_wq_prep(ctx, pre, w_ap, nrow, ncol, wqT_ap, scales_ap=None,
                  dequant=False, after=(), stream=True):
    r = ctx.rec
    NI = nrow // 128
    NC = ncol // 128
    s_w = ctx.sb(pre + "w", [128, ncol], F32)
    s_i = ctx.sb(pre + "i", [128, ncol], I32)
    s_b = ctx.sb(pre + "b", [128, ncol], BF16)
    s_st = ctx.sb(pre + "st", [128, 4], F32)
    s_sc = ctx.sb(pre + "sc", [128, NI], F32)
    s_tr = ctx.sb(pre + "tr", [128, 128], BF16)
    p_t = ctx.ps(pre + "pt", [128, 128], BF16)
    s_id = ctx.sb("s_ident")
    prev = None
    dones = []
    for ic in range(NI):
        em = chain_maker(r, prev, *(after if ic == 0 else ()))
        em("sync", lambda e, ic=ic: e.dma_start(
            out=s_w[:], in_=w_ap[128 * ic:128 * (ic + 1), :]), dma=stream)
        em("vector", lambda e: e.reduce_max(s_st[:, 0:1], s_w[:], AX.X,
                                            apply_absolute_value=True))
        em("vector", lambda e: e.reciprocal(s_st[:, 1:2], s_st[:, 0:1]))
        em("vector", lambda e: e.tensor_scalar_mul(s_st[:, 2:3], s_st[:, 1:2], 127.0))
        em("vector", lambda e, ic=ic: e.tensor_scalar_mul(
            s_sc[:, ic:ic + 1], s_st[:, 0:1], 1.0 / 127.0))
        em("vector", lambda e: e.tensor_scalar(s_w[:], s_w[:], s_st[:, 2:3], None, ALU.mult))
        em("vector", lambda e: e.tensor_copy(s_i[:], s_w[:]))
        em("vector", lambda e: e.tensor_copy(s_w[:], s_i[:]))
        em("vector", lambda e: e.tensor_scalar(
            s_w[:], s_w[:], -128.0, 127.0, ALU.max, ALU.min))
        if dequant:
            em("vector", lambda e, ic=ic: e.tensor_scalar(
                s_w[:], s_w[:], s_sc[:, ic:ic + 1], None, ALU.mult))
        em("vector", lambda e: e.tensor_copy(s_b[:], s_w[:]))
        for c in range(NC):
            em("tensor", lambda e, c=c: e.transpose(
                p_t[:], s_b[:, 128 * c:128 * (c + 1)], s_id[:]))
            em("tensor", lambda e: e.matmul(
                ctx.ps("p_guard")[:], ctx.sb("s_onesbf")[0:1, 0:1],
                ctx.sb("s_onesbf")[0:1, 0:16], start=True, stop=True))
            em("scalar", lambda e: e.copy(s_tr[:], p_t[:]))
            prev = em("sync", lambda e, ic=ic, c=c: e.dma_start(
                out=wqT_ap[128 * c:128 * (c + 1), 128 * ic:128 * (ic + 1)],
                in_=s_tr[:]), dma=stream)
        dones.append(prev)
    if scales_ap is not None:
        dones.append(r.op("sync", lambda e: e.dma_start(
            out=scales_ap, in_=s_sc[:]), dma=stream, waits=(prev,)))
    return dones[-1:]


def phase_absmax(ctx, pre, x_ap, rows, cols, out_ap, after=()):
    r = ctx.rec
    s_x = ctx.sb(pre + "x", [128, cols], F32)
    s_m = ctx.sb(pre + "m", [128, 2], F32)
    s_r = ctx.sb(pre + "r", [1, 128], F32)
    prev = None
    init = None
    for i in range(rows // 128):
        em = chain_maker(r, prev, *(after if i == 0 else ()))
        em("sync", lambda e, i=i: e.dma_start(
            out=s_x[:], in_=x_ap[128 * i:128 * (i + 1), :]), dma=True)
        em("vector", lambda e: e.reduce_max(s_m[:, 1:2], s_x[:], AX.X,
                                            apply_absolute_value=True))
        if i == 0:
            prev = em("vector", lambda e: e.tensor_copy(s_m[:, 0:1], s_m[:, 1:2]))
        else:
            prev = em("vector", lambda e: e.tensor_max(s_m[:, 0:1], s_m[:, 0:1], s_m[:, 1:2]))
    em = chain_maker(r, prev)
    em("sync", lambda e: e.dma_start(out=s_r[:], in_=AP(s_m, 0, [[2, 128], [1, 1]]),
                                     allow_slow_non_contiguous=True), dma=True)
    em("vector", lambda e: e.reduce_max(s_m[0:1, 1:2], s_r[:], AX.X))
    return [em("sync", lambda e: e.dma_start(out=out_ap[None, :], in_=s_m[0:1, 1:2]), dma=True)]


def scalar_prep(ctx, pre, clip_ap, lmax_ap, nmax, after=()):
    r = ctx.rec
    s_l = ctx.sb(pre + "l", [1, 16], F32)
    s_c = ctx.sb(pre + "c", [1, 4], F32)
    s_rep = ctx.sb(pre + "rep", [128, 2], F32)
    p_b = ctx.ps(pre + "pb", [128, 16], F32)
    l1 = r.op("sync", lambda e: e.dma_start(out=s_l[:, 0:nmax], in_=lmax_ap[None, :]),
              dma=True, waits=tuple(after))
    l2 = r.op("sync", lambda e: e.dma_start(out=s_c[:, 0:1], in_=clip_ap[None, :]), dma=True)
    em = chain_maker(r, l1, l2)
    em("vector", lambda e: e.reduce_max(s_c[:, 1:2], s_l[:, 0:nmax], AX.X))
    em("scalar", lambda e: e.activation(s_c[:, 2:3], s_c[:, 0:1], AF.Sigmoid))
    em("vector", lambda e: e.tensor_tensor(s_c[:, 1:2], s_c[:, 1:2], s_c[:, 2:3], ALU.mult))
    em("vector", lambda e: e.tensor_scalar_mul(s_c[:, 2:3], s_c[:, 1:2], 1.0 / 127.0))
    em("vector", lambda e: e.reciprocal(s_c[:, 3:4], s_c[:, 2:3]))
    mm1 = em("tensor", lambda e: e.matmul(
        p_b[:, 0:2], ctx.sb("s_onesf")[:, 0:128], s_c[:, 2:4], start=True, stop=True))
    em("tensor", lambda e: e.matmul(
        ctx.ps("p_guard")[:], ctx.sb("s_onesbf")[0:1, 0:1],
        ctx.sb("s_onesbf")[0:1, 0:16], start=True, stop=True))
    d = em("scalar", lambda e: e.copy(s_rep[:], p_b[:, 0:2]))
    return s_rep, [d]


def phase_ffn_a(ctx, pre, xn1_ap, clip_x_ap, lmax_ap, nmax, wqgT_ap, wquT_ap,
                swg_ap, swu_ap, hF_ap, lmaxh_ap, after=()):
    r = ctx.rec
    NI = IFF // 128
    s_rep, d_s = scalar_prep(ctx, pre + "s", clip_x_ap, lmax_ap, nmax, after)
    s_x = ctx.sb(pre + "x", [128, D], F32)
    s_i = ctx.sb(pre + "i", [128, D], I32)
    s_b = ctx.sb(pre + "xb", [128, D], BF16)
    s_xqT = ctx.sb(pre + "xqT", [128, ND * T], BF16)
    s_tr = ctx.sb(pre + "tr", [128, 128], BF16)
    p_t = ctx.ps(pre + "pt", [128, 128], BF16)
    s_id = ctx.sb("s_ident")
    prev = None
    for i in range(NT):
        em = chain_maker(r, prev, *d_s, *(after if i == 0 else ()))
        em("sync", lambda e, i=i: e.dma_start(
            out=s_x[:], in_=xn1_ap[128 * i:128 * (i + 1), :]), dma=True)
        em("vector", lambda e: e.tensor_scalar(s_x[:], s_x[:], s_rep[:, 1:2], None, ALU.mult))
        em("vector", lambda e: e.tensor_copy(s_i[:], s_x[:]))
        em("vector", lambda e: e.tensor_copy(s_x[:], s_i[:]))
        em("vector", lambda e: e.tensor_scalar(
            s_x[:], s_x[:], -128.0, 127.0, ALU.max, ALU.min))
        em("vector", lambda e: e.tensor_copy(s_b[:], s_x[:]))
        for c in range(ND):
            em("tensor", lambda e, c=c: e.transpose(
                p_t[:], s_b[:, 128 * c:128 * (c + 1)], s_id[:]))
            em("tensor", lambda e: e.matmul(
                ctx.ps("p_guard")[:], ctx.sb("s_onesbf")[0:1, 0:1],
                ctx.sb("s_onesbf")[0:1, 0:16], start=True, stop=True))
            prev = em("scalar", lambda e, c=c, i=i: e.copy(
                s_xqT[:, T * c + 128 * i:T * c + 128 * (i + 1)], p_t[:]))
    d_xq = prev
    s_swg = ctx.sb(pre + "swg", [128, NI], F32)
    s_swu = ctx.sb(pre + "swu", [128, NI], F32)
    lg = r.op("sync", lambda e: e.dma_start(out=s_swg[:, 0:NI], in_=swg_ap), dma=True,
              waits=tuple(after))
    lu = r.op("sync", lambda e: e.dma_start(out=s_swu[:, 0:NI], in_=swu_ap), dma=True)
    sg1 = r.op("vector", lambda e: e.tensor_scalar(
        s_swg[:], s_swg[:], s_rep[:, 0:1], None, ALU.mult),
        mark=True, waits=(lg,) + tuple(d_s))
    sg2 = r.op("vector", lambda e: e.tensor_scalar(
        s_swu[:], s_swu[:], s_rep[:, 0:1], None, ALU.mult), mark=True, waits=(lu, sg1))
    s_wg = [ctx.sb(pre + f"wg{k}", [128, ND * 128], BF16) for k in range(2)]
    s_wu = [ctx.sb(pre + f"wu{k}", [128, ND * 128], BF16) for k in range(2)]
    s_h1 = ctx.sb(pre + "h1", [128, 512], F32)
    s_h = ctx.sb(pre + "h", [128, T], F32)
    s_mh = ctx.sb(pre + "mh", [128, 4], F32)
    s_mr = ctx.sb(pre + "mr", [1, 128], F32)
    p_h = [ctx.ps(pre + f"ph{k}", [128, 512], F32) for k in range(2)]
    wl_prev = [None, None]
    ev_prev = [None, None]
    st_prev = None
    mx_prev = None
    s2_prev = None
    for ic in range(NI):
        b = ic % 2
        wlg = r.op("gpsimd", lambda e, ic=ic, b=b: e.dma_start(
            out=s_wg[b][:].rearrange("p (a c) -> p a c", a=ND),
            in_=dap(wqgT_ap, 128 * ic, [[IFF, 128], [128 * IFF, ND], [1, 128]])),
            dma=True, waits=(wl_prev[b],) + tuple(after))
        wlu = r.op("gpsimd", lambda e, ic=ic, b=b: e.dma_start(
            out=s_wu[b][:].rearrange("p (a c) -> p a c", a=ND),
            in_=dap(wquT_ap, 128 * ic, [[IFF, 128], [128 * IFF, ND], [1, 128]])),
            dma=True, waits=tuple(after) if ic == 0 else ())
        for tc in range(4):
            pb = tc % 2
            mm1 = None
            for c in range(ND):
                mm1 = r.op("tensor", lambda e, b=b, c=c, ic=ic, tc=tc, pb=pb: e.matmul(
                    p_h[pb][:], s_wg[b][:, 128 * c:128 * (c + 1)],
                    s_xqT[:, T * c + 512 * tc:T * c + 512 * (tc + 1)],
                    start=(c == 0), stop=(c == ND - 1)), mark=(c == ND - 1),
                    waits=(wlg, d_xq, ev_prev[pb]) if c == 0 else ())
            g = pe_done(ctx, mm1)
            e1 = r.op("scalar", lambda e, pb=pb, ic=ic: e.activation(
                s_h1[:], p_h[pb][:], AF.Silu, scale=s_swg[:, ic:ic + 1]),
                mark=True, waits=(g, sg2, s2_prev))
            mm3 = None
            for c in range(ND):
                mm3 = r.op("tensor", lambda e, b=b, c=c, ic=ic, tc=tc, pb=pb: e.matmul(
                    p_h[pb][:], s_wu[b][:, 128 * c:128 * (c + 1)],
                    s_xqT[:, T * c + 512 * tc:T * c + 512 * (tc + 1)],
                    start=(c == 0), stop=(c == ND - 1)), mark=(c == ND - 1),
                    waits=(wlu, e1) if c == 0 else ())
            g3 = pe_done(ctx, mm3)
            s2 = r.op("vector", lambda e, pb=pb, ic=ic, tc=tc: e.scalar_tensor_tensor(
                s_h[:, 512 * tc:512 * (tc + 1)], p_h[pb][:], s_swu[:, ic:ic + 1],
                s_h1[:], ALU.mult, ALU.mult), mark=True, waits=(g3, sg2, st_prev))
            ev_prev[pb] = s2
            s2_prev = s2
        mx = r.op("vector", lambda e, ic=ic: e.reduce_max(
            s_mh[:, 1:2], s_h[:], AX.X, apply_absolute_value=True), mark=True)
        if ic == 0:
            mx_prev = r.op("vector", lambda e: e.tensor_copy(s_mh[:, 0:1], s_mh[:, 1:2]),
                           mark=True, waits=(mx,))
        else:
            mx_prev = r.op("vector", lambda e: e.tensor_max(
                s_mh[:, 0:1], s_mh[:, 0:1], s_mh[:, 1:2]), mark=True, waits=(mx, mx_prev))
        st_prev = r.op("sync", lambda e, ic=ic: e.dma_start(
            out=hF_ap[128 * ic:128 * (ic + 1), :], in_=s_h[:]), dma=True, waits=(mx,))
        wl_prev[b] = st_prev
    em = chain_maker(r, mx_prev, st_prev)
    em("sync", lambda e: e.dma_start(out=s_mr[:], in_=AP(s_mh, 0, [[4, 128], [1, 1]]),
                                     allow_slow_non_contiguous=True), dma=True)
    em("vector", lambda e: e.reduce_max(s_mh[0:1, 1:2], s_mr[:], AX.X))
    return [em("sync", lambda e: e.dma_start(out=lmaxh_ap[None, :], in_=s_mh[0:1, 1:2]), dma=True)]


def phase_ffn_b(ctx, pre, hF_ap, clip_d_ap, lmaxh_ap, nmax, wdTs_ap,
                x1_ap, s1_ap, out_ap, after=(), dbg_hq=None, dbg_g=None):
    r = ctx.rec
    NI = IFF // 128
    s_rep, d_s = scalar_prep(ctx, pre + "s", clip_d_ap, lmaxh_ap, nmax, after)
    s_h = ctx.sb(pre + "h", [128, T], F32)
    s_i = ctx.sb(pre + "i", [128, T], I32)
    s_hq2 = [ctx.sb(pre + f"hq{k}", [128, 6 * T], BF16) for k in range(4)]
    prev = None
    for ic in range(NI):
        em = chain_maker(r, prev, *d_s, *(after if ic == 0 else ()))
        em("sync", lambda e, ic=ic: e.dma_start(
            out=s_h[:], in_=hF_ap[128 * ic:128 * (ic + 1), :]), dma=True)
        em("vector", lambda e: e.tensor_scalar(s_h[:], s_h[:], s_rep[:, 1:2], None, ALU.mult))
        em("vector", lambda e: e.tensor_copy(s_i[:], s_h[:]))
        em("vector", lambda e: e.tensor_copy(s_h[:], s_i[:]))
        em("vector", lambda e: e.tensor_scalar(
            s_h[:], s_h[:], -128.0, 127.0, ALU.max, ALU.min))
        prev = em("vector", lambda e, ic=ic: e.tensor_copy(
            s_hq2[ic // 6][:, T * (ic % 6):T * (ic % 6 + 1)], s_h[:]))
    d_hq = prev
    if dbg_hq is not None:
        for k in range(4):
            w_ = min(6 * T, 22 * T - k * 6 * T)
            d_hq = r.op("sync", lambda e, k=k, w_=w_: e.dma_start(
                out=dbg_hq[:, 6 * T * k:6 * T * k + w_], in_=s_hq2[k][:, 0:w_]),
                dma=True, waits=(prev,))
    s_wd2 = [ctx.sb(pre + f"wd{k}", [128, 11 * 1024], BF16) for k in range(2)]
    wl = None
    for k in range(2):
        wl = r.op("gpsimd", lambda e, k=k: e.dma_start(
            out=s_wd2[k][:].rearrange("p (a c) -> p a c", a=11),
            in_=dap(wdTs_ap, 11 * k * 128 * 1024, [[1024, 128], [128 * 1024, 11], [1, 1024]])),
            dma=True, waits=tuple(after))
    s_g = [ctx.sb(pre + f"g{k}", [128, 512], BF16) for k in range(2)]
    s_x = ctx.sb(pre + "x", [128, 512], F32)
    s_s1 = ctx.sb(pre + "s1", [128, 512], BF16)
    s_o = ctx.sb(pre + "o", [128, 512], F32)
    p_d = [ctx.ps(pre + f"pd{k}", [128, 512], F32) for k in range(2)]
    ev_prev = [None, None]
    m1_prev = [None, None]
    st_prev = None
    dones = []
    for i in range(NT):
        for half in range(2):
            pb = half
            mm = None
            for ic in range(NI):
                mm = r.op("tensor", lambda e, i=i, half=half, ic=ic, pb=pb: e.matmul(
                    p_d[pb][:], s_hq2[ic // 6][:, T * (ic % 6) + 128 * i:T * (ic % 6) + 128 * (i + 1)],
                    s_wd2[ic // 11][:, 1024 * (ic % 11) + 512 * half:1024 * (ic % 11) + 512 * (half + 1)],
                    start=(ic == 0), stop=(ic == NI - 1)), mark=(ic == NI - 1),
                    waits=(wl, d_hq, ev_prev[pb]) if ic == 0 else ())
            g = pe_done(ctx, mm)
            ev = r.op("scalar", lambda e, pb=pb: e.activation(
                s_g[pb][:], p_d[pb][:], AF.Silu, scale=s_rep[:, 0:1]),
                mark=True, waits=(g, m1_prev[pb]) + tuple(d_s))
            ev_prev[pb] = ev
            lx = r.op("sync", lambda e, i=i, half=half: e.dma_start(
                out=s_x[:], in_=x1_ap[128 * i:128 * (i + 1), 512 * half:512 * (half + 1)]),
                dma=True, waits=(st_prev,))
            ls = r.op("sync", lambda e, i=i, half=half: e.dma_start(
                out=s_s1[:], in_=s1_ap[128 * i:128 * (i + 1), 512 * half:512 * (half + 1)]),
                dma=True)
            m1 = r.op("vector", lambda e, pb=pb: e.tensor_tensor(s_o[:], s_s1[:], s_g[pb][:], ALU.mult),
                      mark=True, waits=(ev, ls, st_prev))
            m1_prev[pb] = m1
            m2 = r.op("vector", lambda e: e.tensor_add(s_o[:], s_o[:], s_x[:]),
                      mark=True, waits=(m1, lx))
            st_prev = r.op("sync", lambda e, i=i, half=half: e.dma_start(
                out=out_ap[128 * i:128 * (i + 1), 512 * half:512 * (half + 1)],
                in_=s_o[:]), dma=True, waits=(m2,))
            dones.append(st_prev)
    return dones[-1:]


def finalize(ctx, *labels):
    r = ctx.rec
    s_fin = ctx.sb("s_fin", [1, 4], F32)
    return r.op("vector", lambda e: e.tensor_copy(s_fin[:, 0:1], s_fin[:, 1:2]),
                mark=True, waits=tuple(l for l in labels if l is not None))


def build_launch_a():
    ctx = Ctx()
    d_l0 = build_l0(ctx)
    x1 = ctx.dram("x1", [T, D], F32, kind="ExternalOutput").ap()
    xn1 = ctx.dram("xn1", [T, D], F32, kind="ExternalOutput").ap()
    xn1T = ctx.dram("xn1T", [D, T], BF16, kind="Internal").ap()
    lmax1 = ctx.dram("lmax1", [1], F32, kind="ExternalOutput").ap()
    Wattn = ctx.din("Wattn", [D, 3 * D], F32)
    battn = ctx.din("battn", [3 * D], F32)
    Wproj = ctx.din("Wproj", [D, D], F32)
    bproj = ctx.din("bproj", [D], F32)
    embkR = ctx.din("embkR", [NR, HD], F32)
    embvRE = ctx.din("embvRE", [T, HD], F32)
    qkT = ctx.dram("qkT", [2048, T], BF16, kind="Internal").ap()
    v = ctx.dram("v", [T, D], BF16, kind="Internal").ap()
    s1 = ctx.dram("s1", [T, D], BF16, kind="ExternalOutput").ap()

    d_ln = phase_ln(ctx, "l1n_", x1, dict(xn_f32=xn1, xnT_bf=xn1T),
                    affine=None, after=tuple(d_l0))
    ctx.free("l1n_")
    d_qk = phase_l1_qkv(ctx, "qk_", xn1T, Wattn, battn, qkT, v,
                        after=tuple(d_ln))
    ctx.free("qk_")
    d_am = phase_absmax(ctx, "am_", xn1, T, D, lmax1, after=tuple(d_qk))
    ctx.free("am_")
    d_at = phase_attn(ctx, "at_", qkT, v, embkR, embvRE, Wproj, bproj, s1,
                      after=tuple(d_qk) + tuple(d_am))
    finalize(ctx, *(tuple(d_at) + tuple(d_am)))
    return ctx.build()


def build_launch_b():
    ctx = Ctx()
    xn1 = ctx.din("xn1", [T, D], F32)
    lmax8 = ctx.din("lmax8", [8], F32)
    clip_x = ctx.din("clip_x", [1], F32)
    wg = ctx.din("w_gate", [IFF, D], F32)
    wu = ctx.din("w_up", [IFF, D], F32)
    wd = ctx.din("w_down", [D, IFF], F32)
    wqgT = ctx.dram("wqgT", [D, IFF], BF16).ap()
    wquT = ctx.dram("wquT", [D, IFF], BF16).ap()
    wdTs = ctx.dram("wdTs", [IFF, D], BF16, kind="ExternalOutput").ap()
    swg = ctx.dram("swg", [128, IFF // 128], F32).ap()
    swu = ctx.dram("swu", [128, IFF // 128], F32).ap()
    hF = ctx.dram("hF", [IFF, T], F32, kind="ExternalOutput").ap()
    lmaxh = ctx.dram("lmaxh", [1], F32, kind="ExternalOutput").ap()
    cs = load_consts(ctx)
    dq = phase_wq_multi(ctx, "q_", [
        dict(w_ap=wg, nrow=IFF, ncol=D, wqT_ap=wqgT, scales_ap=swg,
             dequant=False, stream="g"),
        dict(w_ap=wu, nrow=IFF, ncol=D, wqT_ap=wquT, scales_ap=swu,
             dequant=False, stream="u"),
        dict(w_ap=wd, nrow=D, ncol=IFF, wqT_ap=wdTs, scales_ap=None,
             dequant=True, stream="w"),
    ], after=tuple(cs["ld"]))
    ctx.free("q_")
    d4 = phase_ffn_a(ctx, "fa_", xn1, clip_x, lmax8, 8, wqgT, wquT, swg, swu,
                     hF, lmaxh, after=tuple(dq))
    finalize(ctx, *(tuple(d4) + tuple(dq)))
    return ctx.build()


def build_launch_c():
    ctx = Ctx()
    hF = ctx.din("hF", [IFF, T], F32)
    lmaxh8 = ctx.din("lmaxh8", [8], F32)
    wdTs = ctx.din("wdTs", [IFF, D], BF16)
    x1 = ctx.din("x1", [T, D], F32)
    s1 = ctx.din("s1", [T, D], BF16)
    clip_d = ctx.din("clip_down", [1], F32)
    out = ctx.dout("out", [T, D], F32)
    cs = load_consts(ctx)
    d5 = phase_ffn_b(ctx, "fb_", hF, clip_d, lmaxh8, 8, wdTs, x1, s1, out,
                     after=tuple(cs["ld"]))
    finalize(ctx, *tuple(d5))
    return ctx.build()


def l1a_host_inputs(z):
    embk = np.asarray(z["L1_attn_embk"])
    embv = np.asarray(z["L1_attn_embv"])
    return dict(
        Wattn=np.asarray(z["L1_attn_Wattn"]),
        battn=np.asarray(z["L1_attn_battn"]),
        Wproj=np.asarray(z["L1_attn_Wproj"]),
        bproj=np.asarray(z["L1_attn_bproj"]),
        embkR=np.ascontiguousarray(embk[::-1]),
        embvRE=np.ascontiguousarray(embv[np.minimum(2047 - np.arange(2048), 1023)]),
    )


def kernel(**inputs):
    z = {k: np.asarray(v) for k, v in inputs.items()}
    hc = host_constants()
    l1a = l1a_host_inputs(z)
    ncA = build_launch_a()
    imsA = []
    for b in range(8):
        im = l0_host_inputs(z, b)
        im.update(l1a)
        imsA.append(im)
    ra = run_bass_kernel_spmd(ncA, imsA, core_ids=list(range(8))).results

    lmax8 = np.ascontiguousarray(
        np.concatenate([np.asarray(ra[b]["lmax1"]) for b in range(8)]))
    ncB = build_launch_b()
    imsB = [dict(xn1=np.asarray(ra[b]["xn1"]), lmax8=lmax8,
                 clip_x=z["L1_ffn_clip_x"], w_gate=z["L1_ffn_w_gate"],
                 w_up=z["L1_ffn_w_up"], w_down=z["L1_ffn_w_down"], **hc)
            for b in range(8)]
    rb = run_bass_kernel_spmd(ncB, imsB, core_ids=list(range(8))).results

    lmaxh8 = np.ascontiguousarray(
        np.concatenate([np.asarray(rb[b]["lmaxh"]) for b in range(8)]))
    ncC = build_launch_c()
    imsC = [dict(hF=np.asarray(rb[b]["hF"]), lmaxh8=lmaxh8,
                 wdTs=np.asarray(rb[b]["wdTs"]), x1=np.asarray(ra[b]["x1"]),
                 s1=np.asarray(ra[b]["s1"]), clip_down=z["L1_ffn_clip_down"],
                 **hc)
            for b in range(8)]
    rc = run_bass_kernel_spmd(ncC, imsC, core_ids=list(range(8))).results
    return np.stack([np.asarray(rc[b]["out"]) for b in range(8)], 0).astype(np.float32)
